# version 05
# baseline (speedup 1.0000x reference)
import os as _os
import numpy as np
import ml_dtypes
from contextlib import ExitStack

SEQ_ALL_EDGES = bool(int(_os.environ.get("SEQ_ALL_EDGES", "1")))

import concourse.bass as bass
import concourse.mybir as mybir
from concourse.bass_utils import run_bass_kernel_spmd

F32 = mybir.dt.float32
BF16 = mybir.dt.bfloat16
I32 = mybir.dt.int32
AF = mybir.ActivationFunctionType
ALU = mybir.AluOpType
AX = mybir.AxisListType

BF16NP = ml_dtypes.bfloat16

D = 1024
MAGIC = 12582912.0
H = 16
HD = 64
TB = 4096
TT = 2048
NB = 4
EPS_LN = 1e-5



class Seq:

    ENGINES = ("sync", "scalar", "vector", "gpsimd", "tensor")

    def __init__(self, nc, stack, serialize=False, serialize_from=None,
                 serialize_to=None):
        self.nc = nc
        self.stack = stack
        self.serialize = serialize
        self.ser_from = serialize_from
        self.ser_to = serialize_to
        self.gidx = 0
        self.sems = {}
        self.count = {}
        for e in self.ENGINES:
            for kind in ("c", "d"):
                k = f"{e}_{kind}"
                self.sems[k] = stack.enter_context(nc.semaphore(f"sq_{k}"))
                self.count[k] = 0
        self.ops = {e: [] for e in self.ENGINES}
        self.waited = {e: {} for e in self.ENGINES}
        self.nprev = 0
        self.floor = []

    def barrier(self):
        self.floor = [(k, v, None) for k, v in self.count.items() if v > 0]

    def op(self, engine, emit, deps=(), dma=False, tag="", private=False,
           skey=None, small=False):
        inc = 16 if dma else 1
        if self.floor:
            deps = list(deps) + self.floor
        self.gidx += 1
        ser = self.serialize or (
            self.ser_from is not None and self.ser_from <= self.gidx
            and (self.ser_to is None or self.gidx < self.ser_to))
        key = f"{engine}_{'d' if dma else 'c'}"
        if skey is not None:
            key = skey
            if key not in self.sems:
                self.sems[key] = self.stack.enter_context(self.nc.semaphore(key))
                self.count[key] = 0
        waits = []
        if ser:
            deps = list(deps) + [(k, v, None) for k, v in self.count.items()
                                 if v > 0 and not k.startswith(engine)]
        for dep in deps:
            if dep is None:
                continue
            dk, dv, ddma = dep
            if (not SEQ_ALL_EDGES and dk == f"{engine}_c" and not dma
                    and ddma is False):
                continue
            if dv > self.waited[engine].get(dk, 0):
                waits.append((dk, dv))
                self.waited[engine][dk] = dv
        if private:
            self.nprev += 1
            pkey = f"pv{self.nprev}"
            self.sems[pkey] = self.stack.enter_context(self.nc.semaphore(pkey))
            self.ops[engine].append((emit, waits, inc, pkey))
            return (pkey, inc, dma or small)
        self.ops[engine].append((emit, waits, inc, key))
        self.count[key] += inc
        return (key, self.count[key], dma or small)

    def emit(self, block):
        for e in self.ENGINES:
            ops = self.ops[e]
            if not ops:
                continue
            def body(eng, ops=ops, e=e):
                for (emit_fn, waits, inc, key) in ops:
                    for (pe, v) in waits:
                        eng.wait_ge(self.sems[pe], v)
                    inst = emit_fn(eng)
                    if inst is not None:
                        inst.then_inc(self.sems[key], inc)
            getattr(block, e)(body)


def mk_identity(seq, ident):
    m = seq.op("gpsimd", lambda e: e.memset(ident[:], 0.0))
    return seq.op("gpsimd", lambda e: e.affine_select(
        out=ident[:], in_=ident[:], compare_op=ALU.not_equal,
        fill=1.0, base=0, pattern=[[-1, ident.shape[1]]],
        channel_multiplier=1), deps=[m])


class LNHelper:

    def __init__(self, seq, sbuf, name, width, eps=EPS_LN, sq=None):
        self.seq = seq
        self.w = width
        self.sq = sq if sq is not None else sbuf(f"{name}_sq", [128, width], F32)
        self.st = sbuf(f"{name}_st", [128, 8], F32)
        self.eps_col = sbuf(f"{name}_eps", [128, 1], F32)
        self.eps_init = seq.op("vector", lambda e: e.memset(self.eps_col[:], eps))

    def ln(self, src_ap, out_ap, deps):
        seq, st, w = self.seq, self.st, self.w
        d1 = seq.op("vector", lambda e: e.reduce_sum(
            out=st[:, 0:1], in_=src_ap, axis=AX.X), deps=deps, small=True)
        d2 = seq.op("scalar", lambda e: e.activation(
            out=self.sq[:], in_=src_ap, func=AF.Square), deps=deps)
        d3 = seq.op("vector", lambda e: e.reduce_sum(
            out=st[:, 1:2], in_=self.sq[:], axis=AX.X), deps=[d2], small=True)
        d4a = seq.op("vector", lambda e: e.tensor_scalar_mul(
            st[:, 2:3], st[:, 0:1], 1.0 / w), deps=[d1], small=True)
        d4b = seq.op("vector", lambda e: e.tensor_mul(
            st[:, 5:6], st[:, 2:3], st[:, 2:3]), deps=[d4a], small=True)
        d4 = seq.op("vector", lambda e: e.tensor_scalar(
            out=st[:, 3:4], in0=st[:, 1:2], scalar1=1.0 / w, scalar2=st[:, 5:6],
            op0=ALU.mult, op1=ALU.subtract), deps=[d3, d4b], small=True)
        d5 = seq.op("scalar", lambda e: e.activation(
            out=st[:, 4:5], in_=st[:, 3:4], func=AF.Sqrt, bias=self.eps_col[:]),
            deps=[d4, self.eps_init], small=True)
        d6 = seq.op("vector", lambda e: e.reciprocal(
            out=st[:, 6:7], in_=st[:, 4:5]), deps=[d5], small=True)
        d7 = seq.op("vector", lambda e: e.tensor_scalar(
            out=out_ap, in0=src_ap, scalar1=st[:, 2:3],
            scalar2=st[:, 6:7], op0=ALU.subtract, op1=ALU.mult), deps=[d4a, d6])
        return d7



def dram_A(nc, kout="ExternalOutput"):
    kin = "ExternalInput"
    return {
        "XQ": nc.dram_tensor("XQ", [TT, D], F32, kind=kin).ap(),
        "XF": nc.dram_tensor("XF", [TB, D], F32, kind=kin).ap(),
        "WQKV": nc.dram_tensor("WQKV", [D, 3 * D], F32, kind=kin).ap(),
        "BQKV": nc.dram_tensor("BQKV", [3 * D], F32, kind=kin).ap(),
        "MASKS": nc.dram_tensor("MASKS", [128, 32 * 512], BF16, kind=kin).ap(),
        "AUN": nc.dram_tensor("AUN", [D, TT], BF16, kind=kout).ap(),
        "DEN": nc.dram_tensor("DEN", [1, H * TT], F32, kind=kout).ap(),
    }


def phase_A(nc, seq, st_, G):
    xq, xf, wqkv = G["XQ"], G["XF"], G["WQKV"]
    bqkv, masks = G["BQKV"], G["MASKS"]
    aun, den = G["AUN"], G["DEN"]
    if True:
        sbuf = lambda n, sh, dt=F32: st_.enter_context(nc.sbuf_tensor("A" + n, sh, dt))
        psum = lambda n, sh, dt=F32: st_.enter_context(nc.psum_tensor("A" + n, sh, dt))

        wq_bf = sbuf("wq_bf", [128, 8 * 3 * D], BF16)
        wstg = [sbuf(f"wstg{i}", [128, 3 * D], F32) for i in range(2)]
        bias_row = sbuf("bias_row", [1, 3 * D], BF16)
        ones_row = sbuf("ones_row", [1, 512], BF16)
        ident = sbuf("ident", [128, 128], BF16)
        mask_sb = sbuf("mask_sb", [128, 32 * 512], BF16)

        xstg = [sbuf(f"xstg{i}", [128, D], F32) for i in range(3)]
        lnq_tm = [sbuf(f"lnq_tm{i}", [128, D], BF16) for i in range(2)]
        lnf_tm = [sbuf(f"lnf_tm{i}", [128, 2 * D], BF16) for i in range(2)]
        lnq_fm = [sbuf(f"lnq_fm{i}", [128, 8 * 128], BF16) for i in range(2)]
        lnf_fm = [sbuf(f"lnf_fm{i}", [128, 8 * 256], BF16) for i in range(2)]

        q_st = sbuf("q_st", [64, TT], BF16)
        k_st = sbuf("k_st", [64, 2 * TT], BF16)
        v_st = sbuf("v_st", [64, 2 * TT], BF16)
        v_pm = sbuf("v_pm", [128, 32 * 65], BF16)
        ebuf = [sbuf(f"ebuf{i}", [128, 512], BF16) for i in range(4)]
        atmp = [sbuf(f"atmp{i}", [64, 512], BF16) for i in range(2)]
        dtmp = [sbuf(f"dtmp{i}", [1, 512], F32) for i in range(2)]

        ps_s = [psum(f"ps_s{i}", [128, 512], F32) for i in range(2)]
        ps_a = [psum(f"ps_a{i}", [128, 512], F32) for i in range(2)]
        ps_x = [psum(f"ps_x{i}", [128, 256], F32) for i in range(2)]
        ps_t = [psum(f"ps_t{i}", [128, 128], BF16) for i in range(2)]

        ld_masks = seq.op("sync", lambda e: e.dma_start(out=mask_sb[:], in_=masks), dma=True, private=True)
        mk_ones = seq.op("vector", lambda e: e.memset(ones_row[:], 1.0))
        id_done = mk_identity(seq, ident)
        mk_vones = seq.op("vector", lambda e: e.memset(v_pm[:], 1.0))

        wcast = []
        for fc in range(8):
            ld = seq.op("sync", lambda e, fc=fc: e.dma_start(
                out=wstg[fc % 2][:], in_=wqkv[128 * fc:128 * (fc + 1), :]),
                dma=True, deps=[wcast[fc - 2] if fc >= 2 else None], private=True)
            cst = seq.op("vector", lambda e, fc=fc: e.tensor_copy(
                out=wq_bf[:, 3 * D * fc: 3 * D * (fc + 1)], in_=wstg[fc % 2][:]),
                deps=[ld])
            wcast.append(cst)
        ld_biasf = seq.op("gpsimd", lambda e: e.dma_start(
            out=wstg[0][0:1, :], in_=bqkv.rearrange("(a b) -> a b", a=1)),
            dma=True, deps=[wcast[6]], private=True)
        cast_bias = seq.op("vector", lambda e: e.tensor_copy(
            out=bias_row[:], in_=wstg[0][0:1, :]), deps=[ld_biasf])

        lnh = LNHelper(seq, sbuf, "lnA", D)

        prev_attn_last = None
        stg_free = {}
        tm_readers = [None, None]
        fm_readers = [None, None]
        last_aevac = [None, None]
        last_sread = [None, None]
        last_xevac = [None, None]
        last_tevac = [None, None]
        last_pv = {}
        eidx = 0
        store_ops = []
        atmp_busy = [[], []]

        for h in range(H):
            w = h % 2
            applies = []
            tiles = [(0, xq, h, lnq_tm[w][:], D),
                     (1, xf, 2 * h, lnf_tm[w][:, 0:D], D),
                     (2, xf, 2 * h + 1, lnf_tm[w][:, D:2 * D], D)]
            for (k_, src, tidx, out_ap, wd) in tiles:
                s = (3 * h + k_) % 3
                ld = seq.op("sync", lambda e, s=s, src=src, t=tidx: e.dma_start(
                    out=xstg[s][:], in_=src[128 * t:128 * (t + 1), :]), dma=True,
                    deps=[stg_free.get(s)], skey=f"xstg{s}")
                ap = lnh.ln(xstg[s][:], out_ap, [ld, tm_readers[w]])
                stg_free[s] = ap
                applies.append(ap)

            fm_ev = []
            tlast = None
            for ch in range(8):
                b_ = ch % 2
                tq = seq.op("tensor", lambda e, ch=ch, w=w: e.matmul(
                    ps_t[ch % 2][:], lnq_tm[w][:, 128 * ch:128 * (ch + 1)], ident[:],
                    is_transpose=True, start=True, stop=True),
                    deps=[applies[0], id_done, last_tevac[b_], fm_readers[w]])
                ev = seq.op("vector", lambda e, ch=ch, w=w: e.tensor_copy(
                    out=lnq_fm[w][:, 128 * ch:128 * (ch + 1)], in_=ps_t[ch % 2][:]),
                    deps=[tq])
                last_tevac[b_] = ev
                fm_ev.append(ev)
                tlast = tq
            for ch in range(8):
                for half in range(2):
                    b_ = half
                    tf = seq.op("tensor", lambda e, ch=ch, half=half, w=w: e.matmul(
                        ps_t[half][:],
                        lnf_tm[w][:, D * half + 128 * ch: D * half + 128 * (ch + 1)],
                        ident[:], is_transpose=True, start=True, stop=True),
                        deps=[applies[1 + half], id_done, last_tevac[b_], fm_readers[w]])
                    ev = seq.op("vector", lambda e, ch=ch, half=half, w=w: e.tensor_copy(
                        out=lnf_fm[w][:, 256 * ch + 128 * half: 256 * ch + 128 * (half + 1)],
                        in_=ps_t[half][:]), deps=[tf])
                    last_tevac[b_] = ev
                    fm_ev.append(ev)
                    tlast = tf
            tm_readers[w] = tlast

            wdeps = [wcast[-1], cast_bias, mk_ones] + fm_ev
            head_guard = [prev_attn_last] if prev_attn_last else []

            def xproj(kind, jp, pb, deps):
                cbase = kind * D + 128 * jp
                nmov = 128 if kind == 0 else 256
                src = lnq_fm[w] if kind == 0 else lnf_fm[w]
                def emit(e, cbase=cbase, nmov=nmov, src=src, pb=pb):
                    last = None
                    for fc in range(8):
                        last = e.matmul(
                            ps_x[pb][:, 0:nmov],
                            wq_bf[:, 3 * D * fc + cbase: 3 * D * fc + cbase + 128],
                            src[:, nmov * fc: nmov * (fc + 1)],
                            start=(fc == 0), stop=False)
                    last = e.matmul(ps_x[pb][:, 0:nmov],
                                    bias_row[0:1, cbase:cbase + 128],
                                    ones_row[0:1, 0:nmov], start=False, stop=True)
                    return last
                return seq.op("tensor", emit, deps=deps)

            q_ev = k_ev = v_ev = None
            v_evs = []
            for jp in range(8):
                pb = jp % 2
                mm = xproj(0, jp, pb, wdeps + [last_xevac[pb]])
                q_ev = seq.op("scalar", lambda e, jp=jp, pb=pb: (
                    e.copy(out=q_st[:, 256 * jp: 256 * jp + 128], in_=ps_x[pb][0:64, 0:128]),
                    e.copy(out=q_st[:, 256 * jp + 128: 256 * jp + 256], in_=ps_x[pb][64:128, 0:128]),
                )[-1], deps=[mm] + head_guard)
                last_xevac[pb] = q_ev
            for jp in range(8):
                pb = jp % 2
                mm = xproj(1, jp, pb, wdeps + [last_xevac[pb]])
                k_ev = seq.op("scalar", lambda e, jp=jp, pb=pb: (
                    e.copy(out=k_st[:, 512 * jp: 512 * jp + 256], in_=ps_x[pb][0:64, 0:256]),
                    e.copy(out=k_st[:, 512 * jp + 256: 512 * jp + 512], in_=ps_x[pb][64:128, 0:256]),
                )[-1], deps=[mm] + head_guard)
                last_xevac[pb] = k_ev
            for jp in range(8):
                pb = jp % 2
                mm = xproj(2, jp, pb, wdeps + [last_xevac[pb]])
                v_ev = seq.op("scalar", lambda e, jp=jp, pb=pb: (
                    e.copy(out=v_st[:, 512 * jp: 512 * jp + 256], in_=ps_x[pb][0:64, 0:256]),
                    e.copy(out=v_st[:, 512 * jp + 256: 512 * jp + 512], in_=ps_x[pb][64:128, 0:256]),
                )[-1], deps=[mm] + head_guard)
                last_xevac[pb] = v_ev
                v_evs.append(v_ev)

            vpm_ready = []
            for j in range(16):
                for th in range(2):
                    kt = 2 * j + th
                    b_ = kt % 2
                    tv = seq.op("tensor", lambda e, j=j, th=th, b_=b_: e.matmul(
                        ps_t[b_][:, 0:64],
                        v_st[:, 256 * j + 128 * th: 256 * j + 128 * (th + 1)],
                        ident[0:64, 0:64], is_transpose=True, start=True, stop=True),
                        deps=[v_evs[j // 2], id_done, last_tevac[b_]])
                    ev = seq.op("vector", lambda e, kt=kt, b_=b_: e.tensor_copy(
                        out=v_pm[:, 65 * kt: 65 * kt + 64], in_=ps_t[b_][:, 0:64]),
                        deps=[tv, mk_vones] + head_guard)
                    last_tevac[b_] = ev
                    vpm_ready.append(ev)

            for qc in range(4):
                ab = qc % 2
                kts = [(jk, th) for th in range(2) if 128 * th <= 64 * qc + 62
                       for jk in range(16)]
                first = True
                pv = None
                for ktidx, (jk, th) in enumerate(kts):
                    is_last = ktidx == len(kts) - 1
                    sb_ = eidx % 2
                    eb = eidx % 4
                    masked = not (th == 0 and qc >= 2)
                    mmS = seq.op("tensor", lambda e, jk=jk, th=th, qc=qc, sb_=sb_: e.matmul(
                        ps_s[sb_][:],
                        k_st[:, 256 * jk + 128 * th: 256 * jk + 128 * (th + 1)],
                        bass.AP(q_st, 32 * qc, [[TT, 64], [1, 32], [128, 16]]),
                        start=True, stop=True),
                        deps=[k_ev, q_ev, last_sread[sb_]])
                    ex = seq.op("scalar", lambda e, sb_=sb_, eb=eb: e.activation(
                        out=ebuf[eb][:], in_=ps_s[sb_][:], func=AF.Exp, scale=0.125),
                        deps=[mmS, last_pv.get(eb)])
                    last_sread[sb_] = ex
                    edone = ex
                    if masked:
                        mi = 2 * jk + (qc - 2 * th)
                        edone = seq.op("vector", lambda e, eb=eb, mi=mi: e.tensor_mul(
                            ebuf[eb][:], ebuf[eb][:], mask_sb[:, 512 * mi: 512 * (mi + 1)]),
                            deps=[ex, ld_masks])
                    kt = 2 * jk + th
                    pv = seq.op("tensor", lambda e, kt=kt, eb=eb, ab=ab, first=first, last=is_last: e.matmul(
                        ps_a[ab][0:65, :], v_pm[:, 65 * kt: 65 * (kt + 1)],
                        ebuf[eb][:], start=first, stop=last),
                        deps=[edone, vpm_ready[kt], last_aevac[ab]])
                    last_pv[eb] = pv
                    first = False
                    eidx += 1
                prev_attn_last = pv
                tb_ = (h * 4 + qc) % 2
                ev_a = seq.op("scalar", lambda e, ab=ab, tb_=tb_: (
                    e.copy(out=atmp[tb_][:], in_=ps_a[ab][0:64, :]),
                    e.copy(out=dtmp[tb_][:], in_=ps_a[ab][64:65, :]),
                )[-1], deps=[pv] + atmp_busy[tb_])
                last_aevac[ab] = ev_a
                st_a = seq.op("scalar", lambda e, tb_=tb_, h=h, qc=qc: e.dma_start(
                    out=aun[64 * h:64 * (h + 1), 512 * qc: 512 * (qc + 1)],
                    in_=atmp[tb_][:]), dma=True, deps=[ev_a], skey=f"aunst{tb_}")
                st_dn = seq.op("gpsimd", lambda e, tb_=tb_, h=h, qc=qc: e.dma_start(
                    out=den[0:1, TT * h + 512 * qc: TT * h + 512 * (qc + 1)],
                    in_=dtmp[tb_][:]), dma=True, deps=[ev_a], skey=f"denst{tb_}")
                atmp_busy[tb_] = [st_a, st_dn]
                store_ops.append(st_a)
                store_ops.append(st_dn)
        seq.op("sync", lambda e: e.nop(), deps=store_ops)


def build_A():
    nc = bass.Bass("TRN2", target_bir_lowering=False)
    G = dram_A(nc)
    with ExitStack() as st_:
        seq = Seq(nc, st_)
        block = st_.enter_context(nc.Block())
        phase_A(nc, seq, st_, G)
        seq.emit(block)
    return nc




def build_B():
    pass


def dram_B(nc, kout="ExternalOutput", r2_ap=None, aun_ap=None, den_ap=None):
    kin = "ExternalInput"
    return {
        "XMY": nc.dram_tensor("XMY", [TT, D], F32, kind=kin).ap(),
        "AUN": aun_ap if aun_ap is not None else
               nc.dram_tensor("AUN", [D, TT], BF16, kind=kin).ap(),
        "DEN": den_ap if den_ap is not None else
               nc.dram_tensor("DEN", [1, H * TT], F32, kind=kin).ap(),
        "WPROJ": nc.dram_tensor("WPROJ", [D, D], F32, kind=kin).ap(),
        "BPROJ": nc.dram_tensor("BPROJ", [D], F32, kind=kin).ap(),
        "W1": nc.dram_tensor("W1", [4, D, 4 * D], F32, kind=kin).ap(),
        "B1": nc.dram_tensor("B1", [4, 4 * D], F32, kind=kin).ap(),
        "W2": nc.dram_tensor("W2", [4, 4 * D, D], F32, kind=kin).ap(),
        "B2": nc.dram_tensor("B2", [4, D], F32, kind=kin).ap(),
        "WU": nc.dram_tensor("WU", [D, 1], F32, kind=kin).ap(),
        "BU": nc.dram_tensor("BU", [1], F32, kind=kin).ap(),
        "R2": r2_ap if r2_ap is not None else
              nc.dram_tensor("R2", [TT, D], F32, kind=kout).ap(),
    }


def phase_B(nc, seq, st_, G):
    xmy, aun, den = G["XMY"], G["AUN"], G["DEN"]
    wproj, bproj = G["WPROJ"], G["BPROJ"]
    w1, b1, w2, b2 = G["W1"], G["B1"], G["W2"], G["B2"]
    wu, bu, r2 = G["WU"], G["BU"], G["R2"]
    drec = nc.dram_tensor("DREC", [H, TT], BF16).ap()
    wbounce = nc.dram_tensor("WBOUNCE", [4, TT], BF16).ap()
    xspill = nc.dram_tensor("XSPILL", [TT, D], F32).ap()

    if True:
        sbuf = lambda n, sh, dt=F32: st_.enter_context(nc.sbuf_tensor("B" + n, sh, dt))
        psum = lambda n, sh, dt=F32: st_.enter_context(nc.psum_tensor("B" + n, sh, dt))

        X = sbuf("X", [128, 16 * D], F32)
        he = X.bitcast(BF16)
        fmA = sbuf("fmA", [128, 8 * TT], BF16)
        zfm = sbuf("zfm", [128, 8 * TT], BF16)
        y_acc = zfm
        w2stg_f = zfm[:, 12288:16384].bitcast(F32)
        wpj = sbuf("wpj", [128, 8 * D], BF16)
        w2bf = wpj
        wstgB = sbuf("wstgB", [128, 2 * D], F32)
        w1bf = [sbuf(f"w1bf{i}", [128, D], BF16) for i in range(2)]
        ztile = [sbuf(f"ztile{i}", [128, D], BF16) for i in range(2)]
        lnterm = sbuf("lnterm", [128, D], F32)
        xres = [sbuf(f"xres{i}", [128, D], F32) for i in range(2)]
        bc = [sbuf("bc0", [128, TT], BF16)] * 2
        webc = [sbuf("webc0", [128, D], BF16)] * 2
        rows = sbuf("rows", [1, 3 * TT], F32)
        browB = sbuf("browB", [1, D], BF16)
        b1row = zfm[0:1, 8192:12288]
        b2row = bc[0][0:1, 0:D]
        wurow = sbuf("wurow", [128, 8], BF16)
        wu8 = sbuf("wu8", [8, 128], F32)
        wu8b = sbuf("wu8b", [8, 128], BF16)
        busb = sbuf("busb", [1, 1], F32)
        ones_row = sbuf("ones_rowB", [1, 512], BF16)
        identB = sbuf("identB", [128, 128], BF16)

        ps_mm = [psum(f"ps_mm{i}", [128, 512], F32) for i in range(2)]
        ps_t = [psum(f"ps_tB{i}", [128, 128], BF16) for i in range(2)]
        ps_u = psum("ps_u", [1, 512], F32)

        id_done = mk_identity(seq, identB)
        mk_ones = seq.op("vector", lambda e: e.memset(ones_row[:], 1.0))
        lnh = LNHelper(seq, sbuf, "lnB", D, sq=lnterm)

        ld_x = seq.op("sync", lambda e: e.dma_start(
            out=X[:], in_=xmy.rearrange("(n p) c -> p n c", p=128)),
            dma=True, private=True)
        ld_y = seq.op("sync", lambda e: e.dma_start(
            out=fmA[:], in_=aun.rearrange("(cc p) t -> p cc t", p=128)),
            dma=True, private=True)
        ld_bp = seq.op("gpsimd", lambda e: e.dma_start(
            out=rows[0:1, 0:D], in_=bproj.rearrange("(a b) -> a b", a=1)),
            dma=True, private=True)
        c_bp = seq.op("vector", lambda e: e.tensor_copy(
            out=browB[0:1, 0:D], in_=rows[0:1, 0:D]), deps=[ld_bp])
        c_b1 = c_bp
        c_b2 = c_bp
        ld_wuA = seq.op("gpsimd", lambda e: e.dma_start(
            out=wu8[:], in_=wu.rearrange("(c p) o -> c (p o)", p=128)),
            dma=True, private=True)
        c_wuA = seq.op("vector", lambda e: e.tensor_copy(
            out=wu8b[:], in_=wu8[:]), deps=[ld_wuA])
        t_wu = seq.op("tensor", lambda e: e.matmul(
            ps_t[0][:, 0:8], wu8b[:], identB[0:8, 0:8],
            is_transpose=True, start=True, stop=True), deps=[c_wuA, id_done])
        ld_wu2 = seq.op("vector", lambda e: e.tensor_copy(
            out=wurow[:], in_=ps_t[0][:, 0:8]), deps=[t_wu])
        ld_bu = seq.op("gpsimd", lambda e: e.dma_start(
            out=busb[:], in_=bu.rearrange("(a b) -> a b", a=1)), dma=True, private=True)

        wp_cast = []
        for ck in range(8):
            sl = ck % 2
            ld = seq.op("sync" if sl == 0 else "gpsimd",
                lambda e, ck=ck, sl=sl: e.dma_start(
                out=wstgB[:, D * sl: D * (sl + 1)],
                in_=wproj[128 * ck:128 * (ck + 1), :]), dma=True,
                deps=[wp_cast[ck - 2] if ck >= 2 else None, c_b1, c_b2, c_bp],
                skey=f"wstgB{sl}")
            cst = seq.op("vector", lambda e, ck=ck, sl=sl: e.tensor_copy(
                out=wpj[:, D * ck: D * (ck + 1)], in_=wstgB[:, D * sl: D * (sl + 1)]),
                deps=[ld])
            wp_cast.append(cst)

        ymul = []
        dload_prev = {}
        for cc in range(8):
            sl = cc % 2
            ldd = seq.op("gpsimd", lambda e, cc=cc: e.dma_start(
                out=rows[0:1, 0:2 * TT],
                in_=den[0:1, 2 * TT * cc: 2 * TT * (cc + 1)]), dma=True,
                deps=[dload_prev.get(0), c_bp], skey="denld")
            rec = seq.op("vector", lambda e: e.reciprocal(
                out=rows[0:1, 0:2 * TT],
                in_=rows[0:1, 0:2 * TT]), deps=[ldd])
            recb = seq.op("vector", lambda e: e.tensor_copy(
                out=rows[0:1, 2 * TT:3 * TT].bitcast(BF16)[0:1, 0:2 * TT],
                in_=rows[0:1, 0:2 * TT]), deps=[rec])
            dload_prev[0] = recb
            std = seq.op("gpsimd", lambda e, cc=cc: e.dma_start(
                out=bass.AP(drec.tensor, 2 * cc * TT, [[2 * TT, 1], [1, 2 * TT]]),
                in_=rows[0:1, 2 * TT:3 * TT].bitcast(BF16)[0:1, 0:2 * TT]),
                dma=True, deps=[recb, dload_prev.get("std")], skey="drecst")
            dload_prev["std"] = std
            ldbc = seq.op("gpsimd", lambda e, cc=cc, sl=sl: e.dma_start(
                out=bc[sl][:],
                in_=bass.AP(drec.tensor, 2 * cc * TT, [[TT, 2], [0, 64], [1, TT]])),
                dma=True, deps=[std, ymul[cc - 1] if cc >= 1 else None],
                skey=f"bcld{sl}")
            ym = seq.op("vector", lambda e, cc=cc, sl=sl: e.tensor_mul(
                fmA[:, TT * cc: TT * (cc + 1)], fmA[:, TT * cc: TT * (cc + 1)],
                bc[sl][:]), deps=[ldbc, ld_y])
            ymul.append(ym)

        z_ev = {}
        last_z_mm = None
        for ec in range(8):
            for n4 in range(4):
                pb = (ec * 4 + n4) % 2
                def zmm(e, ec=ec, n4=n4, pb=pb):
                    last = None
                    for ck in range(8):
                        last = e.matmul(
                            ps_mm[pb][:], wpj[:, D * ck + 128 * ec: D * ck + 128 * (ec + 1)],
                            fmA[:, TT * ck + 512 * n4: TT * ck + 512 * (n4 + 1)],
                            start=(ck == 0), stop=False)
                    return e.matmul(ps_mm[pb][:], browB[0:1, 128 * ec:128 * (ec + 1)],
                                    ones_row[:], start=False, stop=True)
                mm = seq.op("tensor", zmm,
                            deps=[wp_cast[-1], mk_ones] + ymul + [z_ev.get((ec, n4, "war"))]
                            + [z_ev.get(("pb", pb))])
                ev = seq.op("scalar", lambda e, ec=ec, n4=n4, pb=pb: e.copy(
                    out=zfm[:, TT * ec + 512 * n4: TT * ec + 512 * (n4 + 1)],
                    in_=ps_mm[pb][:]), deps=[mm])
                z_ev[(ec, n4)] = ev
                z_ev[("pb", pb)] = ev
                last_z_mm = mm

        lnf_ev = {}
        prev_addx = None
        last_tev = [None, None]
        fm_war = [ymul[-1], ymul[-1]]
        for tt in range(16):
            zsl = tt % 2
            tevs = []
            for ec in range(8):
                b_ = ec % 2
                tr = seq.op("tensor", lambda e, ec=ec, tt=tt, b_=b_: e.matmul(
                    ps_t[b_][:], zfm[:, TT * ec + 128 * tt: TT * ec + 128 * (tt + 1)],
                    identB[:], is_transpose=True, start=True, stop=True),
                    deps=[z_ev[(ec, tt // 4)], id_done, last_tev[b_], last_z_mm])
                ev = seq.op("vector", lambda e, ec=ec, tt=tt, zsl=zsl, b_=b_: e.tensor_copy(
                    out=ztile[zsl][:, 128 * ec:128 * (ec + 1)], in_=ps_t[b_][:]),
                    deps=[tr, lnf_ev.get(("zt", zsl))])
                last_tev[b_] = ev
                tevs.append(ev)
            ap1 = lnh.ln(ztile[zsl][:], lnterm[:], tevs + [prev_addx])
            addx = seq.op("vector", lambda e, tt=tt: e.tensor_add(
                X[:, D * tt: D * (tt + 1)], X[:, D * tt: D * (tt + 1)], lnterm[:]),
                deps=[ap1, ld_x], small=True)
            prev_addx = addx
            ap2 = lnh.ln(X[:, D * tt: D * (tt + 1)], ztile[zsl][:], [addx])
            lnf_ev[("zt", zsl)] = ap2
            evs2 = []
            for ch in range(8):
                b_ = ch % 2
                tr = seq.op("tensor", lambda e, ch=ch, tt=tt, zsl=zsl, b_=b_: e.matmul(
                    ps_t[b_][:], ztile[zsl][:, 128 * ch:128 * (ch + 1)], identB[:],
                    is_transpose=True, start=True, stop=True),
                    deps=[ap2, last_tev[b_], fm_war[0] if tt == 0 else None])
                ev = seq.op("vector", lambda e, ch=ch, tt=tt, b_=b_: e.tensor_copy(
                    out=fmA[:, TT * ch + 128 * tt: TT * ch + 128 * (tt + 1)],
                    in_=ps_t[b_][:]), deps=[tr])
                last_tev[b_] = ev
                evs2.append(ev)
            lnf_ev[tt] = evs2
            lnf_ev[("zt", zsl)] = evs2[-1]
        ln1_all = [op for tt in range(16) for op in lnf_ev[tt]]

        sp_x = seq.op("sync", lambda e: e.dma_start(
            out=xspill.rearrange("(n p) c -> p n c", p=128), in_=X[:]),
            dma=True, private=True,
            deps=[lnf_ev[15][-1]] + [lnf_ev[tt][-1] for tt in range(16)])

        R = lambda k: rows[0:1, k * TT:(k + 1) * TT]
        u_mm = []
        for n4 in range(4):
            def umm(e, n4=n4):
                last = None
                for dck in range(8):
                    last = e.matmul(
                        ps_u[:], wurow[:, dck:dck + 1],
                        fmA[:, TT * dck + 512 * n4: TT * dck + 512 * (n4 + 1)],
                        start=(dck == 0), stop=(dck == 7))
                return last
            mm = seq.op("tensor", umm, deps=ln1_all + [ld_wu2, u_mm[-1][1] if u_mm else None])
            sg = seq.op("scalar", lambda e, n4=n4: e.activation(
                out=rows[0:1, 512 * n4: 512 * (n4 + 1)], in_=ps_u[:],
                func=AF.Sigmoid, bias=busb[:]), deps=[mm, ld_bu, dload_prev.get(0)],
                small=True)
            u_mm.append((mm, sg))
        u_done = u_mm[-1][1]
        we_sts = []
        for ex in range(4):
            i1 = seq.op("gpsimd", lambda e, ex=ex: e.iota(
                R(1).bitcast(I32), [[0, 512], [-1, 4]], base=ex + 4,
                channel_multiplier=0,
                allow_small_or_imprecise_dtypes=True),
                deps=[we_sts[-1] if we_sts else None, dload_prev.get(0), u_done],
                small=True)
            i2 = seq.op("scalar", lambda e: e.copy(
                out=R(1), in_=R(1).bitcast(I32)), deps=[i1], small=True)
            t2 = seq.op("vector", lambda e: e.tensor_single_scalar(
                out=R(2), in_=R(1), scalar=3.5, op=ALU.is_gt), deps=[i2], small=True)
            t3 = seq.op("vector", lambda e: e.scalar_tensor_tensor(
                out=R(2), in0=R(2), scalar=-4.0, in1=R(1),
                op0=ALU.mult, op1=ALU.add), deps=[t2], small=True)
            t4 = seq.op("vector", lambda e: e.tensor_scalar_add(
                R(2), R(2), 1.0), deps=[t3], small=True)
            t5 = seq.op("vector", lambda e: e.scalar_tensor_tensor(
                out=R(1), in0=R(0), scalar=4.0, in1=R(2),
                op0=ALU.mult, op1=ALU.subtract), deps=[t4, u_done], small=True)
            t6 = seq.op("vector", lambda e: e.tensor_scalar_add(
                R(1), R(1), 1.0), deps=[t5], small=True)
            s1 = seq.op("scalar", lambda e: e.activation(
                out=R(1), in_=R(1), func=AF.Sign), deps=[t6], small=True)
            s2 = seq.op("scalar", lambda e: e.activation(
                out=R(1), in_=R(1), func=AF.Relu), deps=[s1], small=True)
            r1_ = seq.op("vector", lambda e: e.reciprocal(
                out=R(2), in_=R(2)), deps=[t5], small=True)
            m1 = seq.op("vector", lambda e: e.tensor_mul(
                R(2), R(2), R(0)), deps=[r1_, u_done], small=True)
            m2 = seq.op("vector", lambda e: e.tensor_mul(
                R(2), R(2), R(1)), deps=[m1, s2], small=True)
            cb_ = seq.op("vector", lambda e: e.tensor_copy(
                out=R(1).bitcast(BF16)[0:1, 0:TT], in_=R(2)), deps=[m2], small=True)
            stw = seq.op("gpsimd", lambda e, ex=ex: e.dma_start(
                out=wbounce[ex:ex + 1, :], in_=R(1).bitcast(BF16)[0:1, 0:TT]),
                dma=True, deps=[cb_, we_sts[-1] if we_sts else None], skey="wbst")
            we_sts.append(stw)

        he_ev = {}
        yacc_done = {}
        last_w2 = {}
        r2_sts = []
        wsl_guard = {}
        all_ln_done = lnf_ev[15][-1]
        he_war = [sp_x]
        w2bf_war = [all_ln_done]
        yacc_war = [all_ln_done]
        last_w1 = {0: wp_cast[6], 1: wp_cast[7]}
        prev_ymm_last = None
        fin_guard = None
        bias_guard = None
        for tc in range(2):
            for ex in range(4):
                ldb1 = seq.op("gpsimd", lambda e, ex=ex: e.dma_start(
                    out=rows[0:1, 0:4 * D], in_=b1[ex:ex + 1, :]), dma=True,
                    deps=[bias_guard, we_sts[-1]], skey="bias_ld")
                ldb2 = seq.op("gpsimd", lambda e, ex=ex: e.dma_start(
                    out=rows[0:1, 4 * D: 5 * D], in_=b2[ex:ex + 1, :]), dma=True,
                    deps=[ldb1], skey="bias_ld")
                cb1 = seq.op("vector", lambda e: e.tensor_copy(
                    out=b1row, in_=rows[0:1, 0:4 * D]),
                    deps=[ldb1, prev_ymm_last, all_ln_done, ymul[-1]])
                cb2 = seq.op("vector", lambda e: e.tensor_copy(
                    out=b2row, in_=rows[0:1, 4 * D: 5 * D]),
                    deps=[ldb2, prev_ymm_last, all_ln_done, ymul[-1]])
                bias_guard = cb2
                for fb in range(32):
                    sl = fb % 2
                    ldw1 = seq.op("sync" if sl == 0 else "gpsimd",
                        lambda e, ex=ex, fb=fb, sl=sl: e.dma_start(
                        out=wstgB[:, D * sl: D * (sl + 1)],
                        in_=bass.AP(w1.tensor, ex * D * 4 * D + 128 * fb,
                                    [[4 * D, 128], [4 * D * 128, 8], [1, 128]])),
                        dma=True, deps=[last_w1.get(sl)], skey=f"wstgB{sl}")
                    cw1 = seq.op("vector", lambda e, sl=sl: e.tensor_copy(
                        out=w1bf[sl][:], in_=wstgB[:, D * sl: D * (sl + 1)]),
                        deps=[ldw1, wsl_guard.get(("w1", sl))])
                    last_w1[sl] = cw1
                    for n2 in range(2):
                        pb = (fb * 2 + n2) % 2
                        def hmm(e, ex=ex, fb=fb, n2=n2, sl=sl, tc=tc, pb=pb):
                            last = None
                            for dck in range(8):
                                last = e.matmul(
                                    ps_mm[pb][:],
                                    w1bf[sl][:, 128 * dck:128 * (dck + 1)],
                                    fmA[:, TT * dck + 1024 * tc + 512 * n2:
                                         TT * dck + 1024 * tc + 512 * (n2 + 1)],
                                    start=(dck == 0), stop=False)
                            return e.matmul(
                                ps_mm[pb][:],
                                b1row[0:1, 128 * fb: 128 * (fb + 1)],
                                ones_row[:], start=False, stop=True)
                        mm = seq.op("tensor", hmm,
                                    deps=[cw1, cb1, he_ev.get(("pb", pb)),
                                          yacc_done.get(("pb", pb))] + ln1_all)
                        ev = seq.op("scalar", lambda e, fb=fb, n2=n2, pb=pb: e.activation(
                            out=he[:, 1024 * fb + 512 * n2: 1024 * fb + 512 * (n2 + 1)],
                            in_=ps_mm[pb][:], func=AF.Relu),
                            deps=[mm] + (he_war + [prev_ymm_last]
                                         if fb == 0 and n2 == 0 else []))
                        he_ev[(fb, n2)] = ev
                        he_ev[("pb", pb)] = ev
                    wsl_guard[("w1", sl)] = he_ev[(fb, 1)]
                he_all_key = [he_ev[(fb, n2)] for fb in range(32) for n2 in range(2)]
                ldbc_e = seq.op("gpsimd", lambda e, ex=ex, tc=tc: e.dma_start(
                    out=webc[tc % 2][:],
                    in_=bass.AP(wbounce.tensor, ex * TT + 1024 * tc,
                                [[0, 128], [1, 1024]])),
                    dma=True, deps=[we_sts[ex], yacc_done.get(("webc", 0))],
                    skey=f"webc{tc % 2}")
                for dc in range(8):
                    sl2 = dc % 2
                    lw2a = seq.op("sync", lambda e, ex=ex, dc=dc: e.dma_start(
                        out=w2stg_f[:], in_=bass.AP(
                            w2.tensor, ex * 4 * D * D + 128 * dc,
                            [[D, 128], [D * 128, 16], [1, 128]])),
                        dma=True, deps=[last_w2.get("f", all_ln_done)], skey="w2stg")
                    cw2a = seq.op("vector", lambda e, sl2=sl2: e.tensor_copy(
                        out=w2bf[:, 4096 * sl2: 4096 * sl2 + 2048], in_=w2stg_f[:]),
                        deps=[lw2a, wsl_guard.get(("w2", sl2), w2bf_war[0])])
                    lw2b = seq.op("sync", lambda e, ex=ex, dc=dc: e.dma_start(
                        out=w2stg_f[:], in_=bass.AP(
                            w2.tensor, ex * 4 * D * D + 128 * dc + D * 2048,
                            [[D, 128], [D * 128, 16], [1, 128]])),
                        dma=True, deps=[cw2a], skey="w2stg")
                    cw2b = seq.op("vector", lambda e, sl2=sl2: e.tensor_copy(
                        out=w2bf[:, 4096 * sl2 + 2048: 4096 * sl2 + 4096], in_=w2stg_f[:]),
                        deps=[lw2b, wsl_guard.get(("w2", sl2))])
                    last_w2["f"] = cw2b
                    for n2 in range(2):
                        pb = (dc * 2 + n2) % 2
                        def ymm(e, ex=ex, dc=dc, n2=n2, sl2=sl2, tc=tc, pb=pb):
                            last = None
                            for fck in range(32):
                                last = e.matmul(
                                    ps_mm[pb][:],
                                    w2bf[:, 4096 * sl2 + 128 * fck: 4096 * sl2 + 128 * (fck + 1)],
                                    he[:, 1024 * fck + 512 * n2: 1024 * fck + 512 * (n2 + 1)],
                                    start=(fck == 0), stop=False)
                            return e.matmul(
                                ps_mm[pb][:],
                                b2row[0:1, 128 * dc: 128 * (dc + 1)],
                                ones_row[:], start=False, stop=True)
                        mm = seq.op("tensor", ymm, deps=[cw2a, cw2b, cb2] + he_all_key
                                    + [yacc_done.get(("pb", pb))])
                        prev_ymm_last = mm
                        if ex == 0:
                            ev = seq.op("vector", lambda e, dc=dc, n2=n2, pb=pb, tc=tc: e.tensor_mul(
                                y_acc[:, 1024 * dc + 512 * n2: 1024 * dc + 512 * (n2 + 1)],
                                ps_mm[pb][:], webc[tc % 2][:, 512 * n2: 512 * (n2 + 1)]),
                                deps=[mm, ldbc_e, fin_guard]
                                + (yacc_war if tc == 0 and dc == 0 and n2 == 0 else []))
                        else:
                            ev0 = seq.op("vector", lambda e, pb=pb, n2=n2, tc=tc: e.tensor_mul(
                                lnterm[:, 0:512].bitcast(BF16)[:, 0:512],
                                ps_mm[pb][:], webc[tc % 2][:, 512 * n2: 512 * (n2 + 1)]),
                                deps=[mm, ldbc_e, fin_guard, prev_addx,
                                      yacc_done.get("lnterm")])
                            ev = seq.op("vector", lambda e, dc=dc, n2=n2: e.tensor_add(
                                y_acc[:, 1024 * dc + 512 * n2: 1024 * dc + 512 * (n2 + 1)],
                                y_acc[:, 1024 * dc + 512 * n2: 1024 * dc + 512 * (n2 + 1)],
                                lnterm[:, 0:512].bitcast(BF16)[:, 0:512]),
                                deps=[ev0])
                        yacc_done[(dc, n2)] = ev
                        yacc_done[("pb", pb)] = ev
                        yacc_done["lnterm"] = ev
                    wsl_guard[("w2", sl2)] = yacc_done[(dc, 1)]
                yacc_done[("webc", 0)] = yacc_done[(7, 1)]
            yall = [yacc_done[(dc, n2)] for dc in range(8) for n2 in range(2)]
            for t8 in range(8):
                tt = 8 * tc + t8
                zsl = t8 % 2
                ldr = seq.op("sync", lambda e, tt=tt, zsl=zsl: e.dma_start(
                    out=xres[zsl][:], in_=xspill[128 * tt:128 * (tt + 1), :]),
                    dma=True, deps=[sp_x, r2_sts[-2] if len(r2_sts) >= 2 else None],
                    skey=f"xres{zsl}")
                tevs = []
                for ch in range(8):
                    b_ = ch % 2
                    tr = seq.op("tensor", lambda e, ch=ch, t8=t8, b_=b_: e.matmul(
                        ps_t[b_][:], y_acc[:, 1024 * ch + 128 * t8: 1024 * ch + 128 * (t8 + 1)],
                        identB[:], is_transpose=True, start=True, stop=True),
                        deps=[id_done, last_tev[b_]] + yall)
                    ev = seq.op("vector", lambda e, ch=ch, zsl=zsl, b_=b_: e.tensor_copy(
                        out=ztile[zsl][:, 128 * ch:128 * (ch + 1)], in_=ps_t[b_][:]),
                        deps=[tr, lnf_ev.get(("zt", zsl))])
                    last_tev[b_] = ev
                    tevs.append(ev)
                    fin_guard = ev
                ap1 = lnh.ln(ztile[zsl][:], lnterm[:], tevs + [prev_addx])
                lnf_ev[("zt", zsl)] = ap1
                addx = seq.op("vector", lambda e, zsl=zsl: e.tensor_add(
                    xres[zsl][:], xres[zsl][:], lnterm[:]), deps=[ap1, ldr],
                    small=True)
                prev_addx = addx
                stx = seq.op("scalar", lambda e, tt=tt, zsl=zsl: e.dma_start(
                    out=r2[128 * tt:128 * (tt + 1), :], in_=xres[zsl][:]),
                    dma=True, deps=[addx], skey=f"r2st{zsl}")
                r2_sts.append(stx)

        seq.op("sync", lambda e: e.nop(), deps=r2_sts)


def build_B():
    nc = bass.Bass("TRN2", target_bir_lowering=False)
    G = dram_B(nc)
    with ExitStack() as st_:
        seq = Seq(nc, st_)
        block = st_.enter_context(nc.Block())
        phase_B(nc, seq, st_, G)
        seq.emit(block)
    return nc



def dram_C(nc, kout="ExternalOutput", r2_ap=None):
    kin = "ExternalInput"
    return {
        "R2": r2_ap if r2_ap is not None else
              nc.dram_tensor("R2", [TT, D], F32, kind=kin).ap(),
        "WK": nc.dram_tensor("WK", [D, D], F32, kind=kin).ap(),
        "WV": nc.dram_tensor("WV", [D, D], F32, kind=kin).ap(),
        "KVP": nc.dram_tensor("KVP", [64, H * HD], F32, kind=kout).ap(),
    }


def phase_C(nc, seq, st_, G):
    r2, wk, wv, kvp = G["R2"], G["WK"], G["WV"], G["KVP"]
    if True:
        sbuf = lambda n, sh, dt=F32: st_.enter_context(nc.sbuf_tensor("C" + n, sh, dt))
        psum = lambda n, sh, dt=F32: st_.enter_context(nc.psum_tensor("C" + n, sh, dt))

        xstg = [sbuf(f"xstgC{i}", [128, D], F32) for i in range(3)]
        lntile = [sbuf(f"lntC{i}", [128, D], BF16) for i in range(2)]
        ln_fm = sbuf("ln_fmC", [128, 8 * TT], BF16)
        wstg = [sbuf(f"wstgC{i}", [128, D], F32) for i in range(2)]
        wbf = sbuf("wbfC", [128, 8 * D], BF16)
        wt = sbuf("wtC", [128, 8 * D], BF16)
        k_tm = sbuf("k_tmC", [128, 16 * D], BF16)
        v_tm = sbuf("v_tmC", [128, 16 * D], BF16)
        kvsb = sbuf("kvsbC", [64, H * HD], F32)
        ident = sbuf("identC", [128, 128], BF16)
        ps_mm = [psum(f"ps_mmC{i}", [128, 512], F32) for i in range(2)]
        ps_t = [psum(f"ps_tC{i}", [128, 128], BF16) for i in range(2)]
        ps_kv = psum("ps_kvC", [64, H * HD], F32)

        id_done = mk_identity(seq, ident)
        lnh = LNHelper(seq, sbuf, "lnC", D)

        stg_free = {}
        tm_guard = [None, None]
        last_tev = [None, None]
        fm_evs = []
        for tt in range(16):
            s = tt % 3
            w = tt % 2
            ld = seq.op("sync", lambda e, s=s, tt=tt: e.dma_start(
                out=xstg[s][:], in_=r2[128 * tt:128 * (tt + 1), :]), dma=True,
                deps=[stg_free.get(s)], skey=f"xstgC{s}")
            ap = lnh.ln(xstg[s][:], lntile[w][:], [ld, tm_guard[w]])
            stg_free[s] = ap
            for ch in range(8):
                b_ = ch % 2
                tr = seq.op("tensor", lambda e, ch=ch, w=w: e.matmul(
                    ps_t[b_ if False else ch % 2][:], lntile[w][:, 128 * ch:128 * (ch + 1)],
                    ident[:], is_transpose=True, start=True, stop=True),
                    deps=[ap, id_done, last_tev[b_]])
                ev = seq.op("vector", lambda e, ch=ch, tt=tt, b_=b_: e.tensor_copy(
                    out=ln_fm[:, TT * ch + 128 * tt: TT * ch + 128 * (tt + 1)],
                    in_=ps_t[b_][:]), deps=[tr])
                last_tev[b_] = ev
                fm_evs.append(ev)
                tm_guard[w] = tr
        fm_done = fm_evs[-1]

        def prep_w(wsrc, prev_done):
            casts = []
            for ec in range(8):
                sl = ec % 2
                ld = seq.op("sync", lambda e, ec=ec, sl=sl, wsrc=wsrc: e.dma_start(
                    out=wstg[sl][:], in_=wsrc[128 * ec:128 * (ec + 1), :]), dma=True,
                    deps=[casts[ec - 2] if ec >= 2 else prev_done], skey=f"wstgC{sl}")
                cst = seq.op("vector", lambda e, ec=ec, sl=sl: e.tensor_copy(
                    out=wbf[:, D * ec: D * (ec + 1)], in_=wstg[sl][:]),
                    deps=[ld, prev_done])
                casts.append(cst)
            tevs = []
            for ec in range(8):
                for dck in range(8):
                    b_ = (ec * 8 + dck) % 2
                    tr = seq.op("tensor", lambda e, ec=ec, dck=dck, b_=b_: e.matmul(
                        ps_t[b_][:], wbf[:, D * ec + 128 * dck: D * ec + 128 * (dck + 1)],
                        ident[:], is_transpose=True, start=True, stop=True),
                        deps=[casts[ec], id_done, last_tev[b_]])
                    ev = seq.op("vector", lambda e, ec=ec, dck=dck, b_=b_: e.tensor_copy(
                        out=wt[:, D * dck + 128 * ec: D * dck + 128 * (ec + 1)],
                        in_=ps_t[b_][:]), deps=[tr])
                    last_tev[b_] = ev
                    tevs.append(ev)
            return casts, tevs

        def proj_tm(out_sb, func, scale, wt_done, out_guard):
            evs = {}
            for tt in range(16):
                for e2 in range(2):
                    pb = (tt * 2 + e2) % 2
                    def pmm(e, tt=tt, e2=e2, pb=pb):
                        last = None
                        for dck in range(8):
                            last = e.matmul(
                                ps_mm[pb][:],
                                ln_fm[:, TT * dck + 128 * tt: TT * dck + 128 * (tt + 1)],
                                wt[:, D * dck + 512 * e2: D * dck + 512 * (e2 + 1)],
                                start=(dck == 0), stop=(dck == 7))
                        return last
                    mm = seq.op("tensor", pmm,
                                deps=wt_done + [fm_done, evs.get(("pb", pb)), out_guard])
                    ev = seq.op("scalar", lambda e, tt=tt, e2=e2, pb=pb, out_sb=out_sb,
                                func=func, scale=scale: e.activation(
                        out=out_sb[:, D * tt + 512 * e2: D * tt + 512 * (e2 + 1)],
                        in_=ps_mm[pb][:], func=func, scale=scale), deps=[mm])
                    evs[(tt, e2)] = ev
                    evs[("pb", pb)] = ev
            return [evs[(tt, e2)] for tt in range(16) for e2 in range(2)]

        ck, tk = prep_w(wk, None)
        k_evs = proj_tm(k_tm, AF.Relu, 0.125, [tk[-1]], None)
        cv, tv = prep_w(wv, k_evs[-1])
        v_evs = proj_tm(v_tm, AF.Copy, 1.0, [tv[-1]], None)

        kv_mm = None
        for h in range(H):
            for tt in range(16):
                kv_mm = seq.op("tensor", lambda e, h=h, tt=tt, first=(tt == 0): e.matmul(
                    ps_kv[:, HD * h: HD * (h + 1)],
                    k_tm[:, D * tt + HD * h: D * tt + HD * (h + 1)],
                    v_tm[:, D * tt + HD * h: D * tt + HD * (h + 1)],
                    start=(tt == 0), stop=(tt == 15)),
                    deps=[k_evs[2 * tt + h // 8], v_evs[2 * tt + h // 8],
                          k_evs[-1]])
        kv_ev = seq.op("vector", lambda e: e.tensor_copy(
            out=kvsb[:], in_=ps_kv[:]), deps=[kv_mm])
        st_kv = seq.op("scalar", lambda e: e.dma_start(out=kvp, in_=kvsb[:]),
                       dma=True, deps=[kv_ev])
        seq.op("sync", lambda e: e.nop(), deps=[st_kv])


def build_C():
    nc = bass.Bass("TRN2", target_bir_lowering=False)
    G = dram_C(nc)
    with ExitStack() as st_:
        seq = Seq(nc, st_)
        block = st_.enter_context(nc.Block())
        phase_C(nc, seq, st_, G)
        seq.emit(block)
    return nc



def dram_D1(nc, kout="ExternalOutput", r2_ap=None, kvps_ap=None, kvpo_ap=None,
            r3_ap=None, wu_name="WUQ"):
    kin = "ExternalInput"
    return {
        "R2": r2_ap if r2_ap is not None else
              nc.dram_tensor("R2", [TT, D], F32, kind=kin).ap(),
        "KVPS": kvps_ap if kvps_ap is not None else
                nc.dram_tensor("KVPS", [64, H * HD], F32, kind=kin).ap(),
        "KVPO": kvpo_ap if kvpo_ap is not None else
                nc.dram_tensor("KVPO", [64, H * HD], F32, kind=kin).ap(),
        "WQ": nc.dram_tensor("WQ", [D, D], F32, kind=kin).ap(),
        "WU": nc.dram_tensor(wu_name, [D, D], F32, kind=kin).ap(),
        "WO": nc.dram_tensor("WO", [D, D], F32, kind=kin).ap(),
        "R3": r3_ap if r3_ap is not None else
              nc.dram_tensor("R3", [TT, D], F32, kind=kout).ap(),
    }


def phase_D1(nc, seq, st_, G):
    r2, kvps, kvpo = G["R2"], G["KVPS"], G["KVPO"]
    wq, wu, wo, r3 = G["WQ"], G["WU"], G["WO"], G["R3"]
    sb_d = nc.dram_tensor("SBOUNCE", [H, TT], BF16, kind="Internal").ap()

    if True:
        sbuf = lambda n, sh, dt=F32: st_.enter_context(nc.sbuf_tensor("D" + n, sh, dt))
        psum = lambda n, sh, dt=F32: st_.enter_context(nc.psum_tensor("D" + n, sh, dt))

        X = sbuf("XD", [128, 16 * D], F32)
        ln_fm = sbuf("lnfmD", [128, 8 * TT], BF16)
        qz = sbuf("qzD", [128, 8 * TT], BF16)
        u_fm = sbuf("uD", [128, 8 * TT], BF16)
        o_fm = sbuf("oD", [128, 8 * TT], BF16)
        wstg0 = sbuf("wstgD0", [128, D], F32)
        wstg = [wstg0, wstg0]
        wts0 = sbuf("wtsD0", [128, D], BF16)
        wts = [wts0, wts0]
        kvf = u_fm[:, 0:2 * D].bitcast(F32)
        kvb = u_fm[:, 2 * D:3 * D]
        ones64 = sbuf("ones64D", [128, 1], BF16)
        lntile = [sbuf(f"lntD{i}", [128, D], BF16) for i in range(2)]
        lnterm = sbuf("lntermD", [128, D], F32)
        wbf = lnterm.bitcast(BF16)[:, 0:D]
        srow = qz[0:1, 0:2 * TT].bitcast(F32)
        srbf = qz[0:1, 2 * TT:3 * TT]
        bc = qz[:, 4 * TT:5 * TT]
        sq4 = qz[:, 6 * TT:7 * TT]
        ident = sbuf("identD", [128, 128], BF16)
        ps_mm = [psum(f"ps_mmD{i}", [128, 512], F32) for i in range(2)]
        ps_t = [psum(f"ps_tD{i}", [128, 128], BF16) for i in range(2)]
        ps_s = psum("ps_sD", [1, 512], F32)

        id_done = mk_identity(seq, ident)
        lnh = LNHelper(seq, sbuf, "lnD", D, sq=lnterm)

        ld_x = seq.op("sync", lambda e: e.dma_start(
            out=X[:], in_=r2.rearrange("(n p) c -> p n c", p=128)),
            dma=True, private=True)
        mk_o64 = seq.op("vector", lambda e: e.memset(ones64[:], 1.0))
        ld_kv1 = seq.op("gpsimd", lambda e: e.dma_start(
            out=kvf[0:64, :], in_=kvps), dma=True, private=True)
        ld_kv1b = seq.op("gpsimd", lambda e: e.dma_start(
            out=kvf[64:128, :], in_=kvps), dma=True, private=True)
        ld_kv2 = seq.op("gpsimd", lambda e: e.dma_start(
            out=wstg[0][0:64, 0:H * HD], in_=kvpo), dma=True, private=True)
        ld_kv2b = seq.op("gpsimd", lambda e: e.dma_start(
            out=wstg[0][64:128, 0:H * HD], in_=kvpo), dma=True, private=True)
        kadd = seq.op("vector", lambda e: e.tensor_add(
            kvf, kvf, wstg[0][:, 0:H * HD]),
            deps=[ld_kv1, ld_kv1b, ld_kv2, ld_kv2b], small=True)
        kcast = seq.op("vector", lambda e: e.tensor_copy(
            out=kvb, in_=kvf), deps=[kadd], small=True)

        tm_guard = [None, None]
        last_tev = [None, None]
        fm_evs = []
        for tt in range(16):
            w = tt % 2
            ap = lnh.ln(X[:, D * tt: D * (tt + 1)], lntile[w][:],
                        [ld_x, tm_guard[w]])
            for ch in range(8):
                b_ = ch % 2
                tr = seq.op("tensor", lambda e, ch=ch, w=w, b_=b_: e.matmul(
                    ps_t[b_][:], lntile[w][:, 128 * ch:128 * (ch + 1)],
                    ident[:], is_transpose=True, start=True, stop=True),
                    deps=[ap, id_done, last_tev[b_]])
                ev = seq.op("vector", lambda e, ch=ch, tt=tt, b_=b_: e.tensor_copy(
                    out=ln_fm[:, TT * ch + 128 * tt: TT * ch + 128 * (tt + 1)],
                    in_=ps_t[b_][:]), deps=[tr])
                last_tev[b_] = ev
                fm_evs.append(ev)
                tm_guard[w] = tr
        fm_done = fm_evs[-1]

        def proj_fm(wsrc, out_sb, func, scale, deps0, out_guard, src_fm=None):
            if src_fm is None:
                src_fm = ln_fm
            evs = {}
            wprev = {}
            for ec in range(8):
                sl = ec % 2
                ld = seq.op("sync", lambda e, ec=ec, sl=sl, wsrc=wsrc: e.dma_start(
                    out=wstg[sl][:], in_=wsrc[128 * ec:128 * (ec + 1), :]), dma=True,
                    deps=[wprev.get(("c", 0)), wprev.get(("t", 1))] + deps0,
                    skey="wstgD0")
                cst = seq.op("vector", lambda e, sl=sl: e.tensor_copy(
                    out=wbf, in_=wstg[sl][:]),
                    deps=[ld, wprev.get(("t", 1)), fm_done] + deps0)
                wprev[("c", 0)] = cst
                tevs = []
                for dck in range(8):
                    b_ = dck % 2
                    tr = seq.op("tensor", lambda e, dck=dck, b_=b_: e.matmul(
                        ps_t[b_][:], wbf[:, 128 * dck:128 * (dck + 1)],
                        ident[:], is_transpose=True, start=True, stop=True),
                        deps=[cst, id_done, last_tev[b_]])
                    ev = seq.op("vector", lambda e, dck=dck, sl=sl, b_=b_: e.tensor_copy(
                        out=wts[sl][:, 128 * dck:128 * (dck + 1)], in_=ps_t[b_][:]),
                        deps=[tr, wprev.get(("mm", sl))])
                    last_tev[b_] = ev
                    tevs.append(ev)
                wprev[("t", 1)] = tevs[-1]
                for n4 in range(4):
                    pb = (ec * 4 + n4) % 2
                    def pmm(e, ec=ec, n4=n4, sl=sl, pb=pb):
                        last = None
                        for dck in range(8):
                            last = e.matmul(
                                ps_mm[pb][:],
                                wts[sl][:, 128 * dck:128 * (dck + 1)],
                                src_fm[:, TT * dck + 512 * n4: TT * dck + 512 * (n4 + 1)],
                                start=(dck == 0), stop=(dck == 7))
                        return last
                    mm = seq.op("tensor", pmm,
                                deps=tevs + [fm_done, evs.get(("pb", pb)), out_guard])
                    wprev[("mm", sl)] = mm
                    ev = seq.op("scalar", lambda e, ec=ec, n4=n4, pb=pb,
                                out_sb=out_sb, func=func, scale=scale: e.activation(
                        out=out_sb[:, TT * ec + 512 * n4: TT * ec + 512 * (n4 + 1)],
                        in_=ps_mm[pb][:], func=func, scale=scale), deps=[mm])
                    evs[(ec, n4)] = ev
                    evs[("pb", pb)] = ev
            return evs

        q_evs = proj_fm(wq, qz, AF.Relu, 0.125, [kcast], None)

        o_evs = {}
        for hp in range(8):
            for n4 in range(4):
                pb = (hp * 4 + n4) % 2
                def omm(e, hp=hp, n4=n4, pb=pb):
                    e.matmul(ps_mm[pb][0:64, :],
                             kvb[0:64, HD * (2 * hp): HD * (2 * hp) + HD],
                             qz[0:64, TT * hp + 512 * n4: TT * hp + 512 * (n4 + 1)],
                             start=True, stop=True)
                    return e.matmul(ps_mm[pb][64:128, :],
                                    kvb[64:128, HD * (2 * hp + 1): HD * (2 * hp + 1) + HD],
                                    qz[64:128, TT * hp + 512 * n4: TT * hp + 512 * (n4 + 1)],
                                    start=True, stop=True)
                mm = seq.op("tensor", omm,
                            deps=[kcast, q_evs[(hp, n4)], q_evs[(7, 3)],
                                  o_evs.get(("pb", pb))])
                ev = seq.op("scalar", lambda e, hp=hp, n4=n4, pb=pb: e.copy(
                    out=o_fm[:, TT * hp + 512 * n4: TT * hp + 512 * (n4 + 1)],
                    in_=ps_mm[pb][:]), deps=[mm])
                o_evs[(hp, n4)] = ev
                o_evs[("pb", pb)] = ev
        o_all = [o_evs[(hp, n4)] for hp in range(8) for n4 in range(4)]
        u_evs = proj_fm(wu, u_fm, AF.Silu, 1.0,
                        [q_evs[(7, 3)], o_all[-1]] + o_all, o_all[-1])

        sq_last = None
        srow_sts = []
        prev = None
        for hp in range(8):
            sq_ev = seq.op("scalar", lambda e, hp=hp: e.activation(
                out=sq4, in_=o_fm[:, TT * hp: TT * (hp + 1)], func=AF.Square),
                deps=o_all + [sq_last])
            for half in range(2):
                h = 2 * hp + half
                for n4 in range(4):
                    smm = seq.op("tensor", lambda e, half=half, n4=n4: e.matmul(
                        ps_s[:], ones64[64 * half:64 * half + 64, 0:1],
                        sq4.tensor[64 * half:64 * half + 64,
                                   6 * TT + 512 * n4:6 * TT + 512 * (n4 + 1)] if False
                        else bass.AP(sq4.tensor, 6 * TT + 64 * half * (8 * TT) // 64,
                                     [[8 * TT, 64], [1, 512]]) if False
                        else qz[64 * half:64 * half + 64,
                                6 * TT + 512 * n4:6 * TT + 512 * (n4 + 1)],
                        start=True, stop=True),
                        deps=[sq_ev, mk_o64, prev], small=True)
                    prev = seq.op("scalar", lambda e, n4=n4: e.activation(
                        out=srow[0:1, 512 * n4:512 * (n4 + 1)], in_=ps_s[:],
                        func=AF.Sqrt, scale=1.0 / 64), deps=[smm], small=True)
                sq_last = prev
                mx = seq.op("vector", lambda e: e.tensor_scalar_max(
                    srow, srow, 1e-12), deps=[prev], small=True)
                rc = seq.op("vector", lambda e: e.reciprocal(
                    out=srow, in_=srow), deps=[mx], small=True)
                cb_ = seq.op("vector", lambda e: e.tensor_copy(
                    out=srbf, in_=srow), deps=[rc], small=True)
                prev = seq.op("gpsimd", lambda e, h=h: e.dma_start(
                    out=sb_d[h:h + 1, :], in_=srbf), dma=True,
                    deps=[cb_, prev], skey="srst")
                srow_sts.append(prev)

        g_evs = {}
        for hp in range(8):
            ldb = seq.op("gpsimd", lambda e, hp=hp: e.dma_start(
                out=bc, in_=bass.AP(sb_d.tensor, 2 * hp * TT,
                                       [[TT, 2], [0, 64], [1, TT]])),
                dma=True, deps=[srow_sts[2 * hp + 1], g_evs.get("bc"),
                                o_evs[("pb", 0)], o_evs[("pb", 1)]], skey="bcldD")
            g1 = seq.op("vector", lambda e, hp=hp: e.tensor_mul(
                o_fm[:, TT * hp: TT * (hp + 1)], o_fm[:, TT * hp: TT * (hp + 1)],
                bc), deps=[ldb, sq_last] + o_all + ([srow_sts[-1]] if hp == 0 else []))
            g2 = seq.op("vector", lambda e, hp=hp: e.tensor_mul(
                o_fm[:, TT * hp: TT * (hp + 1)], o_fm[:, TT * hp: TT * (hp + 1)],
                u_fm[:, TT * hp: TT * (hp + 1)]), deps=[g1, u_evs[(hp, 3)], u_evs[(7, 3)]])
            g_evs[hp] = g2
            g_evs["bc"] = g2
        g_all = [g_evs[hp] for hp in range(8)]

        z_evs = proj_fm(wo, qz, AF.Copy, 1.0, g_all,
                        g_all[-1], src_fm=o_fm)

        prev_addx = None
        r3_sts = []
        zt_guard = {}
        for tt in range(16):
            w = tt % 2
            tevs = []
            for ch in range(8):
                b_ = ch % 2
                tr = seq.op("tensor", lambda e, ch=ch, tt=tt, b_=b_: e.matmul(
                    ps_t[b_][:], qz[:, TT * ch + 128 * tt: TT * ch + 128 * (tt + 1)],
                    ident[:], is_transpose=True, start=True, stop=True),
                    deps=[z_evs[(ch, tt // 4)], z_evs[(7, 3)], id_done, last_tev[b_]])
                ev = seq.op("vector", lambda e, ch=ch, w=w, b_=b_: e.tensor_copy(
                    out=lntile[w][:, 128 * ch:128 * (ch + 1)], in_=ps_t[b_][:]),
                    deps=[tr, zt_guard.get(w)])
                last_tev[b_] = ev
                tevs.append(ev)
            ap1 = lnh.ln(lntile[w][:], lnterm[:], tevs + [prev_addx])
            zt_guard[w] = ap1
            addx = seq.op("vector", lambda e, tt=tt: e.tensor_add(
                X[:, D * tt: D * (tt + 1)], X[:, D * tt: D * (tt + 1)], lnterm[:]),
                deps=[ap1], small=True)
            prev_addx = addx
            stx = seq.op("scalar", lambda e, tt=tt: e.dma_start(
                out=r3[128 * tt:128 * (tt + 1), :], in_=X[:, D * tt: D * (tt + 1)]),
                dma=True, deps=[addx, r3_sts[-2] if len(r3_sts) >= 2 else None],
                skey=f"r3st{tt % 2}")
            r3_sts.append(stx)
        seq.op("sync", lambda e: e.nop(), deps=r3_sts)


def build_D1():
    nc = bass.Bass("TRN2", target_bir_lowering=False)
    G = dram_D1(nc, wu_name="WU")
    with ExitStack() as st_:
        seq = Seq(nc, st_)
        block = st_.enter_context(nc.Block())
        phase_D1(nc, seq, st_, G)
        seq.emit(block)
    return nc



def build_D2():
    pass


def dram_D2(nc, r3_ap=None):
    kin = "ExternalInput"
    return {
        "R3": r3_ap if r3_ap is not None else
              nc.dram_tensor("R3", [TT, D], F32, kind=kin).ap(),
        "LN1W": nc.dram_tensor("LN1W", [D], F32, kind=kin).ap(),
        "LN1B": nc.dram_tensor("LN1B", [D], F32, kind=kin).ap(),
        "FW1": nc.dram_tensor("FW1", [4 * D, D], F32, kind=kin).ap(),
        "FB1": nc.dram_tensor("FB1", [4 * D], F32, kind=kin).ap(),
        "LN2W": nc.dram_tensor("LN2W", [4 * D], F32, kind=kin).ap(),
        "LN2B": nc.dram_tensor("LN2B", [4 * D], F32, kind=kin).ap(),
        "FW2": nc.dram_tensor("FW2", [D, 4 * D], F32, kind=kin).ap(),
        "FB2": nc.dram_tensor("FB2", [D], F32, kind=kin).ap(),
        "R4": nc.dram_tensor("R4", [TT, D], F32, kind="ExternalOutput").ap(),
    }


def phase_D2(nc, seq, st_, G):
    r3, ln1w, ln1b = G["R3"], G["LN1W"], G["LN1B"]
    fw1, fb1, ln2w, ln2b = G["FW1"], G["FB1"], G["LN2W"], G["LN2B"]
    fw2, fb2, r4 = G["FW2"], G["FB2"], G["R4"]
    xsp = nc.dram_tensor("XSP3", [TT, D], F32).ap()
    xq1sp = nc.dram_tensor("XQ1SP", [D, TT], BF16).ap()
    xq2sp = nc.dram_tensor("XQ2SP", [4 * D, TT], BF16).ap()
    swb = nc.dram_tensor("SWB", [1, 4], F32).ap()

    NTC = 8
    TCW = 256

    if True:
        sbuf = lambda n, sh, dt=F32: st_.enter_context(nc.sbuf_tensor("E" + n, sh, dt))
        psum = lambda n, sh, dt=F32: st_.enter_context(nc.psum_tensor("E" + n, sh, dt))

        X = sbuf("XE", [128, 16 * D], F32)
        w1qT = X.bitcast(BF16)
        w2qT = X[:, 0:8 * D].bitcast(BF16)
        o2acc = X[:, 8 * D:12 * D]
        h2 = sbuf("h2E", [128, 2 * 4 * D], BF16)
        xq2buf = sbuf("xq2bufE", [128, 16 * TCW], BF16)
        xq1buf = xq2buf[:, 0:8 * TCW]
        sq4 = sbuf("sq4E", [128, 4 * D], F32)
        l2w_bc = sbuf("l2wE", [128, 4 * D], BF16)
        l2b_bc = sbuf("l2bE", [128, 4 * D], BF16)
        l1w_bc = sbuf("l1wE", [128, D], BF16)
        l1b_bc = sbuf("l1bE", [128, D], BF16)
        t1 = sbuf("t1E", [128, D], F32)
        t2 = sbuf("t2E", [128, D], F32)
        qs = sbuf("qsE", [128, 4 * D], F32)
        qi = qs.bitcast(I32)
        qb = sbuf("qbE", [128, 4 * D], BF16)
        stg2 = sbuf("stg2E", [128, 4 * D], BF16)
        wstg = [sbuf(f"wstgE{i}", [128, D], F32) for i in range(2)]
        scol = sbuf("scolE", [128, 20], F32)
        icol = sbuf("icolE", [128, 16], F32)
        icolq = [sbuf(f"icolqE{i}", [128, 4], F32) for i in range(1)][0]
        acc1 = sbuf("acc1E", [128, 2], F32)
        srow = sbuf("srowE", [1, 16 * 128], BF16)
        s2row = sbuf("s2rowE", [1, 4 * 128], BF16)
        fb1row = sbuf("fb1rowE", [1, 4 * D], BF16)
        fb2row = sbuf("fb2rowE", [1, D], BF16)
        rowf = qs[0:1, 0:4 * D]
        swcol = sbuf("swcolE", [128, 4], F32)
        ident = sbuf("identE", [128, 128], BF16)
        ps_mm = [psum(f"ps_mmE{i}", [128, 512], F32) for i in range(2)]
        ps_t = [psum(f"ps_tE{i}", [128, 128], BF16) for i in range(2)]
        ps_r = psum("ps_rE", [1, 128], F32)
        ps_rb = psum("ps_rbE", [1, 128], BF16)

        id_done = mk_identity(seq, ident)
        lnh = LNHelper(seq, sbuf, "lnE", D, sq=sq4[:, 0:D])
        lnh4 = LNHelper(seq, sbuf, "lnE4", 4 * D, sq=sq4)

        last_tevE = [None, None]
        ld_x = seq.op("sync", lambda e: e.dma_start(
            out=X[:], in_=r3.rearrange("(n p) c -> p n c", p=128)),
            dma=True, private=True)
        ld1 = seq.op("gpsimd", lambda e: e.dma_start(
            out=wstg[0][:, 0:D], in_=bass.AP(ln1w.tensor, 0, [[0, 128], [1, D]])),
            dma=True, private=True)
        c1 = seq.op("vector", lambda e: e.tensor_copy(out=l1w_bc[:], in_=wstg[0][:, 0:D]),
                    deps=[ld1])
        ld2 = seq.op("gpsimd", lambda e: e.dma_start(
            out=wstg[1][:, 0:D], in_=bass.AP(ln1b.tensor, 0, [[0, 128], [1, D]])),
            dma=True, private=True)
        c2 = seq.op("vector", lambda e: e.tensor_copy(out=l1b_bc[:], in_=wstg[1][:, 0:D]),
                    deps=[ld2])
        ld3 = seq.op("gpsimd", lambda e: e.dma_start(
            out=sq4[:], in_=bass.AP(ln2w.tensor, 0, [[0, 128], [1, 4 * D]])),
            dma=True, private=True)
        c3 = seq.op("vector", lambda e: e.tensor_copy(out=l2w_bc[:], in_=sq4[:]),
                    deps=[ld3])
        ld4 = seq.op("gpsimd", lambda e: e.dma_start(
            out=sq4[:], in_=bass.AP(ln2b.tensor, 0, [[0, 128], [1, 4 * D]])),
            dma=True, private=True, deps=[c3])
        c4 = seq.op("vector", lambda e: e.tensor_copy(out=l2b_bc[:], in_=sq4[:]),
                    deps=[ld4])
        ld5 = seq.op("gpsimd", lambda e: e.dma_start(
            out=rowf[:], in_=fb1.rearrange("(a b) -> a b", a=1)), dma=True, private=True)
        c5 = seq.op("vector", lambda e: e.tensor_copy(out=fb1row[:], in_=rowf[:]),
                    deps=[ld5])
        ld6 = seq.op("gpsimd", lambda e: e.dma_start(
            out=rowf[0:1, 0:D], in_=fb2.rearrange("(a b) -> a b", a=1)), dma=True,
            private=True, deps=[c5])
        c6 = seq.op("vector", lambda e: e.tensor_copy(out=fb2row[:], in_=rowf[0:1, 0:D]),
                    deps=[ld6])

        def wmean(wsrc, nblk, acc_slice, prev):
            accops = []
            for blk in range(nblk):
                sl = blk % 2
                ld = seq.op("sync" if sl == 0 else "gpsimd",
                    lambda e, blk=blk, sl=sl, wsrc=wsrc: e.dma_start(
                    out=wstg[sl][:], in_=bass.AP(wsrc.tensor, blk * 128 * D,
                                                 [[D, 128], [1, D]])), dma=True,
                    deps=[accops[blk - 2] if blk >= 2 else prev], skey=f"wstgE{sl}")
                rd = seq.op("vector", lambda e, sl=sl, blk=blk: e.tensor_reduce(
                    out=scol[:, 18:19], in_=wstg[sl][:], axis=AX.X, op=ALU.add,
                    apply_absolute_value=True),
                    deps=[ld, accops[-1] if accops else None], small=True)
                ac = seq.op("vector", lambda e, blk=blk, acc_slice=acc_slice: (
                    e.tensor_copy(out=acc_slice, in_=scol[:, 18:19]) if blk == 0
                    else e.tensor_add(acc_slice, acc_slice, scol[:, 18:19])),
                    deps=[rd], small=True)
                accops.append(ac)
            return accops[-1]
        m1 = wmean(fw1, 32, acc1[:, 0:1], c6)
        m2 = wmean(fw2, 32, acc1[:, 1:2], m1)
        cacc = seq.op("vector", lambda e: e.tensor_copy(
            out=qb[:, 0:2], in_=acc1[:]), deps=[m2], small=True)
        mko = seq.op("vector", lambda e: e.memset(qb[:, 2:3], 1.0), small=True,
                     deps=[m2])
        tacc = seq.op("tensor", lambda e: e.matmul(
            ps_r[0:1, 0:2], qb[:, 2:3], qb[:, 0:2],
            start=True, stop=True), deps=[cacc, mko], small=True)
        sums = seq.op("scalar", lambda e: e.copy(
            out=rowf[0:1, 0:2], in_=ps_r[0:1, 0:2]), deps=[tacc], small=True)
        mn1 = seq.op("vector", lambda e: e.tensor_scalar_mul(
            rowf[0:1, 0:2], rowf[0:1, 0:2], 1.0 / (4 * D * D)),
            deps=[sums], small=True)
        mn2 = seq.op("vector", lambda e: e.tensor_scalar_max(
            rowf[0:1, 0:2], rowf[0:1, 0:2], 1e-5), deps=[mn1], small=True)
        mn = seq.op("vector", lambda e: e.reciprocal(
            out=rowf[0:1, 2:4], in_=rowf[0:1, 0:2]), deps=[mn2], small=True)
        pk = seq.op("vector", lambda e: (
            e.tensor_copy(out=rowf[0:1, 4:5], in_=rowf[0:1, 2:3]),
            e.tensor_copy(out=rowf[0:1, 5:6], in_=rowf[0:1, 0:1]),
            e.tensor_copy(out=rowf[0:1, 6:7], in_=rowf[0:1, 3:4]),
            e.tensor_copy(out=rowf[0:1, 7:8], in_=rowf[0:1, 1:2]),
        )[-1], deps=[mn], small=True)
        stsw = seq.op("gpsimd", lambda e: e.dma_start(
            out=swb, in_=rowf[0:1, 4:8]), dma=True, deps=[pk], skey="swst")
        ldsw = seq.op("gpsimd", lambda e: e.dma_start(
            out=swcol[:], in_=bass.AP(swb.tensor, 0, [[0, 128], [1, 4]])),
            dma=True, deps=[stsw], skey="swld")

        xq_sts = []
        prev_q = None
        srow_evs = []
        for tt in range(16):
            a1 = lnh.ln(X[:, D * tt: D * (tt + 1)], t1[:], [ld_x, prev_q])
            a2 = lnh.ln(t1[:], t2[:], [a1])
            af1 = seq.op("vector", lambda e: e.tensor_mul(t2[:], t2[:], l1w_bc[:]),
                         deps=[a2, c1])
            af2 = seq.op("vector", lambda e: e.tensor_add(t2[:], t2[:], l1b_bc[:]),
                         deps=[af1, c2])
            am = seq.op("vector", lambda e, tt=tt: e.tensor_reduce(
                out=scol[:, 16:17], in_=t2[:], axis=AX.X, op=ALU.max,
                apply_absolute_value=True), deps=[af2], small=True)
            am2a = seq.op("vector", lambda e: e.tensor_scalar_max(
                scol[:, 16:17], scol[:, 16:17], 1e-5), deps=[am], small=True)
            am2b = seq.op("vector", lambda e, tt=tt: e.tensor_scalar_mul(
                icol[:, tt:tt + 1], scol[:, 16:17], 1.0 / 127), deps=[am2a], small=True)
            am2c = seq.op("vector", lambda e: e.reciprocal(
                out=scol[:, 17:18], in_=scol[:, 16:17]), deps=[am2a], small=True)
            am2 = seq.op("vector", lambda e, tt=tt: e.tensor_scalar_mul(
                scol[:, tt:tt + 1], scol[:, 17:18], 127.0), deps=[am2c, am2b], small=True)
            qm = seq.op("vector", lambda e, tt=tt: e.tensor_scalar_mul(
                qs[:, 0:D], t2[:], scol[:, tt:tt + 1]), deps=[am2, ldsw])
            qm = seq.op("vector", lambda e: e.tensor_scalar(
                out=qs[:, 0:D], in0=qs[:, 0:D], scalar1=MAGIC, scalar2=-MAGIC,
                op0=ALU.add, op1=ALU.add), deps=[qm])
            qc = seq.op("vector", lambda e: e.tensor_copy(
                out=qi[:, 0:D], in_=qs[:, 0:D]), deps=[qm])
            qf = seq.op("scalar", lambda e: e.copy(
                out=qb[:, 0:D], in_=qi[:, 0:D]), deps=[qc])
            tevs = []
            for dck in range(8):
                b_ = dck % 2
                tr = seq.op("tensor", lambda e, dck=dck, b_=b_: e.matmul(
                    ps_t[b_][:], qb[:, 128 * dck:128 * (dck + 1)], ident[:],
                    is_transpose=True, start=True, stop=True),
                    deps=[qf, id_done, prev_q, last_tevE[b_]])
                ev = seq.op("vector", lambda e, dck=dck, b_=b_: e.tensor_copy(
                    out=stg2[:, 128 * dck:128 * (dck + 1)], in_=ps_t[b_][:]),
                    deps=[tr, prev_q])
                last_tevE[b_] = ev
                tevs.append(ev)
            st = seq.op("sync", lambda e, tt=tt: e.dma_start(
                out=bass.AP(xq1sp.tensor, 128 * tt,
                            [[TT, 128], [128 * TT, 8], [1, 128]]),
                in_=stg2[:, 0:8 * 128]), dma=True, deps=tevs, skey="xq1st")
            xq_sts.append(st)
            prev_q = st
        srow_ops = []
        for tt in range(16):
            scm = seq.op("vector", lambda e, tt=tt: e.tensor_mul(
                scol[:, 16:17], scol[:, tt:tt + 1], swcol[:, 0:1]),
                deps=[ldsw, xq_sts[-1] if tt == 0 else None,
                      srow_ops[-1] if srow_ops else None], small=True)
            scb = seq.op("vector", lambda e: e.tensor_copy(
                out=qb[:, 0:1], in_=scol[:, 16:17]), deps=[scm], small=True)
            trw = seq.op("tensor", lambda e: e.matmul(
                ps_rb[0:1, 0:128], qb[:, 0:1], ident[:, 0:128],
                is_transpose=True, start=True, stop=True), deps=[scb, id_done],
                small=True)
            evw = seq.op("vector", lambda e, tt=tt: e.tensor_copy(
                out=srow[0:1, 128 * tt:128 * (tt + 1)], in_=ps_rb[0:1, 0:128]),
                deps=[trw], small=True)
            srow_ops.append(evw)
        srow_done = srow_ops[-1]
        ic2 = seq.op("vector", lambda e: e.tensor_scalar_mul(
            icol[:], icol[:], swcol[:, 1:2]), deps=[ldsw, srow_done], small=True)

        sp_x = seq.op("sync", lambda e: e.dma_start(
            out=xsp.rearrange("(n p) c -> p n c", p=128), in_=X[:]),
            dma=True, private=True, deps=[prev_q, srow_done])

        r4_sts = []
        w1_guard = [sp_x]
        prevld = {0: m2, 1: m2}
        fin_guard = None
        for tc in range(NTC):
            w1tevs = []
            for blk in range(32):
                sl = blk % 2
                ld = seq.op("sync" if sl == 0 else "gpsimd",
                    lambda e, blk=blk, sl=sl: e.dma_start(
                    out=wstg[sl][:], in_=bass.AP(fw1.tensor, blk * 128 * D,
                                                 [[D, 128], [1, D]])), dma=True,
                    deps=[prevld.get(sl)], skey=f"wstgE{sl}")
                qm1 = seq.op("vector", lambda e, sl=sl: e.tensor_scalar_mul(
                    qs[:, 0:D], wstg[sl][:], swcol[:, 0:1]),
                    deps=[ld, ldsw] + (w1_guard if blk < 2 else []) +
                    ([fin_guard] if blk < 2 else []))
                prevld[sl] = qm1
                qm1 = seq.op("vector", lambda e: e.tensor_scalar(
                    out=qs[:, 0:D], in0=qs[:, 0:D], scalar1=MAGIC, scalar2=-MAGIC,
                    op0=ALU.add, op1=ALU.add), deps=[qm1])
                qc1 = seq.op("vector", lambda e: e.tensor_copy(
                    out=qi[:, 0:D], in_=qs[:, 0:D]), deps=[qm1])
                qf1 = seq.op("scalar", lambda e: e.copy(
                    out=qb[:, 0:D], in_=qi[:, 0:D]), deps=[qc1])
                qcl = seq.op("vector", lambda e: e.tensor_scalar(
                    out=qb[:, 0:D], in0=qb[:, 0:D], scalar1=1.0, scalar2=-1.0,
                    op0=ALU.min, op1=ALU.max), deps=[qf1])
                for dck in range(8):
                    b_ = dck % 2
                    tr = seq.op("tensor", lambda e, dck=dck, b_=b_: e.matmul(
                        ps_t[b_][:], qb[:, 128 * dck:128 * (dck + 1)], ident[:],
                        is_transpose=True, start=True, stop=True),
                        deps=[qcl, id_done, last_tevE[b_]])
                    ev = seq.op("vector", lambda e, dck=dck, blk=blk, b_=b_: e.tensor_copy(
                        out=w1qT[:, 4096 * dck + 128 * blk: 4096 * dck + 128 * (blk + 1)],
                        in_=ps_t[b_][:]), deps=[tr] + (w1_guard if blk < 2 else []))
                    last_tevE[b_] = ev
                    w1tevs.append(ev)
            ldxq1 = seq.op("sync", lambda e, tc=tc: e.dma_start(
                out=xq1buf, in_=bass.AP(xq1sp.tensor, TCW * tc,
                                        [[TT, 128], [128 * TT, 8], [1, TCW]])),
                dma=True, deps=xq_sts + [prevld.get("xq1"), fin_guard], skey="xq1ld")
            he_evs = {}
            NTT2 = TCW // 128
            for tt2 in range(NTT2):
                for hb8 in range(8):
                    pb = (tt2 * 8 + hb8) % 2
                    def hmm(e, tt2=tt2, hb8=hb8, pb=pb, tc=tc):
                        last = None
                        for dck in range(8):
                            last = e.matmul(
                                ps_mm[pb][:],
                                xq1buf[:, TCW * dck + 128 * tt2: TCW * dck + 128 * (tt2 + 1)],
                                w1qT[:, 4096 * dck + 512 * hb8: 4096 * dck + 512 * (hb8 + 1)],
                                start=(dck == 0), stop=False)
                        return e.matmul(
                            ps_mm[pb][:],
                            srow[0:1, 128 * (NTT2 * tc + tt2): 128 * (NTT2 * tc + tt2 + 1)],
                            fb1row[0:1, 512 * hb8: 512 * (hb8 + 1)],
                            start=False, stop=True)
                    mm = seq.op("tensor", hmm,
                                deps=[ldxq1, srow_done, c5, he_evs.get(("pb", pb))]
                                + w1tevs[-1:])
                    iv = seq.op("vector", lambda e, tt2=tt2, hb8=hb8, pb=pb, tc=tc: e.tensor_scalar_mul(
                        qs[:, 512 * pb:512 * (pb + 1)], ps_mm[pb][:],
                        icol[:, NTT2 * tc + tt2: NTT2 * tc + tt2 + 1]),
                        deps=[mm, ic2, he_evs.get(("iv", pb))])
                    ev = seq.op("scalar", lambda e, tt2=tt2, hb8=hb8, pb=pb: e.activation(
                        out=h2[:, 4096 * tt2 + 512 * hb8: 4096 * tt2 + 512 * (hb8 + 1)],
                        in_=qs[:, 512 * pb:512 * (pb + 1)], func=AF.Gelu), deps=[iv])
                    he_evs[(tt2, hb8)] = ev
                    he_evs[("pb", pb)] = ev
                    he_evs[("iv", pb)] = iv
            he_all = [he_evs[(a, b)] for a in range(NTT2) for b in range(8)]
            xq2_sts = []
            prev2 = None
            for tt2 in range(NTT2):
                a2 = lnh4.ln(h2[:, 4096 * tt2: 4096 * (tt2 + 1)], qs[:],
                             he_all + [prev2])
                af1 = seq.op("vector", lambda e: e.tensor_mul(qs[:], qs[:], l2w_bc[:]),
                             deps=[a2, c3])
                af2 = seq.op("vector", lambda e: e.tensor_add(qs[:], qs[:], l2b_bc[:]),
                             deps=[af1, c4])
                am = seq.op("vector", lambda e: e.tensor_reduce(
                    out=scol[:, 16:17], in_=qs[:], axis=AX.X, op=ALU.max,
                    apply_absolute_value=True), deps=[af2], small=True)
                am2a = seq.op("vector", lambda e: e.tensor_scalar_max(
                    scol[:, 16:17], scol[:, 16:17], 1e-5), deps=[am], small=True)
                am2b = seq.op("vector", lambda e, tt2=tt2: e.tensor_scalar_mul(
                    icolq[:, tt2:tt2 + 1], scol[:, 16:17], 1.0 / 127),
                    deps=[am2a], small=True)
                am2c = seq.op("vector", lambda e: e.reciprocal(
                    out=scol[:, 17:18], in_=scol[:, 16:17]), deps=[am2a], small=True)
                am2 = seq.op("vector", lambda e: e.tensor_scalar_mul(
                    scol[:, 18:19], scol[:, 17:18], 127.0), deps=[am2c, am2b], small=True)
                qm = seq.op("vector", lambda e: e.tensor_scalar_mul(
                    qs[:], qs[:], scol[:, 18:19]), deps=[am2])
                qm = seq.op("vector", lambda e: e.tensor_scalar(
                    out=qs[:], in0=qs[:], scalar1=MAGIC, scalar2=-MAGIC,
                    op0=ALU.add, op1=ALU.add), deps=[qm])
                qc = seq.op("vector", lambda e: e.tensor_copy(
                    out=qi[:], in_=qs[:]), deps=[qm])
                qf = seq.op("scalar", lambda e: e.copy(
                    out=qb[:], in_=qi[:]), deps=[qc])
                scm = seq.op("vector", lambda e, tt2=tt2: e.tensor_mul(
                    scol[:, 16:17], scol[:, 18:19], swcol[:, 2:3]),
                    deps=[qm, ldsw], small=True)
                scb = seq.op("vector", lambda e: e.tensor_copy(
                    out=stg2[:, 0:1], in_=scol[:, 16:17]), deps=[scm, prev2], small=True)
                trw = seq.op("tensor", lambda e: e.matmul(
                    ps_rb[0:1, 0:128], stg2[:, 0:1], ident[:, 0:128],
                    is_transpose=True, start=True, stop=True), deps=[scb, id_done],
                    small=True)
                evw = seq.op("vector", lambda e, tt2=tt2: e.tensor_copy(
                    out=s2row[0:1, 128 * tt2:128 * (tt2 + 1)], in_=ps_rb[0:1, 0:128]),
                    deps=[trw], small=True)
                ic2b = seq.op("vector", lambda e, tt2=tt2: e.tensor_scalar_mul(
                    icolq[:, tt2:tt2 + 1], icolq[:, tt2:tt2 + 1], swcol[:, 3:4]),
                    deps=[am2, ldsw], small=True)
                tevs = []
                for hck in range(32):
                    b_ = hck % 2
                    tr = seq.op("tensor", lambda e, hck=hck, b_=b_: e.matmul(
                        ps_t[b_][:], qb[:, 128 * hck:128 * (hck + 1)], ident[:],
                        is_transpose=True, start=True, stop=True),
                        deps=[qf, id_done, prev2, last_tevE[b_]])
                    ev = seq.op("vector", lambda e, hck=hck, b_=b_: e.tensor_copy(
                        out=stg2[:, 128 * hck:128 * (hck + 1)], in_=ps_t[b_][:]),
                        deps=[tr, prev2, evw])
                    last_tevE[b_] = ev
                    tevs.append(ev)
                st = seq.op("sync", lambda e, tc=tc, tt2=tt2: e.dma_start(
                    out=bass.AP(xq2sp.tensor, TCW * tc + 128 * tt2,
                                [[TT, 128], [128 * TT, 32], [1, 128]]),
                    in_=stg2[:]), dma=True, deps=tevs, skey="xq2st")
                xq2_sts.append(st)
                prev2 = st
            o2evs = {}
            for half in range(2):
                w2tevs = []
                for dblk in range(8):
                    for hq in range(2):
                        hqg = 2 * half + hq
                        sl = (dblk * 2 + hq) % 2
                        ld = seq.op("sync" if sl == 0 else "gpsimd",
                            lambda e, dblk=dblk, hqg=hqg, sl=sl: e.dma_start(
                            out=wstg[sl][:], in_=bass.AP(
                                fw2.tensor, dblk * 128 * 4 * D + D * hqg,
                                [[4 * D, 128], [1, D]])), dma=True,
                            deps=[prevld.get(sl)], skey=f"wstgE{sl}")
                        qm1 = seq.op("vector", lambda e, sl=sl: e.tensor_scalar_mul(
                            qs[:, 0:D], wstg[sl][:], swcol[:, 2:3]),
                            deps=[ld, ldsw, xq2_sts[-1],
                                  o2evs.get("done") if half == 1 else he_evs[("pb", 0)],
                                  he_evs[("pb", 1)]])
                        prevld[sl] = qm1
                        qm1 = seq.op("vector", lambda e: e.tensor_scalar(
                            out=qs[:, 0:D], in0=qs[:, 0:D], scalar1=MAGIC, scalar2=-MAGIC,
                            op0=ALU.add, op1=ALU.add), deps=[qm1])
                        qc1 = seq.op("vector", lambda e: e.tensor_copy(
                            out=qi[:, 0:D], in_=qs[:, 0:D]), deps=[qm1])
                        qf1 = seq.op("scalar", lambda e: e.copy(
                            out=qb[:, 0:D], in_=qi[:, 0:D]), deps=[qc1])
                        qcl = seq.op("vector", lambda e: e.tensor_scalar(
                            out=qb[:, 0:D], in0=qb[:, 0:D], scalar1=1.0, scalar2=-1.0,
                            op0=ALU.min, op1=ALU.max), deps=[qf1])
                        for j in range(8):
                            b_ = j % 2
                            hckL = 8 * hq + j
                            tr = seq.op("tensor", lambda e, j=j, b_=b_: e.matmul(
                                ps_t[b_][:], qb[:, 128 * j:128 * (j + 1)], ident[:],
                                is_transpose=True, start=True, stop=True),
                                deps=[qcl, id_done, last_tevE[b_]])
                            ev = seq.op("vector", lambda e, hckL=hckL, dblk=dblk, b_=b_: e.tensor_copy(
                                out=w2qT[:, 1024 * hckL + 128 * dblk: 1024 * hckL + 128 * (dblk + 1)],
                                in_=ps_t[b_][:]), deps=[tr])
                            last_tevE[b_] = ev
                            w2tevs.append(ev)
                ldxq2 = seq.op("sync", lambda e, tc=tc, half=half: e.dma_start(
                    out=xq2buf[:], in_=bass.AP(
                        xq2sp.tensor, TCW * tc + 16 * 128 * TT * half,
                        [[TT, 128], [128 * TT, 16], [1, TCW]])),
                    dma=True, deps=xq2_sts + [prevld.get("xq2"),
                                              he_evs[("pb", 0)], he_evs[("pb", 1)]],
                    skey="xq2ld")
                prevld["xq2"] = ldxq2
                prevld["xq1"] = ldxq2
                for tt2 in range(NTT2):
                    for dc2 in range(2):
                        pb = (tt2 * 2 + dc2) % 2
                        def ymm(e, tt2=tt2, dc2=dc2, pb=pb, half=half):
                            last = None
                            for j in range(16):
                                last = e.matmul(
                                    ps_mm[pb][:],
                                    xq2buf[:, TCW * j + 128 * tt2: TCW * j + 128 * (tt2 + 1)],
                                    w2qT[:, 1024 * j + 512 * dc2: 1024 * j + 512 * (dc2 + 1)],
                                    start=(j == 0), stop=(half == 1 and j == 15))
                            if half == 0:
                                last = e.matmul(
                                    ps_mm[pb][:],
                                    s2row[0:1, 128 * tt2: 128 * (tt2 + 1)],
                                    fb2row[0:1, 512 * dc2: 512 * (dc2 + 1)],
                                    start=False, stop=True)
                            return last
                        mm = seq.op("tensor", ymm,
                                    deps=[ldxq2, w2tevs[-1], c6, o2evs.get(("pb", pb))])
                        if half == 0:
                            ev = seq.op("vector", lambda e, tt2=tt2, dc2=dc2, pb=pb: e.tensor_copy(
                                out=o2acc[:, 1024 * tt2 + 512 * dc2: 1024 * tt2 + 512 * (dc2 + 1)],
                                in_=ps_mm[pb][:]),
                                deps=[mm] + ([fin_guard] if tt2 == 0 and dc2 == 0 else []))
                        else:
                            ev = seq.op("vector", lambda e, tt2=tt2, dc2=dc2, pb=pb: e.tensor_add(
                                o2acc[:, 1024 * tt2 + 512 * dc2: 1024 * tt2 + 512 * (dc2 + 1)],
                                o2acc[:, 1024 * tt2 + 512 * dc2: 1024 * tt2 + 512 * (dc2 + 1)],
                                ps_mm[pb][:]), deps=[mm])
                        o2evs[(half, tt2, dc2)] = ev
                        o2evs[("pb", pb)] = ev
                o2evs["done"] = o2evs[("pb", 0)]
            o2_all = [o2evs[(h_, a, b)] for h_ in range(2) for a in range(NTT2)
                      for b in range(2)]
            for tt2 in range(NTT2):
                tt = NTT2 * tc + tt2
                iv = seq.op("vector", lambda e, tt2=tt2: e.tensor_scalar_mul(
                    qs[:, 0:D], o2acc[:, 1024 * tt2: 1024 * (tt2 + 1)],
                    icolq[:, tt2:tt2 + 1]), deps=o2_all + [fin_guard])
                a1 = lnh.ln(qs[:, 0:D], t1[:], [iv])
                ldr = seq.op("sync", lambda e, tt=tt: e.dma_start(
                    out=t2[:], in_=xsp[128 * tt:128 * (tt + 1), :]), dma=True,
                    deps=[sp_x, fin_guard], skey="xrld")
                ad = seq.op("vector", lambda e: e.tensor_add(
                    t2[:], t2[:], t1[:]), deps=[a1, ldr], small=True)
                stx = seq.op("scalar", lambda e, tt=tt: e.dma_start(
                    out=r4[128 * tt:128 * (tt + 1), :], in_=t2[:]), dma=True,
                    deps=[ad], skey="r4st")
                r4_sts.append(stx)
                fin_guard = stx
        seq.op("sync", lambda e: e.nop(), deps=r4_sts)


def build_D2():
    nc = bass.Bass("TRN2", target_bir_lowering=False)
    G = dram_D2(nc)
    with ExitStack() as st_:
        seq = Seq(nc, st_)
        block = st_.enter_context(nc.Block())
        phase_D2(nc, seq, st_, G)
        seq.emit(block)
    return nc



def build_FUSED():
    nc = bass.Bass("TRN2", target_bir_lowering=False)
    GA = dram_A(nc, kout="Internal")
    GB = dram_B(nc, kout="Internal", aun_ap=GA["AUN"], den_ap=GA["DEN"])
    GC = dram_C(nc, kout="Internal", r2_ap=GB["R2"])
    kvx = nc.dram_tensor("KVX", [128, H * HD], F32).ap()
    GD1 = dram_D1(nc, kout="Internal", r2_ap=GB["R2"],
                  kvps_ap=kvx[0:64, :], kvpo_ap=kvx[64:128, :])
    GD2 = dram_D2(nc, r3_ap=GD1["R3"])
    with ExitStack() as st_top:
        seq = Seq(nc, st_top)
        block = st_top.enter_context(nc.Block())
        for ph, G in ((phase_A, GA), (phase_B, GB), (phase_C, GC)):
            with ExitStack() as st_:
                ph(nc, seq, st_, G)
            seq.barrier()
        seq.op("gpsimd", lambda e: e.collective_compute(
            "AllGather", ALU.bypass,
            replica_groups=[[0, 1], [2, 3], [4, 5], [6, 7]],
            ins=[GC["KVP"]], outs=[kvx]))
        seq.barrier()
        for ph, G in ((phase_D1, GD1), (phase_D2, GD2)):
            with ExitStack() as st_:
                ph(nc, seq, st_, G)
            seq.barrier()
        seq.emit(block)
    return nc



def _perm_my(p):
    col = np.arange(TT)
    return 32 * (col // 16) + 16 * p + (col % 16)


def _perm_q(p):
    idx = np.arange(TT)
    return 256 * (idx // 128) + 2 * (idx % 128) + p


def _masks_np(p):
    r = np.arange(128)[:, None, None, None]
    a_s = np.arange(32)[None, :, None]
    j = np.arange(16)[None, None, :]
    out = np.zeros((128, 32, 32, 16), np.float32)
    for jk in range(16):
        for dD in range(2):
            F = 1024 * dD + 32 * a_s + 16 * p + j - 16 * r[:, 0] - jk
            out[:, 2 * jk + dD] = (F >= 0).astype(np.float32)
    return np.ascontiguousarray(out.reshape(128, 32 * 512)).astype(BF16NP)


def run_launch_A(x, wqkv, bqkv):
    ncA = build_A()
    in_maps = []
    for i in range(8):
        b, p = i // 2, i % 2
        in_maps.append({
            "XQ": np.ascontiguousarray(x[b][_perm_q(p)]),
            "XF": np.ascontiguousarray(x[b]),
            "WQKV": np.ascontiguousarray(wqkv),
            "BQKV": np.ascontiguousarray(bqkv),
            "MASKS": _masks_np(p),
        })
    res = run_bass_kernel_spmd(ncA, in_maps, core_ids=list(range(8)))
    return [{k: np.asarray(v) for k, v in r.items()} for r in res.results]


def kernel(**inputs):
    f32c = lambda a: np.ascontiguousarray(np.asarray(a), dtype=np.float32)
    x = f32c(inputs["L0_attn_x"])
    ncF = build_FUSED()
    in_maps = []
    for i in range(8):
        b, p = i // 2, i % 2
        in_maps.append({
            "XQ": np.ascontiguousarray(x[b][_perm_q(p)]),
            "XF": np.ascontiguousarray(x[b]),
            "WQKV": f32c(inputs["L0_attn_Wqkv"]),
            "BQKV": f32c(inputs["L0_attn_bqkv"]),
            "MASKS": _masks_np(p),
            "XMY": np.ascontiguousarray(x[b][_perm_my(p)]),
            "WPROJ": f32c(inputs["L0_attn_Wproj"]),
            "BPROJ": f32c(inputs["L0_attn_bproj"]),
            "W1": f32c(inputs["L0_ffn_W1"]), "B1": f32c(inputs["L0_ffn_b1"]),
            "W2": f32c(inputs["L0_ffn_W2"]), "B2": f32c(inputs["L0_ffn_b2"]),
            "WU": f32c(inputs["L0_ffn_Wu"]), "BU": f32c(inputs["L0_ffn_bu"]),
            "WK": f32c(inputs["L1_attn_Wk"]), "WV": f32c(inputs["L1_attn_Wv"]),
            "WQ": f32c(inputs["L1_attn_Wq"]), "WUQ": f32c(inputs["L1_attn_Wu"]),
            "WO": f32c(inputs["L1_attn_Wo"]),
            "LN1W": f32c(inputs["L1_ffn_ln1_w"]), "LN1B": f32c(inputs["L1_ffn_ln1_b"]),
            "FW1": f32c(inputs["L1_ffn_w1"]), "FB1": f32c(inputs["L1_ffn_b1"]),
            "LN2W": f32c(inputs["L1_ffn_ln2_w"]), "LN2B": f32c(inputs["L1_ffn_ln2_b"]),
            "FW2": f32c(inputs["L1_ffn_w2"]), "FB2": f32c(inputs["L1_ffn_b2"]),
        })
    res = run_bass_kernel_spmd(ncF, in_maps, core_ids=list(range(8)))
    out = np.empty((NB, TB, D), dtype=np.float32)
    for i in range(8):
        b, p = i // 2, i % 2
        out[b, _perm_my(p)] = np.asarray(res.results[i]["R4"])
    return out


def kernel_unfused(**inputs):
    f32c = lambda a: np.ascontiguousarray(np.asarray(a), dtype=np.float32)
    x = f32c(inputs["L0_attn_x"])
    cores = list(range(8))

    outsA = run_launch_A(x, f32c(inputs["L0_attn_Wqkv"]), f32c(inputs["L0_attn_bqkv"]))

    ncB = build_B()
    in_maps = []
    for i in cores:
        b, p = i // 2, i % 2
        in_maps.append({
            "XMY": np.ascontiguousarray(x[b][_perm_my(p)]),
            "AUN": outsA[i]["AUN"], "DEN": outsA[i]["DEN"],
            "WPROJ": f32c(inputs["L0_attn_Wproj"]),
            "BPROJ": f32c(inputs["L0_attn_bproj"]),
            "W1": f32c(inputs["L0_ffn_W1"]), "B1": f32c(inputs["L0_ffn_b1"]),
            "W2": f32c(inputs["L0_ffn_W2"]), "B2": f32c(inputs["L0_ffn_b2"]),
            "WU": f32c(inputs["L0_ffn_Wu"]), "BU": f32c(inputs["L0_ffn_bu"]),
        })
    resB = run_bass_kernel_spmd(ncB, in_maps, core_ids=cores)
    r2s = [np.asarray(resB.results[i]["R2"]) for i in cores]

    ncC = build_C()
    in_maps = [{"R2": r2s[i],
                "WK": f32c(inputs["L1_attn_Wk"]),
                "WV": f32c(inputs["L1_attn_Wv"])} for i in cores]
    resC = run_bass_kernel_spmd(ncC, in_maps, core_ids=cores)
    kvps = [np.asarray(resC.results[i]["KVP"]) for i in cores]

    ncD1 = build_D1()
    in_maps = [{"R2": r2s[i],
                "KVPS": kvps[i], "KVPO": kvps[i ^ 1],
                "WQ": f32c(inputs["L1_attn_Wq"]),
                "WU": f32c(inputs["L1_attn_Wu"]),
                "WO": f32c(inputs["L1_attn_Wo"])} for i in cores]
    resD1 = run_bass_kernel_spmd(ncD1, in_maps, core_ids=cores)
    r3s = [np.asarray(resD1.results[i]["R3"]) for i in cores]

    ncD2 = build_D2()
    in_maps = [{"R3": r3s[i],
                "LN1W": f32c(inputs["L1_ffn_ln1_w"]), "LN1B": f32c(inputs["L1_ffn_ln1_b"]),
                "FW1": f32c(inputs["L1_ffn_w1"]), "FB1": f32c(inputs["L1_ffn_b1"]),
                "LN2W": f32c(inputs["L1_ffn_ln2_w"]), "LN2B": f32c(inputs["L1_ffn_ln2_b"]),
                "FW2": f32c(inputs["L1_ffn_w2"]), "FB2": f32c(inputs["L1_ffn_b2"])}
               for i in cores]
    resD2 = run_bass_kernel_spmd(ncD2, in_maps, core_ids=cores)

    out = np.empty((NB, TB, D), dtype=np.float32)
    for i in cores:
        b, p = i // 2, i % 2
        out[b, _perm_my(p)] = np.asarray(resD2.results[i]["R4"])
    return out


# version 06 (final)
# speedup vs baseline: 1.0166x; 1.0166x over previous
import os as _os
import numpy as np
import ml_dtypes
from contextlib import ExitStack

SEQ_ALL_EDGES = bool(int(_os.environ.get("SEQ_ALL_EDGES", "0")))

import concourse.bass as bass
import concourse.mybir as mybir
from concourse.bass_utils import run_bass_kernel_spmd

F32 = mybir.dt.float32
BF16 = mybir.dt.bfloat16
I32 = mybir.dt.int32
AF = mybir.ActivationFunctionType
ALU = mybir.AluOpType
AX = mybir.AxisListType

BF16NP = ml_dtypes.bfloat16

D = 1024
MAGIC = 12582912.0
H = 16
HD = 64
TB = 4096
TT = 2048
NB = 4
EPS_LN = 1e-5



class Seq:

    ENGINES = ("sync", "scalar", "vector", "gpsimd", "tensor")

    def __init__(self, nc, stack, serialize=False, serialize_from=None,
                 serialize_to=None):
        self.nc = nc
        self.stack = stack
        self.serialize = serialize
        self.ser_from = serialize_from
        self.ser_to = serialize_to
        self.gidx = 0
        self.sems = {}
        self.count = {}
        for e in self.ENGINES:
            for kind in ("c", "d"):
                k = f"{e}_{kind}"
                self.sems[k] = stack.enter_context(nc.semaphore(f"sq_{k}"))
                self.count[k] = 0
        self.ops = {e: [] for e in self.ENGINES}
        self.waited = {e: {} for e in self.ENGINES}
        self.nprev = 0
        self.floor = []

    def barrier(self):
        self.floor = [(k, v, None) for k, v in self.count.items() if v > 0]

    def op(self, engine, emit, deps=(), dma=False, tag="", private=False,
           skey=None, small=False):
        inc = 16 if dma else 1
        if self.floor:
            deps = list(deps) + self.floor
        self.gidx += 1
        ser = self.serialize or (
            self.ser_from is not None and self.ser_from <= self.gidx
            and (self.ser_to is None or self.gidx < self.ser_to))
        key = f"{engine}_{'d' if dma else 'c'}"
        if skey is not None:
            key = skey
            if key not in self.sems:
                self.sems[key] = self.stack.enter_context(self.nc.semaphore(key))
                self.count[key] = 0
        waits = []
        if ser:
            deps = list(deps) + [(k, v, None) for k, v in self.count.items()
                                 if v > 0 and not k.startswith(engine)]
        for dep in deps:
            if dep is None:
                continue
            dk, dv, ddma = dep
            if (not SEQ_ALL_EDGES and dk == f"{engine}_c" and not dma
                    and ddma is False):
                continue
            if dv > self.waited[engine].get(dk, 0):
                waits.append((dk, dv))
                self.waited[engine][dk] = dv
        if private:
            self.nprev += 1
            pkey = f"pv{self.nprev}"
            self.sems[pkey] = self.stack.enter_context(self.nc.semaphore(pkey))
            self.ops[engine].append((emit, waits, inc, pkey))
            return (pkey, inc, dma or small)
        self.ops[engine].append((emit, waits, inc, key))
        self.count[key] += inc
        return (key, self.count[key], dma or small)

    def emit(self, block):
        for e in self.ENGINES:
            ops = self.ops[e]
            if not ops:
                continue
            def body(eng, ops=ops, e=e):
                for (emit_fn, waits, inc, key) in ops:
                    for (pe, v) in waits:
                        eng.wait_ge(self.sems[pe], v)
                    inst = emit_fn(eng)
                    if inst is not None:
                        inst.then_inc(self.sems[key], inc)
            getattr(block, e)(body)


def mk_identity(seq, ident):
    m = seq.op("gpsimd", lambda e: e.memset(ident[:], 0.0))
    return seq.op("gpsimd", lambda e: e.affine_select(
        out=ident[:], in_=ident[:], compare_op=ALU.not_equal,
        fill=1.0, base=0, pattern=[[-1, ident.shape[1]]],
        channel_multiplier=1), deps=[m])


class LNHelper:

    def __init__(self, seq, sbuf, name, width, eps=EPS_LN, sq=None):
        self.seq = seq
        self.w = width
        self.sq = sq if sq is not None else sbuf(f"{name}_sq", [128, width], F32)
        self.st = sbuf(f"{name}_st", [128, 8], F32)
        self.eps_col = sbuf(f"{name}_eps", [128, 1], F32)
        self.eps_init = seq.op("vector", lambda e: e.memset(self.eps_col[:], eps))

    def ln(self, src_ap, out_ap, deps):
        seq, st, w = self.seq, self.st, self.w
        d1 = seq.op("vector", lambda e: e.reduce_sum(
            out=st[:, 0:1], in_=src_ap, axis=AX.X), deps=deps, small=True)
        d2 = seq.op("scalar", lambda e: e.activation(
            out=self.sq[:], in_=src_ap, func=AF.Square), deps=deps)
        d3 = seq.op("vector", lambda e: e.reduce_sum(
            out=st[:, 1:2], in_=self.sq[:], axis=AX.X), deps=[d2], small=True)
        d4a = seq.op("vector", lambda e: e.tensor_scalar_mul(
            st[:, 2:3], st[:, 0:1], 1.0 / w), deps=[d1], small=True)
        d4b = seq.op("vector", lambda e: e.tensor_mul(
            st[:, 5:6], st[:, 2:3], st[:, 2:3]), deps=[d4a], small=True)
        d4 = seq.op("vector", lambda e: e.tensor_scalar(
            out=st[:, 3:4], in0=st[:, 1:2], scalar1=1.0 / w, scalar2=st[:, 5:6],
            op0=ALU.mult, op1=ALU.subtract), deps=[d3, d4b], small=True)
        d5 = seq.op("scalar", lambda e: e.activation(
            out=st[:, 4:5], in_=st[:, 3:4], func=AF.Sqrt, bias=self.eps_col[:]),
            deps=[d4, self.eps_init], small=True)
        d6 = seq.op("vector", lambda e: e.reciprocal(
            out=st[:, 6:7], in_=st[:, 4:5]), deps=[d5], small=True)
        d7 = seq.op("vector", lambda e: e.tensor_scalar(
            out=out_ap, in0=src_ap, scalar1=st[:, 2:3],
            scalar2=st[:, 6:7], op0=ALU.subtract, op1=ALU.mult), deps=[d4a, d6])
        return d7



def dram_A(nc, kout="ExternalOutput"):
    kin = "ExternalInput"
    return {
        "XQ": nc.dram_tensor("XQ", [TT, D], F32, kind=kin).ap(),
        "XF": nc.dram_tensor("XF", [TB, D], F32, kind=kin).ap(),
        "WQKV": nc.dram_tensor("WQKV", [D, 3 * D], F32, kind=kin).ap(),
        "BQKV": nc.dram_tensor("BQKV", [3 * D], F32, kind=kin).ap(),
        "MASKS": nc.dram_tensor("MASKS", [128, 32 * 512], BF16, kind=kin).ap(),
        "AUN": nc.dram_tensor("AUN", [D, TT], BF16, kind=kout).ap(),
        "DEN": nc.dram_tensor("DEN", [1, H * TT], F32, kind=kout).ap(),
    }


def phase_A(nc, seq, st_, G):
    xq, xf, wqkv = G["XQ"], G["XF"], G["WQKV"]
    bqkv, masks = G["BQKV"], G["MASKS"]
    aun, den = G["AUN"], G["DEN"]
    if True:
        sbuf = lambda n, sh, dt=F32: st_.enter_context(nc.sbuf_tensor("A" + n, sh, dt))
        psum = lambda n, sh, dt=F32: st_.enter_context(nc.psum_tensor("A" + n, sh, dt))

        wq_bf = sbuf("wq_bf", [128, 8 * 3 * D], BF16)
        wstg = [sbuf(f"wstg{i}", [128, 3 * D], F32) for i in range(2)]
        bias_row = sbuf("bias_row", [1, 3 * D], BF16)
        ones_row = sbuf("ones_row", [1, 512], BF16)
        ident = sbuf("ident", [128, 128], BF16)
        mask_sb = sbuf("mask_sb", [128, 32 * 512], BF16)

        xstg = [sbuf(f"xstg{i}", [128, D], F32) for i in range(3)]
        lnq_tm = [sbuf(f"lnq_tm{i}", [128, D], BF16) for i in range(2)]
        lnf_tm = [sbuf(f"lnf_tm{i}", [128, 2 * D], BF16) for i in range(2)]
        lnq_fm = [sbuf(f"lnq_fm{i}", [128, 8 * 128], BF16) for i in range(2)]
        lnf_fm = [sbuf(f"lnf_fm{i}", [128, 8 * 256], BF16) for i in range(2)]

        q_st = sbuf("q_st", [64, TT], BF16)
        k_st = sbuf("k_st", [64, 2 * TT], BF16)
        v_st = sbuf("v_st", [64, 2 * TT], BF16)
        v_pm = sbuf("v_pm", [128, 32 * 65], BF16)
        ebuf = [sbuf(f"ebuf{i}", [128, 512], BF16) for i in range(4)]
        atmp = [sbuf(f"atmp{i}", [64, 512], BF16) for i in range(2)]
        dtmp = [sbuf(f"dtmp{i}", [1, 512], F32) for i in range(2)]

        ps_s = [psum(f"ps_s{i}", [128, 512], F32) for i in range(2)]
        ps_a = [psum(f"ps_a{i}", [128, 512], F32) for i in range(2)]
        ps_x = [psum(f"ps_x{i}", [128, 256], F32) for i in range(2)]
        ps_t = [psum(f"ps_t{i}", [128, 128], BF16) for i in range(2)]

        ld_masks = seq.op("sync", lambda e: e.dma_start(out=mask_sb[:], in_=masks), dma=True, private=True)
        mk_ones = seq.op("vector", lambda e: e.memset(ones_row[:], 1.0))
        id_done = mk_identity(seq, ident)
        mk_vones = seq.op("vector", lambda e: e.memset(v_pm[:], 1.0))

        wcast = []
        for fc in range(8):
            ld = seq.op("sync", lambda e, fc=fc: e.dma_start(
                out=wstg[fc % 2][:], in_=wqkv[128 * fc:128 * (fc + 1), :]),
                dma=True, deps=[wcast[fc - 2] if fc >= 2 else None], private=True)
            cst = seq.op("vector", lambda e, fc=fc: e.tensor_copy(
                out=wq_bf[:, 3 * D * fc: 3 * D * (fc + 1)], in_=wstg[fc % 2][:]),
                deps=[ld])
            wcast.append(cst)
        ld_biasf = seq.op("gpsimd", lambda e: e.dma_start(
            out=wstg[0][0:1, :], in_=bqkv.rearrange("(a b) -> a b", a=1)),
            dma=True, deps=[wcast[6]], private=True)
        cast_bias = seq.op("vector", lambda e: e.tensor_copy(
            out=bias_row[:], in_=wstg[0][0:1, :]), deps=[ld_biasf])

        lnh = LNHelper(seq, sbuf, "lnA", D)

        prev_attn_last = None
        stg_free = {}
        tm_readers = [None, None]
        fm_readers = [None, None]
        last_aevac = [None, None]
        last_sread = [None, None]
        last_xevac = [None, None]
        last_tevac = [None, None]
        last_pv = {}
        eidx = 0
        store_ops = []
        atmp_busy = [[], []]

        for h in range(H):
            w = h % 2
            applies = []
            tiles = [(0, xq, h, lnq_tm[w][:], D),
                     (1, xf, 2 * h, lnf_tm[w][:, 0:D], D),
                     (2, xf, 2 * h + 1, lnf_tm[w][:, D:2 * D], D)]
            for (k_, src, tidx, out_ap, wd) in tiles:
                s = (3 * h + k_) % 3
                ld = seq.op("sync", lambda e, s=s, src=src, t=tidx: e.dma_start(
                    out=xstg[s][:], in_=src[128 * t:128 * (t + 1), :]), dma=True,
                    deps=[stg_free.get(s)], skey=f"xstg{s}")
                ap = lnh.ln(xstg[s][:], out_ap, [ld, tm_readers[w]])
                stg_free[s] = ap
                applies.append(ap)

            fm_ev = []
            tlast = None
            for ch in range(8):
                b_ = ch % 2
                tq = seq.op("tensor", lambda e, ch=ch, w=w: e.matmul(
                    ps_t[ch % 2][:], lnq_tm[w][:, 128 * ch:128 * (ch + 1)], ident[:],
                    is_transpose=True, start=True, stop=True),
                    deps=[applies[0], id_done, last_tevac[b_], fm_readers[w]])
                ev = seq.op("vector", lambda e, ch=ch, w=w: e.tensor_copy(
                    out=lnq_fm[w][:, 128 * ch:128 * (ch + 1)], in_=ps_t[ch % 2][:]),
                    deps=[tq])
                last_tevac[b_] = ev
                fm_ev.append(ev)
                tlast = tq
            for ch in range(8):
                for half in range(2):
                    b_ = half
                    tf = seq.op("tensor", lambda e, ch=ch, half=half, w=w: e.matmul(
                        ps_t[half][:],
                        lnf_tm[w][:, D * half + 128 * ch: D * half + 128 * (ch + 1)],
                        ident[:], is_transpose=True, start=True, stop=True),
                        deps=[applies[1 + half], id_done, last_tevac[b_], fm_readers[w]])
                    ev = seq.op("vector", lambda e, ch=ch, half=half, w=w: e.tensor_copy(
                        out=lnf_fm[w][:, 256 * ch + 128 * half: 256 * ch + 128 * (half + 1)],
                        in_=ps_t[half][:]), deps=[tf])
                    last_tevac[b_] = ev
                    fm_ev.append(ev)
                    tlast = tf
            tm_readers[w] = tlast

            wdeps = [wcast[-1], cast_bias, mk_ones] + fm_ev
            head_guard = [prev_attn_last] if prev_attn_last else []

            def xproj(kind, jp, pb, deps):
                cbase = kind * D + 128 * jp
                nmov = 128 if kind == 0 else 256
                src = lnq_fm[w] if kind == 0 else lnf_fm[w]
                def emit(e, cbase=cbase, nmov=nmov, src=src, pb=pb):
                    last = None
                    for fc in range(8):
                        last = e.matmul(
                            ps_x[pb][:, 0:nmov],
                            wq_bf[:, 3 * D * fc + cbase: 3 * D * fc + cbase + 128],
                            src[:, nmov * fc: nmov * (fc + 1)],
                            start=(fc == 0), stop=False)
                    last = e.matmul(ps_x[pb][:, 0:nmov],
                                    bias_row[0:1, cbase:cbase + 128],
                                    ones_row[0:1, 0:nmov], start=False, stop=True)
                    return last
                return seq.op("tensor", emit, deps=deps)

            q_ev = k_ev = v_ev = None
            v_evs = []
            for jp in range(8):
                pb = jp % 2
                mm = xproj(0, jp, pb, wdeps + [last_xevac[pb]])
                q_ev = seq.op("scalar", lambda e, jp=jp, pb=pb: (
                    e.copy(out=q_st[:, 256 * jp: 256 * jp + 128], in_=ps_x[pb][0:64, 0:128]),
                    e.copy(out=q_st[:, 256 * jp + 128: 256 * jp + 256], in_=ps_x[pb][64:128, 0:128]),
                )[-1], deps=[mm] + head_guard)
                last_xevac[pb] = q_ev
            for jp in range(8):
                pb = jp % 2
                mm = xproj(1, jp, pb, wdeps + [last_xevac[pb]])
                k_ev = seq.op("scalar", lambda e, jp=jp, pb=pb: (
                    e.copy(out=k_st[:, 512 * jp: 512 * jp + 256], in_=ps_x[pb][0:64, 0:256]),
                    e.copy(out=k_st[:, 512 * jp + 256: 512 * jp + 512], in_=ps_x[pb][64:128, 0:256]),
                )[-1], deps=[mm] + head_guard)
                last_xevac[pb] = k_ev
            for jp in range(8):
                pb = jp % 2
                mm = xproj(2, jp, pb, wdeps + [last_xevac[pb]])
                v_ev = seq.op("scalar", lambda e, jp=jp, pb=pb: (
                    e.copy(out=v_st[:, 512 * jp: 512 * jp + 256], in_=ps_x[pb][0:64, 0:256]),
                    e.copy(out=v_st[:, 512 * jp + 256: 512 * jp + 512], in_=ps_x[pb][64:128, 0:256]),
                )[-1], deps=[mm] + head_guard)
                last_xevac[pb] = v_ev
                v_evs.append(v_ev)

            vpm_ready = []
            for j in range(16):
                for th in range(2):
                    kt = 2 * j + th
                    b_ = kt % 2
                    tv = seq.op("tensor", lambda e, j=j, th=th, b_=b_: e.matmul(
                        ps_t[b_][:, 0:64],
                        v_st[:, 256 * j + 128 * th: 256 * j + 128 * (th + 1)],
                        ident[0:64, 0:64], is_transpose=True, start=True, stop=True),
                        deps=[v_evs[j // 2], id_done, last_tevac[b_]])
                    ev = seq.op("vector", lambda e, kt=kt, b_=b_: e.tensor_copy(
                        out=v_pm[:, 65 * kt: 65 * kt + 64], in_=ps_t[b_][:, 0:64]),
                        deps=[tv, mk_vones] + head_guard)
                    last_tevac[b_] = ev
                    vpm_ready.append(ev)

            for qc in range(4):
                ab = qc % 2
                kts = [(jk, th) for th in range(2) if 128 * th <= 64 * qc + 62
                       for jk in range(16)]
                first = True
                pv = None
                for ktidx, (jk, th) in enumerate(kts):
                    is_last = ktidx == len(kts) - 1
                    sb_ = eidx % 2
                    eb = eidx % 4
                    masked = not (th == 0 and qc >= 2)
                    mmS = seq.op("tensor", lambda e, jk=jk, th=th, qc=qc, sb_=sb_: e.matmul(
                        ps_s[sb_][:],
                        k_st[:, 256 * jk + 128 * th: 256 * jk + 128 * (th + 1)],
                        bass.AP(q_st, 32 * qc, [[TT, 64], [1, 32], [128, 16]]),
                        start=True, stop=True),
                        deps=[k_ev, q_ev, last_sread[sb_]])
                    ex = seq.op("scalar", lambda e, sb_=sb_, eb=eb: e.activation(
                        out=ebuf[eb][:], in_=ps_s[sb_][:], func=AF.Exp, scale=0.125),
                        deps=[mmS, last_pv.get(eb)])
                    last_sread[sb_] = ex
                    edone = ex
                    if masked:
                        mi = 2 * jk + (qc - 2 * th)
                        edone = seq.op("vector", lambda e, eb=eb, mi=mi: e.tensor_mul(
                            ebuf[eb][:], ebuf[eb][:], mask_sb[:, 512 * mi: 512 * (mi + 1)]),
                            deps=[ex, ld_masks])
                    kt = 2 * jk + th
                    pv = seq.op("tensor", lambda e, kt=kt, eb=eb, ab=ab, first=first, last=is_last: e.matmul(
                        ps_a[ab][0:65, :], v_pm[:, 65 * kt: 65 * (kt + 1)],
                        ebuf[eb][:], start=first, stop=last),
                        deps=[edone, vpm_ready[kt], last_aevac[ab]])
                    last_pv[eb] = pv
                    first = False
                    eidx += 1
                prev_attn_last = pv
                tb_ = (h * 4 + qc) % 2
                ev_a = seq.op("scalar", lambda e, ab=ab, tb_=tb_: (
                    e.copy(out=atmp[tb_][:], in_=ps_a[ab][0:64, :]),
                    e.copy(out=dtmp[tb_][:], in_=ps_a[ab][64:65, :]),
                )[-1], deps=[pv] + atmp_busy[tb_])
                last_aevac[ab] = ev_a
                st_a = seq.op("scalar", lambda e, tb_=tb_, h=h, qc=qc: e.dma_start(
                    out=aun[64 * h:64 * (h + 1), 512 * qc: 512 * (qc + 1)],
                    in_=atmp[tb_][:]), dma=True, deps=[ev_a], skey=f"aunst{tb_}")
                st_dn = seq.op("gpsimd", lambda e, tb_=tb_, h=h, qc=qc: e.dma_start(
                    out=den[0:1, TT * h + 512 * qc: TT * h + 512 * (qc + 1)],
                    in_=dtmp[tb_][:]), dma=True, deps=[ev_a], skey=f"denst{tb_}")
                atmp_busy[tb_] = [st_a, st_dn]
                store_ops.append(st_a)
                store_ops.append(st_dn)
        seq.op("sync", lambda e: e.nop(), deps=store_ops)


def build_A():
    nc = bass.Bass("TRN2", target_bir_lowering=False)
    G = dram_A(nc)
    with ExitStack() as st_:
        seq = Seq(nc, st_)
        block = st_.enter_context(nc.Block())
        phase_A(nc, seq, st_, G)
        seq.emit(block)
    return nc




def build_B():
    pass


def dram_B(nc, kout="ExternalOutput", r2_ap=None, aun_ap=None, den_ap=None):
    kin = "ExternalInput"
    return {
        "XMY": nc.dram_tensor("XMY", [TT, D], F32, kind=kin).ap(),
        "AUN": aun_ap if aun_ap is not None else
               nc.dram_tensor("AUN", [D, TT], BF16, kind=kin).ap(),
        "DEN": den_ap if den_ap is not None else
               nc.dram_tensor("DEN", [1, H * TT], F32, kind=kin).ap(),
        "WPROJ": nc.dram_tensor("WPROJ", [D, D], F32, kind=kin).ap(),
        "BPROJ": nc.dram_tensor("BPROJ", [D], F32, kind=kin).ap(),
        "W1": nc.dram_tensor("W1", [4, D, 4 * D], F32, kind=kin).ap(),
        "B1": nc.dram_tensor("B1", [4, 4 * D], F32, kind=kin).ap(),
        "W2": nc.dram_tensor("W2", [4, 4 * D, D], F32, kind=kin).ap(),
        "B2": nc.dram_tensor("B2", [4, D], F32, kind=kin).ap(),
        "WU": nc.dram_tensor("WU", [D, 1], F32, kind=kin).ap(),
        "BU": nc.dram_tensor("BU", [1], F32, kind=kin).ap(),
        "R2": r2_ap if r2_ap is not None else
              nc.dram_tensor("R2", [TT, D], F32, kind=kout).ap(),
    }


def phase_B(nc, seq, st_, G):
    xmy, aun, den = G["XMY"], G["AUN"], G["DEN"]
    wproj, bproj = G["WPROJ"], G["BPROJ"]
    w1, b1, w2, b2 = G["W1"], G["B1"], G["W2"], G["B2"]
    wu, bu, r2 = G["WU"], G["BU"], G["R2"]
    drec = nc.dram_tensor("DREC", [H, TT], BF16).ap()
    wbounce = nc.dram_tensor("WBOUNCE", [4, TT], BF16).ap()
    xspill = nc.dram_tensor("XSPILL", [TT, D], F32).ap()

    if True:
        sbuf = lambda n, sh, dt=F32: st_.enter_context(nc.sbuf_tensor("B" + n, sh, dt))
        psum = lambda n, sh, dt=F32: st_.enter_context(nc.psum_tensor("B" + n, sh, dt))

        X = sbuf("X", [128, 16 * D], F32)
        he = X.bitcast(BF16)
        fmA = sbuf("fmA", [128, 8 * TT], BF16)
        zfm = sbuf("zfm", [128, 8 * TT], BF16)
        y_acc = zfm
        w2stg_f = zfm[:, 12288:16384].bitcast(F32)
        wpj = sbuf("wpj", [128, 8 * D], BF16)
        w2bf = wpj
        wstgB = sbuf("wstgB", [128, 2 * D], F32)
        w1bf = [sbuf(f"w1bf{i}", [128, D], BF16) for i in range(2)]
        ztile = [sbuf(f"ztile{i}", [128, D], BF16) for i in range(2)]
        lnterm = sbuf("lnterm", [128, D], F32)
        xres = [sbuf(f"xres{i}", [128, D], F32) for i in range(2)]
        bc = [sbuf("bc0", [128, TT], BF16)] * 2
        webc = [sbuf("webc0", [128, D], BF16)] * 2
        rows = sbuf("rows", [1, 3 * TT], F32)
        browB = sbuf("browB", [1, D], BF16)
        b1row = zfm[0:1, 8192:12288]
        b2row = bc[0][0:1, 0:D]
        wurow = sbuf("wurow", [128, 8], BF16)
        wu8 = sbuf("wu8", [8, 128], F32)
        wu8b = sbuf("wu8b", [8, 128], BF16)
        busb = sbuf("busb", [1, 1], F32)
        ones_row = sbuf("ones_rowB", [1, 512], BF16)
        identB = sbuf("identB", [128, 128], BF16)

        ps_mm = [psum(f"ps_mm{i}", [128, 512], F32) for i in range(2)]
        ps_t = [psum(f"ps_tB{i}", [128, 128], BF16) for i in range(2)]
        ps_u = psum("ps_u", [1, 512], F32)

        id_done = mk_identity(seq, identB)
        mk_ones = seq.op("vector", lambda e: e.memset(ones_row[:], 1.0))
        lnh = LNHelper(seq, sbuf, "lnB", D, sq=lnterm)

        ld_x = seq.op("sync", lambda e: e.dma_start(
            out=X[:], in_=xmy.rearrange("(n p) c -> p n c", p=128)),
            dma=True, private=True)
        ld_y = seq.op("sync", lambda e: e.dma_start(
            out=fmA[:], in_=aun.rearrange("(cc p) t -> p cc t", p=128)),
            dma=True, private=True)
        ld_bp = seq.op("gpsimd", lambda e: e.dma_start(
            out=rows[0:1, 0:D], in_=bproj.rearrange("(a b) -> a b", a=1)),
            dma=True, private=True)
        c_bp = seq.op("vector", lambda e: e.tensor_copy(
            out=browB[0:1, 0:D], in_=rows[0:1, 0:D]), deps=[ld_bp])
        c_b1 = c_bp
        c_b2 = c_bp
        ld_wuA = seq.op("gpsimd", lambda e: e.dma_start(
            out=wu8[:], in_=wu.rearrange("(c p) o -> c (p o)", p=128)),
            dma=True, private=True)
        c_wuA = seq.op("vector", lambda e: e.tensor_copy(
            out=wu8b[:], in_=wu8[:]), deps=[ld_wuA])
        t_wu = seq.op("tensor", lambda e: e.matmul(
            ps_t[0][:, 0:8], wu8b[:], identB[0:8, 0:8],
            is_transpose=True, start=True, stop=True), deps=[c_wuA, id_done])
        ld_wu2 = seq.op("vector", lambda e: e.tensor_copy(
            out=wurow[:], in_=ps_t[0][:, 0:8]), deps=[t_wu])
        ld_bu = seq.op("gpsimd", lambda e: e.dma_start(
            out=busb[:], in_=bu.rearrange("(a b) -> a b", a=1)), dma=True, private=True)

        wp_cast = []
        for ck in range(8):
            sl = ck % 2
            ld = seq.op("sync" if sl == 0 else "gpsimd",
                lambda e, ck=ck, sl=sl: e.dma_start(
                out=wstgB[:, D * sl: D * (sl + 1)],
                in_=wproj[128 * ck:128 * (ck + 1), :]), dma=True,
                deps=[wp_cast[ck - 2] if ck >= 2 else None, c_b1, c_b2, c_bp],
                skey=f"wstgB{sl}")
            cst = seq.op("vector", lambda e, ck=ck, sl=sl: e.tensor_copy(
                out=wpj[:, D * ck: D * (ck + 1)], in_=wstgB[:, D * sl: D * (sl + 1)]),
                deps=[ld])
            wp_cast.append(cst)

        ymul = []
        dload_prev = {}
        for cc in range(8):
            sl = cc % 2
            ldd = seq.op("gpsimd", lambda e, cc=cc: e.dma_start(
                out=rows[0:1, 0:2 * TT],
                in_=den[0:1, 2 * TT * cc: 2 * TT * (cc + 1)]), dma=True,
                deps=[dload_prev.get(0), c_bp], skey="denld")
            rec = seq.op("vector", lambda e: e.reciprocal(
                out=rows[0:1, 0:2 * TT],
                in_=rows[0:1, 0:2 * TT]), deps=[ldd])
            recb = seq.op("vector", lambda e: e.tensor_copy(
                out=rows[0:1, 2 * TT:3 * TT].bitcast(BF16)[0:1, 0:2 * TT],
                in_=rows[0:1, 0:2 * TT]), deps=[rec])
            dload_prev[0] = recb
            std = seq.op("gpsimd", lambda e, cc=cc: e.dma_start(
                out=bass.AP(drec.tensor, 2 * cc * TT, [[2 * TT, 1], [1, 2 * TT]]),
                in_=rows[0:1, 2 * TT:3 * TT].bitcast(BF16)[0:1, 0:2 * TT]),
                dma=True, deps=[recb, dload_prev.get("std")], skey="drecst")
            dload_prev["std"] = std
            ldbc = seq.op("gpsimd", lambda e, cc=cc, sl=sl: e.dma_start(
                out=bc[sl][:],
                in_=bass.AP(drec.tensor, 2 * cc * TT, [[TT, 2], [0, 64], [1, TT]])),
                dma=True, deps=[std, ymul[cc - 1] if cc >= 1 else None],
                skey=f"bcld{sl}")
            ym = seq.op("vector", lambda e, cc=cc, sl=sl: e.tensor_mul(
                fmA[:, TT * cc: TT * (cc + 1)], fmA[:, TT * cc: TT * (cc + 1)],
                bc[sl][:]), deps=[ldbc, ld_y])
            ymul.append(ym)

        z_ev = {}
        last_z_mm = None
        for ec in range(8):
            for n4 in range(4):
                pb = (ec * 4 + n4) % 2
                def zmm(e, ec=ec, n4=n4, pb=pb):
                    last = None
                    for ck in range(8):
                        last = e.matmul(
                            ps_mm[pb][:], wpj[:, D * ck + 128 * ec: D * ck + 128 * (ec + 1)],
                            fmA[:, TT * ck + 512 * n4: TT * ck + 512 * (n4 + 1)],
                            start=(ck == 0), stop=False)
                    return e.matmul(ps_mm[pb][:], browB[0:1, 128 * ec:128 * (ec + 1)],
                                    ones_row[:], start=False, stop=True)
                mm = seq.op("tensor", zmm,
                            deps=[wp_cast[-1], mk_ones] + ymul + [z_ev.get((ec, n4, "war"))]
                            + [z_ev.get(("pb", pb))])
                ev = seq.op("scalar", lambda e, ec=ec, n4=n4, pb=pb: e.copy(
                    out=zfm[:, TT * ec + 512 * n4: TT * ec + 512 * (n4 + 1)],
                    in_=ps_mm[pb][:]), deps=[mm])
                z_ev[(ec, n4)] = ev
                z_ev[("pb", pb)] = ev
                last_z_mm = mm

        lnf_ev = {}
        prev_addx = None
        last_tev = [None, None]
        fm_war = [ymul[-1], ymul[-1]]
        for tt in range(16):
            zsl = tt % 2
            tevs = []
            for ec in range(8):
                b_ = ec % 2
                tr = seq.op("tensor", lambda e, ec=ec, tt=tt, b_=b_: e.matmul(
                    ps_t[b_][:], zfm[:, TT * ec + 128 * tt: TT * ec + 128 * (tt + 1)],
                    identB[:], is_transpose=True, start=True, stop=True),
                    deps=[z_ev[(ec, tt // 4)], id_done, last_tev[b_], last_z_mm])
                ev = seq.op("vector", lambda e, ec=ec, tt=tt, zsl=zsl, b_=b_: e.tensor_copy(
                    out=ztile[zsl][:, 128 * ec:128 * (ec + 1)], in_=ps_t[b_][:]),
                    deps=[tr, lnf_ev.get(("zt", zsl))])
                last_tev[b_] = ev
                tevs.append(ev)
            ap1 = lnh.ln(ztile[zsl][:], lnterm[:], tevs + [prev_addx])
            addx = seq.op("vector", lambda e, tt=tt: e.tensor_add(
                X[:, D * tt: D * (tt + 1)], X[:, D * tt: D * (tt + 1)], lnterm[:]),
                deps=[ap1, ld_x], small=True)
            prev_addx = addx
            ap2 = lnh.ln(X[:, D * tt: D * (tt + 1)], ztile[zsl][:], [addx])
            lnf_ev[("zt", zsl)] = ap2
            evs2 = []
            for ch in range(8):
                b_ = ch % 2
                tr = seq.op("tensor", lambda e, ch=ch, tt=tt, zsl=zsl, b_=b_: e.matmul(
                    ps_t[b_][:], ztile[zsl][:, 128 * ch:128 * (ch + 1)], identB[:],
                    is_transpose=True, start=True, stop=True),
                    deps=[ap2, last_tev[b_], fm_war[0] if tt == 0 else None])
                ev = seq.op("vector", lambda e, ch=ch, tt=tt, b_=b_: e.tensor_copy(
                    out=fmA[:, TT * ch + 128 * tt: TT * ch + 128 * (tt + 1)],
                    in_=ps_t[b_][:]), deps=[tr])
                last_tev[b_] = ev
                evs2.append(ev)
            lnf_ev[tt] = evs2
            lnf_ev[("zt", zsl)] = evs2[-1]
        ln1_all = [op for tt in range(16) for op in lnf_ev[tt]]

        sp_x = seq.op("sync", lambda e: e.dma_start(
            out=xspill.rearrange("(n p) c -> p n c", p=128), in_=X[:]),
            dma=True, private=True,
            deps=[lnf_ev[15][-1]] + [lnf_ev[tt][-1] for tt in range(16)])

        R = lambda k: rows[0:1, k * TT:(k + 1) * TT]
        u_mm = []
        for n4 in range(4):
            def umm(e, n4=n4):
                last = None
                for dck in range(8):
                    last = e.matmul(
                        ps_u[:], wurow[:, dck:dck + 1],
                        fmA[:, TT * dck + 512 * n4: TT * dck + 512 * (n4 + 1)],
                        start=(dck == 0), stop=(dck == 7))
                return last
            mm = seq.op("tensor", umm, deps=ln1_all + [ld_wu2, u_mm[-1][1] if u_mm else None])
            sg = seq.op("scalar", lambda e, n4=n4: e.activation(
                out=rows[0:1, 512 * n4: 512 * (n4 + 1)], in_=ps_u[:],
                func=AF.Sigmoid, bias=busb[:]), deps=[mm, ld_bu, dload_prev.get(0)],
                small=True)
            u_mm.append((mm, sg))
        u_done = u_mm[-1][1]
        we_sts = []
        for ex in range(4):
            i1 = seq.op("gpsimd", lambda e, ex=ex: e.iota(
                R(1).bitcast(I32), [[0, 512], [-1, 4]], base=ex + 4,
                channel_multiplier=0,
                allow_small_or_imprecise_dtypes=True),
                deps=[we_sts[-1] if we_sts else None, dload_prev.get(0), u_done],
                small=True)
            i2 = seq.op("scalar", lambda e: e.copy(
                out=R(1), in_=R(1).bitcast(I32)), deps=[i1], small=True)
            t2 = seq.op("vector", lambda e: e.tensor_single_scalar(
                out=R(2), in_=R(1), scalar=3.5, op=ALU.is_gt), deps=[i2], small=True)
            t3 = seq.op("vector", lambda e: e.scalar_tensor_tensor(
                out=R(2), in0=R(2), scalar=-4.0, in1=R(1),
                op0=ALU.mult, op1=ALU.add), deps=[t2], small=True)
            t4 = seq.op("vector", lambda e: e.tensor_scalar_add(
                R(2), R(2), 1.0), deps=[t3], small=True)
            t5 = seq.op("vector", lambda e: e.scalar_tensor_tensor(
                out=R(1), in0=R(0), scalar=4.0, in1=R(2),
                op0=ALU.mult, op1=ALU.subtract), deps=[t4, u_done], small=True)
            t6 = seq.op("vector", lambda e: e.tensor_scalar_add(
                R(1), R(1), 1.0), deps=[t5], small=True)
            s1 = seq.op("scalar", lambda e: e.activation(
                out=R(1), in_=R(1), func=AF.Sign), deps=[t6], small=True)
            s2 = seq.op("scalar", lambda e: e.activation(
                out=R(1), in_=R(1), func=AF.Relu), deps=[s1], small=True)
            r1_ = seq.op("vector", lambda e: e.reciprocal(
                out=R(2), in_=R(2)), deps=[t5], small=True)
            m1 = seq.op("vector", lambda e: e.tensor_mul(
                R(2), R(2), R(0)), deps=[r1_, u_done], small=True)
            m2 = seq.op("vector", lambda e: e.tensor_mul(
                R(2), R(2), R(1)), deps=[m1, s2], small=True)
            cb_ = seq.op("vector", lambda e: e.tensor_copy(
                out=R(1).bitcast(BF16)[0:1, 0:TT], in_=R(2)), deps=[m2], small=True)
            stw = seq.op("gpsimd", lambda e, ex=ex: e.dma_start(
                out=wbounce[ex:ex + 1, :], in_=R(1).bitcast(BF16)[0:1, 0:TT]),
                dma=True, deps=[cb_, we_sts[-1] if we_sts else None], skey="wbst")
            we_sts.append(stw)

        he_ev = {}
        yacc_done = {}
        last_w2 = {}
        r2_sts = []
        wsl_guard = {}
        all_ln_done = lnf_ev[15][-1]
        he_war = [sp_x]
        w2bf_war = [all_ln_done]
        yacc_war = [all_ln_done]
        last_w1 = {0: wp_cast[6], 1: wp_cast[7]}
        prev_ymm_last = None
        fin_guard = None
        bias_guard = None
        for tc in range(2):
            for ex in range(4):
                ldb1 = seq.op("gpsimd", lambda e, ex=ex: e.dma_start(
                    out=rows[0:1, 0:4 * D], in_=b1[ex:ex + 1, :]), dma=True,
                    deps=[bias_guard, we_sts[-1]], skey="bias_ld")
                ldb2 = seq.op("gpsimd", lambda e, ex=ex: e.dma_start(
                    out=rows[0:1, 4 * D: 5 * D], in_=b2[ex:ex + 1, :]), dma=True,
                    deps=[ldb1], skey="bias_ld")
                cb1 = seq.op("vector", lambda e: e.tensor_copy(
                    out=b1row, in_=rows[0:1, 0:4 * D]),
                    deps=[ldb1, prev_ymm_last, all_ln_done, ymul[-1]])
                cb2 = seq.op("vector", lambda e: e.tensor_copy(
                    out=b2row, in_=rows[0:1, 4 * D: 5 * D]),
                    deps=[ldb2, prev_ymm_last, all_ln_done, ymul[-1]])
                bias_guard = cb2
                for fb in range(32):
                    sl = fb % 2
                    ldw1 = seq.op("sync" if sl == 0 else "gpsimd",
                        lambda e, ex=ex, fb=fb, sl=sl: e.dma_start(
                        out=wstgB[:, D * sl: D * (sl + 1)],
                        in_=bass.AP(w1.tensor, ex * D * 4 * D + 128 * fb,
                                    [[4 * D, 128], [4 * D * 128, 8], [1, 128]])),
                        dma=True, deps=[last_w1.get(sl)], skey=f"wstgB{sl}")
                    cw1 = seq.op("vector", lambda e, sl=sl: e.tensor_copy(
                        out=w1bf[sl][:], in_=wstgB[:, D * sl: D * (sl + 1)]),
                        deps=[ldw1, wsl_guard.get(("w1", sl))])
                    last_w1[sl] = cw1
                    for n2 in range(2):
                        pb = (fb * 2 + n2) % 2
                        def hmm(e, ex=ex, fb=fb, n2=n2, sl=sl, tc=tc, pb=pb):
                            last = None
                            for dck in range(8):
                                last = e.matmul(
                                    ps_mm[pb][:],
                                    w1bf[sl][:, 128 * dck:128 * (dck + 1)],
                                    fmA[:, TT * dck + 1024 * tc + 512 * n2:
                                         TT * dck + 1024 * tc + 512 * (n2 + 1)],
                                    start=(dck == 0), stop=False)
                            return e.matmul(
                                ps_mm[pb][:],
                                b1row[0:1, 128 * fb: 128 * (fb + 1)],
                                ones_row[:], start=False, stop=True)
                        mm = seq.op("tensor", hmm,
                                    deps=[cw1, cb1, he_ev.get(("pb", pb)),
                                          yacc_done.get(("pb", pb))] + ln1_all)
                        ev = seq.op("scalar", lambda e, fb=fb, n2=n2, pb=pb: e.activation(
                            out=he[:, 1024 * fb + 512 * n2: 1024 * fb + 512 * (n2 + 1)],
                            in_=ps_mm[pb][:], func=AF.Relu),
                            deps=[mm] + (he_war + [prev_ymm_last]
                                         if fb == 0 and n2 == 0 else []))
                        he_ev[(fb, n2)] = ev
                        he_ev[("pb", pb)] = ev
                    wsl_guard[("w1", sl)] = he_ev[(fb, 1)]
                he_all_key = [he_ev[(fb, n2)] for fb in range(32) for n2 in range(2)]
                ldbc_e = seq.op("gpsimd", lambda e, ex=ex, tc=tc: e.dma_start(
                    out=webc[tc % 2][:],
                    in_=bass.AP(wbounce.tensor, ex * TT + 1024 * tc,
                                [[0, 128], [1, 1024]])),
                    dma=True, deps=[we_sts[ex], yacc_done.get(("webc", 0))],
                    skey=f"webc{tc % 2}")
                for dc in range(8):
                    sl2 = dc % 2
                    lw2a = seq.op("sync", lambda e, ex=ex, dc=dc: e.dma_start(
                        out=w2stg_f[:], in_=bass.AP(
                            w2.tensor, ex * 4 * D * D + 128 * dc,
                            [[D, 128], [D * 128, 16], [1, 128]])),
                        dma=True, deps=[last_w2.get("f", all_ln_done)], skey="w2stg")
                    cw2a = seq.op("vector", lambda e, sl2=sl2: e.tensor_copy(
                        out=w2bf[:, 4096 * sl2: 4096 * sl2 + 2048], in_=w2stg_f[:]),
                        deps=[lw2a, wsl_guard.get(("w2", sl2), w2bf_war[0])])
                    lw2b = seq.op("sync", lambda e, ex=ex, dc=dc: e.dma_start(
                        out=w2stg_f[:], in_=bass.AP(
                            w2.tensor, ex * 4 * D * D + 128 * dc + D * 2048,
                            [[D, 128], [D * 128, 16], [1, 128]])),
                        dma=True, deps=[cw2a], skey="w2stg")
                    cw2b = seq.op("vector", lambda e, sl2=sl2: e.tensor_copy(
                        out=w2bf[:, 4096 * sl2 + 2048: 4096 * sl2 + 4096], in_=w2stg_f[:]),
                        deps=[lw2b, wsl_guard.get(("w2", sl2))])
                    last_w2["f"] = cw2b
                    for n2 in range(2):
                        pb = (dc * 2 + n2) % 2
                        def ymm(e, ex=ex, dc=dc, n2=n2, sl2=sl2, tc=tc, pb=pb):
                            last = None
                            for fck in range(32):
                                last = e.matmul(
                                    ps_mm[pb][:],
                                    w2bf[:, 4096 * sl2 + 128 * fck: 4096 * sl2 + 128 * (fck + 1)],
                                    he[:, 1024 * fck + 512 * n2: 1024 * fck + 512 * (n2 + 1)],
                                    start=(fck == 0), stop=False)
                            return e.matmul(
                                ps_mm[pb][:],
                                b2row[0:1, 128 * dc: 128 * (dc + 1)],
                                ones_row[:], start=False, stop=True)
                        mm = seq.op("tensor", ymm, deps=[cw2a, cw2b, cb2] + he_all_key
                                    + [yacc_done.get(("pb", pb))])
                        prev_ymm_last = mm
                        if ex == 0:
                            ev = seq.op("vector", lambda e, dc=dc, n2=n2, pb=pb, tc=tc: e.tensor_mul(
                                y_acc[:, 1024 * dc + 512 * n2: 1024 * dc + 512 * (n2 + 1)],
                                ps_mm[pb][:], webc[tc % 2][:, 512 * n2: 512 * (n2 + 1)]),
                                deps=[mm, ldbc_e, fin_guard]
                                + (yacc_war if tc == 0 and dc == 0 and n2 == 0 else []))
                        else:
                            ev0 = seq.op("vector", lambda e, pb=pb, n2=n2, tc=tc: e.tensor_mul(
                                lnterm[:, 0:512].bitcast(BF16)[:, 0:512],
                                ps_mm[pb][:], webc[tc % 2][:, 512 * n2: 512 * (n2 + 1)]),
                                deps=[mm, ldbc_e, fin_guard, prev_addx,
                                      yacc_done.get("lnterm")])
                            ev = seq.op("vector", lambda e, dc=dc, n2=n2: e.tensor_add(
                                y_acc[:, 1024 * dc + 512 * n2: 1024 * dc + 512 * (n2 + 1)],
                                y_acc[:, 1024 * dc + 512 * n2: 1024 * dc + 512 * (n2 + 1)],
                                lnterm[:, 0:512].bitcast(BF16)[:, 0:512]),
                                deps=[ev0])
                        yacc_done[(dc, n2)] = ev
                        yacc_done[("pb", pb)] = ev
                        yacc_done["lnterm"] = ev
                    wsl_guard[("w2", sl2)] = yacc_done[(dc, 1)]
                yacc_done[("webc", 0)] = yacc_done[(7, 1)]
            yall = [yacc_done[(dc, n2)] for dc in range(8) for n2 in range(2)]
            for t8 in range(8):
                tt = 8 * tc + t8
                zsl = t8 % 2
                ldr = seq.op("sync", lambda e, tt=tt, zsl=zsl: e.dma_start(
                    out=xres[zsl][:], in_=xspill[128 * tt:128 * (tt + 1), :]),
                    dma=True, deps=[sp_x, r2_sts[-2] if len(r2_sts) >= 2 else None],
                    skey=f"xres{zsl}")
                tevs = []
                for ch in range(8):
                    b_ = ch % 2
                    tr = seq.op("tensor", lambda e, ch=ch, t8=t8, b_=b_: e.matmul(
                        ps_t[b_][:], y_acc[:, 1024 * ch + 128 * t8: 1024 * ch + 128 * (t8 + 1)],
                        identB[:], is_transpose=True, start=True, stop=True),
                        deps=[id_done, last_tev[b_]] + yall)
                    ev = seq.op("vector", lambda e, ch=ch, zsl=zsl, b_=b_: e.tensor_copy(
                        out=ztile[zsl][:, 128 * ch:128 * (ch + 1)], in_=ps_t[b_][:]),
                        deps=[tr, lnf_ev.get(("zt", zsl))])
                    last_tev[b_] = ev
                    tevs.append(ev)
                    fin_guard = ev
                ap1 = lnh.ln(ztile[zsl][:], lnterm[:], tevs + [prev_addx])
                lnf_ev[("zt", zsl)] = ap1
                addx = seq.op("vector", lambda e, zsl=zsl: e.tensor_add(
                    xres[zsl][:], xres[zsl][:], lnterm[:]), deps=[ap1, ldr],
                    small=True)
                prev_addx = addx
                stx = seq.op("scalar", lambda e, tt=tt, zsl=zsl: e.dma_start(
                    out=r2[128 * tt:128 * (tt + 1), :], in_=xres[zsl][:]),
                    dma=True, deps=[addx], skey=f"r2st{zsl}")
                r2_sts.append(stx)

        seq.op("sync", lambda e: e.nop(), deps=r2_sts)


def build_B():
    nc = bass.Bass("TRN2", target_bir_lowering=False)
    G = dram_B(nc)
    with ExitStack() as st_:
        seq = Seq(nc, st_)
        block = st_.enter_context(nc.Block())
        phase_B(nc, seq, st_, G)
        seq.emit(block)
    return nc



def dram_C(nc, kout="ExternalOutput", r2_ap=None):
    kin = "ExternalInput"
    return {
        "R2": r2_ap if r2_ap is not None else
              nc.dram_tensor("R2", [TT, D], F32, kind=kin).ap(),
        "WK": nc.dram_tensor("WK", [D, D], F32, kind=kin).ap(),
        "WV": nc.dram_tensor("WV", [D, D], F32, kind=kin).ap(),
        "KVP": nc.dram_tensor("KVP", [64, H * HD], F32, kind=kout).ap(),
    }


def phase_C(nc, seq, st_, G):
    r2, wk, wv, kvp = G["R2"], G["WK"], G["WV"], G["KVP"]
    if True:
        sbuf = lambda n, sh, dt=F32: st_.enter_context(nc.sbuf_tensor("C" + n, sh, dt))
        psum = lambda n, sh, dt=F32: st_.enter_context(nc.psum_tensor("C" + n, sh, dt))

        xstg = [sbuf(f"xstgC{i}", [128, D], F32) for i in range(3)]
        lntile = [sbuf(f"lntC{i}", [128, D], BF16) for i in range(2)]
        ln_fm = sbuf("ln_fmC", [128, 8 * TT], BF16)
        wstg = [sbuf(f"wstgC{i}", [128, D], F32) for i in range(2)]
        wbf = sbuf("wbfC", [128, 8 * D], BF16)
        wt = sbuf("wtC", [128, 8 * D], BF16)
        k_tm = sbuf("k_tmC", [128, 16 * D], BF16)
        v_tm = sbuf("v_tmC", [128, 16 * D], BF16)
        kvsb = sbuf("kvsbC", [64, H * HD], F32)
        ident = sbuf("identC", [128, 128], BF16)
        ps_mm = [psum(f"ps_mmC{i}", [128, 512], F32) for i in range(2)]
        ps_t = [psum(f"ps_tC{i}", [128, 128], BF16) for i in range(2)]
        ps_kv = psum("ps_kvC", [64, H * HD], F32)

        id_done = mk_identity(seq, ident)
        lnh = LNHelper(seq, sbuf, "lnC", D)

        stg_free = {}
        tm_guard = [None, None]
        last_tev = [None, None]
        fm_evs = []
        for tt in range(16):
            s = tt % 3
            w = tt % 2
            ld = seq.op("sync", lambda e, s=s, tt=tt: e.dma_start(
                out=xstg[s][:], in_=r2[128 * tt:128 * (tt + 1), :]), dma=True,
                deps=[stg_free.get(s)], skey=f"xstgC{s}")
            ap = lnh.ln(xstg[s][:], lntile[w][:], [ld, tm_guard[w]])
            stg_free[s] = ap
            for ch in range(8):
                b_ = ch % 2
                tr = seq.op("tensor", lambda e, ch=ch, w=w: e.matmul(
                    ps_t[b_ if False else ch % 2][:], lntile[w][:, 128 * ch:128 * (ch + 1)],
                    ident[:], is_transpose=True, start=True, stop=True),
                    deps=[ap, id_done, last_tev[b_]])
                ev = seq.op("vector", lambda e, ch=ch, tt=tt, b_=b_: e.tensor_copy(
                    out=ln_fm[:, TT * ch + 128 * tt: TT * ch + 128 * (tt + 1)],
                    in_=ps_t[b_][:]), deps=[tr])
                last_tev[b_] = ev
                fm_evs.append(ev)
                tm_guard[w] = tr
        fm_done = fm_evs[-1]

        def prep_w(wsrc, prev_done):
            casts = []
            for ec in range(8):
                sl = ec % 2
                ld = seq.op("sync", lambda e, ec=ec, sl=sl, wsrc=wsrc: e.dma_start(
                    out=wstg[sl][:], in_=wsrc[128 * ec:128 * (ec + 1), :]), dma=True,
                    deps=[casts[ec - 2] if ec >= 2 else prev_done], skey=f"wstgC{sl}")
                cst = seq.op("vector", lambda e, ec=ec, sl=sl: e.tensor_copy(
                    out=wbf[:, D * ec: D * (ec + 1)], in_=wstg[sl][:]),
                    deps=[ld, prev_done])
                casts.append(cst)
            tevs = []
            for ec in range(8):
                for dck in range(8):
                    b_ = (ec * 8 + dck) % 2
                    tr = seq.op("tensor", lambda e, ec=ec, dck=dck, b_=b_: e.matmul(
                        ps_t[b_][:], wbf[:, D * ec + 128 * dck: D * ec + 128 * (dck + 1)],
                        ident[:], is_transpose=True, start=True, stop=True),
                        deps=[casts[ec], id_done, last_tev[b_]])
                    ev = seq.op("vector", lambda e, ec=ec, dck=dck, b_=b_: e.tensor_copy(
                        out=wt[:, D * dck + 128 * ec: D * dck + 128 * (ec + 1)],
                        in_=ps_t[b_][:]), deps=[tr])
                    last_tev[b_] = ev
                    tevs.append(ev)
            return casts, tevs

        def proj_tm(out_sb, func, scale, wt_done, out_guard):
            evs = {}
            for tt in range(16):
                for e2 in range(2):
                    pb = (tt * 2 + e2) % 2
                    def pmm(e, tt=tt, e2=e2, pb=pb):
                        last = None
                        for dck in range(8):
                            last = e.matmul(
                                ps_mm[pb][:],
                                ln_fm[:, TT * dck + 128 * tt: TT * dck + 128 * (tt + 1)],
                                wt[:, D * dck + 512 * e2: D * dck + 512 * (e2 + 1)],
                                start=(dck == 0), stop=(dck == 7))
                        return last
                    mm = seq.op("tensor", pmm,
                                deps=wt_done + [fm_done, evs.get(("pb", pb)), out_guard])
                    ev = seq.op("scalar", lambda e, tt=tt, e2=e2, pb=pb, out_sb=out_sb,
                                func=func, scale=scale: e.activation(
                        out=out_sb[:, D * tt + 512 * e2: D * tt + 512 * (e2 + 1)],
                        in_=ps_mm[pb][:], func=func, scale=scale), deps=[mm])
                    evs[(tt, e2)] = ev
                    evs[("pb", pb)] = ev
            return [evs[(tt, e2)] for tt in range(16) for e2 in range(2)]

        ck, tk = prep_w(wk, None)
        k_evs = proj_tm(k_tm, AF.Relu, 0.125, [tk[-1]], None)
        cv, tv = prep_w(wv, k_evs[-1])
        v_evs = proj_tm(v_tm, AF.Copy, 1.0, [tv[-1]], None)

        kv_mm = None
        for h in range(H):
            for tt in range(16):
                kv_mm = seq.op("tensor", lambda e, h=h, tt=tt, first=(tt == 0): e.matmul(
                    ps_kv[:, HD * h: HD * (h + 1)],
                    k_tm[:, D * tt + HD * h: D * tt + HD * (h + 1)],
                    v_tm[:, D * tt + HD * h: D * tt + HD * (h + 1)],
                    start=(tt == 0), stop=(tt == 15)),
                    deps=[k_evs[2 * tt + h // 8], v_evs[2 * tt + h // 8],
                          k_evs[-1]])
        kv_ev = seq.op("vector", lambda e: e.tensor_copy(
            out=kvsb[:], in_=ps_kv[:]), deps=[kv_mm])
        st_kv = seq.op("scalar", lambda e: e.dma_start(out=kvp, in_=kvsb[:]),
                       dma=True, deps=[kv_ev])
        seq.op("sync", lambda e: e.nop(), deps=[st_kv])


def build_C():
    nc = bass.Bass("TRN2", target_bir_lowering=False)
    G = dram_C(nc)
    with ExitStack() as st_:
        seq = Seq(nc, st_)
        block = st_.enter_context(nc.Block())
        phase_C(nc, seq, st_, G)
        seq.emit(block)
    return nc



def dram_D1(nc, kout="ExternalOutput", r2_ap=None, kvps_ap=None, kvpo_ap=None,
            r3_ap=None, wu_name="WUQ"):
    kin = "ExternalInput"
    return {
        "R2": r2_ap if r2_ap is not None else
              nc.dram_tensor("R2", [TT, D], F32, kind=kin).ap(),
        "KVPS": kvps_ap if kvps_ap is not None else
                nc.dram_tensor("KVPS", [64, H * HD], F32, kind=kin).ap(),
        "KVPO": kvpo_ap if kvpo_ap is not None else
                nc.dram_tensor("KVPO", [64, H * HD], F32, kind=kin).ap(),
        "WQ": nc.dram_tensor("WQ", [D, D], F32, kind=kin).ap(),
        "WU": nc.dram_tensor(wu_name, [D, D], F32, kind=kin).ap(),
        "WO": nc.dram_tensor("WO", [D, D], F32, kind=kin).ap(),
        "R3": r3_ap if r3_ap is not None else
              nc.dram_tensor("R3", [TT, D], F32, kind=kout).ap(),
    }


def phase_D1(nc, seq, st_, G):
    r2, kvps, kvpo = G["R2"], G["KVPS"], G["KVPO"]
    wq, wu, wo, r3 = G["WQ"], G["WU"], G["WO"], G["R3"]
    sb_d = nc.dram_tensor("SBOUNCE", [H, TT], BF16, kind="Internal").ap()

    if True:
        sbuf = lambda n, sh, dt=F32: st_.enter_context(nc.sbuf_tensor("D" + n, sh, dt))
        psum = lambda n, sh, dt=F32: st_.enter_context(nc.psum_tensor("D" + n, sh, dt))

        X = sbuf("XD", [128, 16 * D], F32)
        ln_fm = sbuf("lnfmD", [128, 8 * TT], BF16)
        qz = sbuf("qzD", [128, 8 * TT], BF16)
        u_fm = sbuf("uD", [128, 8 * TT], BF16)
        o_fm = sbuf("oD", [128, 8 * TT], BF16)
        wstg0 = sbuf("wstgD0", [128, D], F32)
        wstg = [wstg0, wstg0]
        wts0 = sbuf("wtsD0", [128, D], BF16)
        wts = [wts0, wts0]
        kvf = u_fm[:, 0:2 * D].bitcast(F32)
        kvb = u_fm[:, 2 * D:3 * D]
        ones64 = sbuf("ones64D", [128, 1], BF16)
        lntile = [sbuf(f"lntD{i}", [128, D], BF16) for i in range(2)]
        lnterm = sbuf("lntermD", [128, D], F32)
        wbf = lnterm.bitcast(BF16)[:, 0:D]
        srow = qz[0:1, 0:2 * TT].bitcast(F32)
        srbf = qz[0:1, 2 * TT:3 * TT]
        bc = qz[:, 4 * TT:5 * TT]
        sq4 = qz[:, 6 * TT:7 * TT]
        ident = sbuf("identD", [128, 128], BF16)
        ps_mm = [psum(f"ps_mmD{i}", [128, 512], F32) for i in range(2)]
        ps_t = [psum(f"ps_tD{i}", [128, 128], BF16) for i in range(2)]
        ps_s = psum("ps_sD", [1, 512], F32)

        id_done = mk_identity(seq, ident)
        lnh = LNHelper(seq, sbuf, "lnD", D, sq=lnterm)

        ld_x = seq.op("sync", lambda e: e.dma_start(
            out=X[:], in_=r2.rearrange("(n p) c -> p n c", p=128)),
            dma=True, private=True)
        mk_o64 = seq.op("vector", lambda e: e.memset(ones64[:], 1.0))
        ld_kv1 = seq.op("gpsimd", lambda e: e.dma_start(
            out=kvf[0:64, :], in_=kvps), dma=True, private=True)
        ld_kv1b = seq.op("gpsimd", lambda e: e.dma_start(
            out=kvf[64:128, :], in_=kvps), dma=True, private=True)
        ld_kv2 = seq.op("gpsimd", lambda e: e.dma_start(
            out=wstg[0][0:64, 0:H * HD], in_=kvpo), dma=True, private=True)
        ld_kv2b = seq.op("gpsimd", lambda e: e.dma_start(
            out=wstg[0][64:128, 0:H * HD], in_=kvpo), dma=True, private=True)
        kadd = seq.op("vector", lambda e: e.tensor_add(
            kvf, kvf, wstg[0][:, 0:H * HD]),
            deps=[ld_kv1, ld_kv1b, ld_kv2, ld_kv2b], small=True)
        kcast = seq.op("vector", lambda e: e.tensor_copy(
            out=kvb, in_=kvf), deps=[kadd], small=True)

        tm_guard = [None, None]
        last_tev = [None, None]
        fm_evs = []
        for tt in range(16):
            w = tt % 2
            ap = lnh.ln(X[:, D * tt: D * (tt + 1)], lntile[w][:],
                        [ld_x, tm_guard[w]])
            for ch in range(8):
                b_ = ch % 2
                tr = seq.op("tensor", lambda e, ch=ch, w=w, b_=b_: e.matmul(
                    ps_t[b_][:], lntile[w][:, 128 * ch:128 * (ch + 1)],
                    ident[:], is_transpose=True, start=True, stop=True),
                    deps=[ap, id_done, last_tev[b_]])
                ev = seq.op("vector", lambda e, ch=ch, tt=tt, b_=b_: e.tensor_copy(
                    out=ln_fm[:, TT * ch + 128 * tt: TT * ch + 128 * (tt + 1)],
                    in_=ps_t[b_][:]), deps=[tr])
                last_tev[b_] = ev
                fm_evs.append(ev)
                tm_guard[w] = tr
        fm_done = fm_evs[-1]

        def proj_fm(wsrc, out_sb, func, scale, deps0, out_guard, src_fm=None):
            if src_fm is None:
                src_fm = ln_fm
            evs = {}
            wprev = {}
            for ec in range(8):
                sl = ec % 2
                ld = seq.op("sync", lambda e, ec=ec, sl=sl, wsrc=wsrc: e.dma_start(
                    out=wstg[sl][:], in_=wsrc[128 * ec:128 * (ec + 1), :]), dma=True,
                    deps=[wprev.get(("c", 0)), wprev.get(("t", 1))] + deps0,
                    skey="wstgD0")
                cst = seq.op("vector", lambda e, sl=sl: e.tensor_copy(
                    out=wbf, in_=wstg[sl][:]),
                    deps=[ld, wprev.get(("t", 1)), fm_done] + deps0)
                wprev[("c", 0)] = cst
                tevs = []
                for dck in range(8):
                    b_ = dck % 2
                    tr = seq.op("tensor", lambda e, dck=dck, b_=b_: e.matmul(
                        ps_t[b_][:], wbf[:, 128 * dck:128 * (dck + 1)],
                        ident[:], is_transpose=True, start=True, stop=True),
                        deps=[cst, id_done, last_tev[b_]])
                    ev = seq.op("vector", lambda e, dck=dck, sl=sl, b_=b_: e.tensor_copy(
                        out=wts[sl][:, 128 * dck:128 * (dck + 1)], in_=ps_t[b_][:]),
                        deps=[tr, wprev.get(("mm", sl))])
                    last_tev[b_] = ev
                    tevs.append(ev)
                wprev[("t", 1)] = tevs[-1]
                for n4 in range(4):
                    pb = (ec * 4 + n4) % 2
                    def pmm(e, ec=ec, n4=n4, sl=sl, pb=pb):
                        last = None
                        for dck in range(8):
                            last = e.matmul(
                                ps_mm[pb][:],
                                wts[sl][:, 128 * dck:128 * (dck + 1)],
                                src_fm[:, TT * dck + 512 * n4: TT * dck + 512 * (n4 + 1)],
                                start=(dck == 0), stop=(dck == 7))
                        return last
                    mm = seq.op("tensor", pmm,
                                deps=tevs + [fm_done, evs.get(("pb", pb)), out_guard])
                    wprev[("mm", sl)] = mm
                    ev = seq.op("scalar", lambda e, ec=ec, n4=n4, pb=pb,
                                out_sb=out_sb, func=func, scale=scale: e.activation(
                        out=out_sb[:, TT * ec + 512 * n4: TT * ec + 512 * (n4 + 1)],
                        in_=ps_mm[pb][:], func=func, scale=scale), deps=[mm])
                    evs[(ec, n4)] = ev
                    evs[("pb", pb)] = ev
            return evs

        q_evs = proj_fm(wq, qz, AF.Relu, 0.125, [kcast], None)

        o_evs = {}
        for hp in range(8):
            for n4 in range(4):
                pb = (hp * 4 + n4) % 2
                def omm(e, hp=hp, n4=n4, pb=pb):
                    e.matmul(ps_mm[pb][0:64, :],
                             kvb[0:64, HD * (2 * hp): HD * (2 * hp) + HD],
                             qz[0:64, TT * hp + 512 * n4: TT * hp + 512 * (n4 + 1)],
                             start=True, stop=True)
                    return e.matmul(ps_mm[pb][64:128, :],
                                    kvb[64:128, HD * (2 * hp + 1): HD * (2 * hp + 1) + HD],
                                    qz[64:128, TT * hp + 512 * n4: TT * hp + 512 * (n4 + 1)],
                                    start=True, stop=True)
                mm = seq.op("tensor", omm,
                            deps=[kcast, q_evs[(hp, n4)], q_evs[(7, 3)],
                                  o_evs.get(("pb", pb))])
                ev = seq.op("scalar", lambda e, hp=hp, n4=n4, pb=pb: e.copy(
                    out=o_fm[:, TT * hp + 512 * n4: TT * hp + 512 * (n4 + 1)],
                    in_=ps_mm[pb][:]), deps=[mm])
                o_evs[(hp, n4)] = ev
                o_evs[("pb", pb)] = ev
        o_all = [o_evs[(hp, n4)] for hp in range(8) for n4 in range(4)]
        u_evs = proj_fm(wu, u_fm, AF.Silu, 1.0,
                        [q_evs[(7, 3)], o_all[-1]] + o_all, o_all[-1])

        sq_last = None
        srow_sts = []
        prev = None
        for hp in range(8):
            sq_ev = seq.op("scalar", lambda e, hp=hp: e.activation(
                out=sq4, in_=o_fm[:, TT * hp: TT * (hp + 1)], func=AF.Square),
                deps=o_all + [sq_last])
            for half in range(2):
                h = 2 * hp + half
                for n4 in range(4):
                    smm = seq.op("tensor", lambda e, half=half, n4=n4: e.matmul(
                        ps_s[:], ones64[64 * half:64 * half + 64, 0:1],
                        sq4.tensor[64 * half:64 * half + 64,
                                   6 * TT + 512 * n4:6 * TT + 512 * (n4 + 1)] if False
                        else bass.AP(sq4.tensor, 6 * TT + 64 * half * (8 * TT) // 64,
                                     [[8 * TT, 64], [1, 512]]) if False
                        else qz[64 * half:64 * half + 64,
                                6 * TT + 512 * n4:6 * TT + 512 * (n4 + 1)],
                        start=True, stop=True),
                        deps=[sq_ev, mk_o64, prev], small=True)
                    prev = seq.op("scalar", lambda e, n4=n4: e.activation(
                        out=srow[0:1, 512 * n4:512 * (n4 + 1)], in_=ps_s[:],
                        func=AF.Sqrt, scale=1.0 / 64), deps=[smm], small=True)
                sq_last = prev
                mx = seq.op("vector", lambda e: e.tensor_scalar_max(
                    srow, srow, 1e-12), deps=[prev], small=True)
                rc = seq.op("vector", lambda e: e.reciprocal(
                    out=srow, in_=srow), deps=[mx], small=True)
                cb_ = seq.op("vector", lambda e: e.tensor_copy(
                    out=srbf, in_=srow), deps=[rc], small=True)
                prev = seq.op("gpsimd", lambda e, h=h: e.dma_start(
                    out=sb_d[h:h + 1, :], in_=srbf), dma=True,
                    deps=[cb_, prev], skey="srst")
                srow_sts.append(prev)

        g_evs = {}
        for hp in range(8):
            ldb = seq.op("gpsimd", lambda e, hp=hp: e.dma_start(
                out=bc, in_=bass.AP(sb_d.tensor, 2 * hp * TT,
                                       [[TT, 2], [0, 64], [1, TT]])),
                dma=True, deps=[srow_sts[2 * hp + 1], g_evs.get("bc"),
                                o_evs[("pb", 0)], o_evs[("pb", 1)]], skey="bcldD")
            g1 = seq.op("vector", lambda e, hp=hp: e.tensor_mul(
                o_fm[:, TT * hp: TT * (hp + 1)], o_fm[:, TT * hp: TT * (hp + 1)],
                bc), deps=[ldb, sq_last] + o_all + ([srow_sts[-1]] if hp == 0 else []))
            g2 = seq.op("vector", lambda e, hp=hp: e.tensor_mul(
                o_fm[:, TT * hp: TT * (hp + 1)], o_fm[:, TT * hp: TT * (hp + 1)],
                u_fm[:, TT * hp: TT * (hp + 1)]), deps=[g1, u_evs[(hp, 3)], u_evs[(7, 3)]])
            g_evs[hp] = g2
            g_evs["bc"] = g2
        g_all = [g_evs[hp] for hp in range(8)]

        z_evs = proj_fm(wo, qz, AF.Copy, 1.0, g_all,
                        g_all[-1], src_fm=o_fm)

        prev_addx = None
        r3_sts = []
        zt_guard = {}
        for tt in range(16):
            w = tt % 2
            tevs = []
            for ch in range(8):
                b_ = ch % 2
                tr = seq.op("tensor", lambda e, ch=ch, tt=tt, b_=b_: e.matmul(
                    ps_t[b_][:], qz[:, TT * ch + 128 * tt: TT * ch + 128 * (tt + 1)],
                    ident[:], is_transpose=True, start=True, stop=True),
                    deps=[z_evs[(ch, tt // 4)], z_evs[(7, 3)], id_done, last_tev[b_]])
                ev = seq.op("vector", lambda e, ch=ch, w=w, b_=b_: e.tensor_copy(
                    out=lntile[w][:, 128 * ch:128 * (ch + 1)], in_=ps_t[b_][:]),
                    deps=[tr, zt_guard.get(w)])
                last_tev[b_] = ev
                tevs.append(ev)
            ap1 = lnh.ln(lntile[w][:], lnterm[:], tevs + [prev_addx])
            zt_guard[w] = ap1
            addx = seq.op("vector", lambda e, tt=tt: e.tensor_add(
                X[:, D * tt: D * (tt + 1)], X[:, D * tt: D * (tt + 1)], lnterm[:]),
                deps=[ap1], small=True)
            prev_addx = addx
            stx = seq.op("scalar", lambda e, tt=tt: e.dma_start(
                out=r3[128 * tt:128 * (tt + 1), :], in_=X[:, D * tt: D * (tt + 1)]),
                dma=True, deps=[addx, r3_sts[-2] if len(r3_sts) >= 2 else None],
                skey=f"r3st{tt % 2}")
            r3_sts.append(stx)
        seq.op("sync", lambda e: e.nop(), deps=r3_sts)


def build_D1():
    nc = bass.Bass("TRN2", target_bir_lowering=False)
    G = dram_D1(nc, wu_name="WU")
    with ExitStack() as st_:
        seq = Seq(nc, st_)
        block = st_.enter_context(nc.Block())
        phase_D1(nc, seq, st_, G)
        seq.emit(block)
    return nc



def build_D2():
    pass


def dram_D2(nc, r3_ap=None):
    kin = "ExternalInput"
    return {
        "R3": r3_ap if r3_ap is not None else
              nc.dram_tensor("R3", [TT, D], F32, kind=kin).ap(),
        "LN1W": nc.dram_tensor("LN1W", [D], F32, kind=kin).ap(),
        "LN1B": nc.dram_tensor("LN1B", [D], F32, kind=kin).ap(),
        "FW1": nc.dram_tensor("FW1", [4 * D, D], F32, kind=kin).ap(),
        "FB1": nc.dram_tensor("FB1", [4 * D], F32, kind=kin).ap(),
        "LN2W": nc.dram_tensor("LN2W", [4 * D], F32, kind=kin).ap(),
        "LN2B": nc.dram_tensor("LN2B", [4 * D], F32, kind=kin).ap(),
        "FW2": nc.dram_tensor("FW2", [D, 4 * D], F32, kind=kin).ap(),
        "FB2": nc.dram_tensor("FB2", [D], F32, kind=kin).ap(),
        "R4": nc.dram_tensor("R4", [TT, D], F32, kind="ExternalOutput").ap(),
    }


def phase_D2(nc, seq, st_, G):
    r3, ln1w, ln1b = G["R3"], G["LN1W"], G["LN1B"]
    fw1, fb1, ln2w, ln2b = G["FW1"], G["FB1"], G["LN2W"], G["LN2B"]
    fw2, fb2, r4 = G["FW2"], G["FB2"], G["R4"]
    xsp = nc.dram_tensor("XSP3", [TT, D], F32).ap()
    xq1sp = nc.dram_tensor("XQ1SP", [D, TT], BF16).ap()
    xq2sp = nc.dram_tensor("XQ2SP", [4 * D, TT], BF16).ap()
    swb = nc.dram_tensor("SWB", [1, 4], F32).ap()

    NTC = 8
    TCW = 256

    if True:
        sbuf = lambda n, sh, dt=F32: st_.enter_context(nc.sbuf_tensor("E" + n, sh, dt))
        psum = lambda n, sh, dt=F32: st_.enter_context(nc.psum_tensor("E" + n, sh, dt))

        X = sbuf("XE", [128, 16 * D], F32)
        w1qT = X.bitcast(BF16)
        w2qT = X[:, 0:8 * D].bitcast(BF16)
        o2acc = X[:, 8 * D:12 * D]
        h2 = sbuf("h2E", [128, 2 * 4 * D], BF16)
        xq2buf = sbuf("xq2bufE", [128, 16 * TCW], BF16)
        xq1buf = xq2buf[:, 0:8 * TCW]
        sq4 = sbuf("sq4E", [128, 4 * D], F32)
        l2w_bc = sbuf("l2wE", [128, 4 * D], BF16)
        l2b_bc = sbuf("l2bE", [128, 4 * D], BF16)
        l1w_bc = sbuf("l1wE", [128, D], BF16)
        l1b_bc = sbuf("l1bE", [128, D], BF16)
        t1 = sbuf("t1E", [128, D], F32)
        t2 = sbuf("t2E", [128, D], F32)
        qs = sbuf("qsE", [128, 4 * D], F32)
        qi = qs.bitcast(I32)
        qb = sbuf("qbE", [128, 4 * D], BF16)
        stg2 = sbuf("stg2E", [128, 4 * D], BF16)
        wstg = [sbuf(f"wstgE{i}", [128, D], F32) for i in range(2)]
        scol = sbuf("scolE", [128, 20], F32)
        icol = sbuf("icolE", [128, 16], F32)
        icolq = [sbuf(f"icolqE{i}", [128, 4], F32) for i in range(1)][0]
        acc1 = sbuf("acc1E", [128, 2], F32)
        srow = sbuf("srowE", [1, 16 * 128], BF16)
        s2row = sbuf("s2rowE", [1, 4 * 128], BF16)
        fb1row = sbuf("fb1rowE", [1, 4 * D], BF16)
        fb2row = sbuf("fb2rowE", [1, D], BF16)
        rowf = qs[0:1, 0:4 * D]
        swcol = sbuf("swcolE", [128, 4], F32)
        ident = sbuf("identE", [128, 128], BF16)
        ps_mm = [psum(f"ps_mmE{i}", [128, 512], F32) for i in range(2)]
        ps_t = [psum(f"ps_tE{i}", [128, 128], BF16) for i in range(2)]
        ps_r = psum("ps_rE", [1, 128], F32)
        ps_rb = psum("ps_rbE", [1, 128], BF16)

        id_done = mk_identity(seq, ident)
        lnh = LNHelper(seq, sbuf, "lnE", D, sq=sq4[:, 0:D])
        lnh4 = LNHelper(seq, sbuf, "lnE4", 4 * D, sq=sq4)

        last_tevE = [None, None]
        ld_x = seq.op("sync", lambda e: e.dma_start(
            out=X[:], in_=r3.rearrange("(n p) c -> p n c", p=128)),
            dma=True, private=True)
        ld1 = seq.op("gpsimd", lambda e: e.dma_start(
            out=wstg[0][:, 0:D], in_=bass.AP(ln1w.tensor, 0, [[0, 128], [1, D]])),
            dma=True, private=True)
        c1 = seq.op("vector", lambda e: e.tensor_copy(out=l1w_bc[:], in_=wstg[0][:, 0:D]),
                    deps=[ld1])
        ld2 = seq.op("gpsimd", lambda e: e.dma_start(
            out=wstg[1][:, 0:D], in_=bass.AP(ln1b.tensor, 0, [[0, 128], [1, D]])),
            dma=True, private=True)
        c2 = seq.op("vector", lambda e: e.tensor_copy(out=l1b_bc[:], in_=wstg[1][:, 0:D]),
                    deps=[ld2])
        ld3 = seq.op("gpsimd", lambda e: e.dma_start(
            out=sq4[:], in_=bass.AP(ln2w.tensor, 0, [[0, 128], [1, 4 * D]])),
            dma=True, private=True)
        c3 = seq.op("vector", lambda e: e.tensor_copy(out=l2w_bc[:], in_=sq4[:]),
                    deps=[ld3])
        ld4 = seq.op("gpsimd", lambda e: e.dma_start(
            out=sq4[:], in_=bass.AP(ln2b.tensor, 0, [[0, 128], [1, 4 * D]])),
            dma=True, private=True, deps=[c3])
        c4 = seq.op("vector", lambda e: e.tensor_copy(out=l2b_bc[:], in_=sq4[:]),
                    deps=[ld4])
        ld5 = seq.op("gpsimd", lambda e: e.dma_start(
            out=rowf[:], in_=fb1.rearrange("(a b) -> a b", a=1)), dma=True, private=True)
        c5 = seq.op("vector", lambda e: e.tensor_copy(out=fb1row[:], in_=rowf[:]),
                    deps=[ld5])
        ld6 = seq.op("gpsimd", lambda e: e.dma_start(
            out=rowf[0:1, 0:D], in_=fb2.rearrange("(a b) -> a b", a=1)), dma=True,
            private=True, deps=[c5])
        c6 = seq.op("vector", lambda e: e.tensor_copy(out=fb2row[:], in_=rowf[0:1, 0:D]),
                    deps=[ld6])

        def wmean(wsrc, nblk, acc_slice, prev):
            accops = []
            for blk in range(nblk):
                sl = blk % 2
                ld = seq.op("sync" if sl == 0 else "gpsimd",
                    lambda e, blk=blk, sl=sl, wsrc=wsrc: e.dma_start(
                    out=wstg[sl][:], in_=bass.AP(wsrc.tensor, blk * 128 * D,
                                                 [[D, 128], [1, D]])), dma=True,
                    deps=[accops[blk - 2] if blk >= 2 else prev], skey=f"wstgE{sl}")
                rd = seq.op("vector", lambda e, sl=sl, blk=blk: e.tensor_reduce(
                    out=scol[:, 18:19], in_=wstg[sl][:], axis=AX.X, op=ALU.add,
                    apply_absolute_value=True),
                    deps=[ld, accops[-1] if accops else None], small=True)
                ac = seq.op("vector", lambda e, blk=blk, acc_slice=acc_slice: (
                    e.tensor_copy(out=acc_slice, in_=scol[:, 18:19]) if blk == 0
                    else e.tensor_add(acc_slice, acc_slice, scol[:, 18:19])),
                    deps=[rd], small=True)
                accops.append(ac)
            return accops[-1]
        m1 = wmean(fw1, 32, acc1[:, 0:1], c6)
        m2 = wmean(fw2, 32, acc1[:, 1:2], m1)
        cacc = seq.op("vector", lambda e: e.tensor_copy(
            out=qb[:, 0:2], in_=acc1[:]), deps=[m2], small=True)
        mko = seq.op("vector", lambda e: e.memset(qb[:, 2:3], 1.0), small=True,
                     deps=[m2])
        tacc = seq.op("tensor", lambda e: e.matmul(
            ps_r[0:1, 0:2], qb[:, 2:3], qb[:, 0:2],
            start=True, stop=True), deps=[cacc, mko], small=True)
        sums = seq.op("scalar", lambda e: e.copy(
            out=rowf[0:1, 0:2], in_=ps_r[0:1, 0:2]), deps=[tacc], small=True)
        mn1 = seq.op("vector", lambda e: e.tensor_scalar_mul(
            rowf[0:1, 0:2], rowf[0:1, 0:2], 1.0 / (4 * D * D)),
            deps=[sums], small=True)
        mn2 = seq.op("vector", lambda e: e.tensor_scalar_max(
            rowf[0:1, 0:2], rowf[0:1, 0:2], 1e-5), deps=[mn1], small=True)
        mn = seq.op("vector", lambda e: e.reciprocal(
            out=rowf[0:1, 2:4], in_=rowf[0:1, 0:2]), deps=[mn2], small=True)
        pk = seq.op("vector", lambda e: (
            e.tensor_copy(out=rowf[0:1, 4:5], in_=rowf[0:1, 2:3]),
            e.tensor_copy(out=rowf[0:1, 5:6], in_=rowf[0:1, 0:1]),
            e.tensor_copy(out=rowf[0:1, 6:7], in_=rowf[0:1, 3:4]),
            e.tensor_copy(out=rowf[0:1, 7:8], in_=rowf[0:1, 1:2]),
        )[-1], deps=[mn], small=True)
        stsw = seq.op("gpsimd", lambda e: e.dma_start(
            out=swb, in_=rowf[0:1, 4:8]), dma=True, deps=[pk], skey="swst")
        ldsw = seq.op("gpsimd", lambda e: e.dma_start(
            out=swcol[:], in_=bass.AP(swb.tensor, 0, [[0, 128], [1, 4]])),
            dma=True, deps=[stsw], skey="swld")

        xq_sts = []
        prev_q = None
        srow_evs = []
        for tt in range(16):
            a1 = lnh.ln(X[:, D * tt: D * (tt + 1)], t1[:], [ld_x, prev_q])
            a2 = lnh.ln(t1[:], t2[:], [a1])
            af1 = seq.op("vector", lambda e: e.tensor_mul(t2[:], t2[:], l1w_bc[:]),
                         deps=[a2, c1])
            af2 = seq.op("vector", lambda e: e.tensor_add(t2[:], t2[:], l1b_bc[:]),
                         deps=[af1, c2])
            am = seq.op("vector", lambda e, tt=tt: e.tensor_reduce(
                out=scol[:, 16:17], in_=t2[:], axis=AX.X, op=ALU.max,
                apply_absolute_value=True), deps=[af2], small=True)
            am2a = seq.op("vector", lambda e: e.tensor_scalar_max(
                scol[:, 16:17], scol[:, 16:17], 1e-5), deps=[am], small=True)
            am2b = seq.op("vector", lambda e, tt=tt: e.tensor_scalar_mul(
                icol[:, tt:tt + 1], scol[:, 16:17], 1.0 / 127), deps=[am2a], small=True)
            am2c = seq.op("vector", lambda e: e.reciprocal(
                out=scol[:, 17:18], in_=scol[:, 16:17]), deps=[am2a], small=True)
            am2 = seq.op("vector", lambda e, tt=tt: e.tensor_scalar_mul(
                scol[:, tt:tt + 1], scol[:, 17:18], 127.0), deps=[am2c, am2b], small=True)
            qm = seq.op("vector", lambda e, tt=tt: e.tensor_scalar_mul(
                qs[:, 0:D], t2[:], scol[:, tt:tt + 1]), deps=[am2, ldsw])
            qm = seq.op("vector", lambda e: e.tensor_scalar(
                out=qs[:, 0:D], in0=qs[:, 0:D], scalar1=MAGIC, scalar2=-MAGIC,
                op0=ALU.add, op1=ALU.add), deps=[qm])
            qc = seq.op("vector", lambda e: e.tensor_copy(
                out=qi[:, 0:D], in_=qs[:, 0:D]), deps=[qm])
            qf = seq.op("scalar", lambda e: e.copy(
                out=qb[:, 0:D], in_=qi[:, 0:D]), deps=[qc])
            tevs = []
            for dck in range(8):
                b_ = dck % 2
                tr = seq.op("tensor", lambda e, dck=dck, b_=b_: e.matmul(
                    ps_t[b_][:], qb[:, 128 * dck:128 * (dck + 1)], ident[:],
                    is_transpose=True, start=True, stop=True),
                    deps=[qf, id_done, prev_q, last_tevE[b_]])
                ev = seq.op("vector", lambda e, dck=dck, b_=b_: e.tensor_copy(
                    out=stg2[:, 128 * dck:128 * (dck + 1)], in_=ps_t[b_][:]),
                    deps=[tr, prev_q])
                last_tevE[b_] = ev
                tevs.append(ev)
            st = seq.op("sync", lambda e, tt=tt: e.dma_start(
                out=bass.AP(xq1sp.tensor, 128 * tt,
                            [[TT, 128], [128 * TT, 8], [1, 128]]),
                in_=stg2[:, 0:8 * 128]), dma=True, deps=tevs, skey="xq1st")
            xq_sts.append(st)
            prev_q = st
        srow_ops = []
        for tt in range(16):
            scm = seq.op("vector", lambda e, tt=tt: e.tensor_mul(
                scol[:, 16:17], scol[:, tt:tt + 1], swcol[:, 0:1]),
                deps=[ldsw, xq_sts[-1] if tt == 0 else None,
                      srow_ops[-1] if srow_ops else None], small=True)
            scb = seq.op("vector", lambda e: e.tensor_copy(
                out=qb[:, 0:1], in_=scol[:, 16:17]), deps=[scm], small=True)
            trw = seq.op("tensor", lambda e: e.matmul(
                ps_rb[0:1, 0:128], qb[:, 0:1], ident[:, 0:128],
                is_transpose=True, start=True, stop=True), deps=[scb, id_done],
                small=True)
            evw = seq.op("vector", lambda e, tt=tt: e.tensor_copy(
                out=srow[0:1, 128 * tt:128 * (tt + 1)], in_=ps_rb[0:1, 0:128]),
                deps=[trw], small=True)
            srow_ops.append(evw)
        srow_done = srow_ops[-1]
        ic2 = seq.op("vector", lambda e: e.tensor_scalar_mul(
            icol[:], icol[:], swcol[:, 1:2]), deps=[ldsw, srow_done], small=True)

        sp_x = seq.op("sync", lambda e: e.dma_start(
            out=xsp.rearrange("(n p) c -> p n c", p=128), in_=X[:]),
            dma=True, private=True, deps=[prev_q, srow_done])

        r4_sts = []
        w1_guard = [sp_x]
        prevld = {0: m2, 1: m2}
        fin_guard = None
        for tc in range(NTC):
            w1tevs = []
            for blk in range(32):
                sl = blk % 2
                ld = seq.op("sync" if sl == 0 else "gpsimd",
                    lambda e, blk=blk, sl=sl: e.dma_start(
                    out=wstg[sl][:], in_=bass.AP(fw1.tensor, blk * 128 * D,
                                                 [[D, 128], [1, D]])), dma=True,
                    deps=[prevld.get(sl)], skey=f"wstgE{sl}")
                qm1 = seq.op("vector", lambda e, sl=sl: e.tensor_scalar_mul(
                    qs[:, 0:D], wstg[sl][:], swcol[:, 0:1]),
                    deps=[ld, ldsw] + (w1_guard if blk < 2 else []) +
                    ([fin_guard] if blk < 2 else []))
                prevld[sl] = qm1
                qm1 = seq.op("vector", lambda e: e.tensor_scalar(
                    out=qs[:, 0:D], in0=qs[:, 0:D], scalar1=MAGIC, scalar2=-MAGIC,
                    op0=ALU.add, op1=ALU.add), deps=[qm1])
                qc1 = seq.op("vector", lambda e: e.tensor_copy(
                    out=qi[:, 0:D], in_=qs[:, 0:D]), deps=[qm1])
                qf1 = seq.op("scalar", lambda e: e.copy(
                    out=qb[:, 0:D], in_=qi[:, 0:D]), deps=[qc1])
                qcl = seq.op("vector", lambda e: e.tensor_scalar(
                    out=qb[:, 0:D], in0=qb[:, 0:D], scalar1=1.0, scalar2=-1.0,
                    op0=ALU.min, op1=ALU.max), deps=[qf1])
                for dck in range(8):
                    b_ = dck % 2
                    tr = seq.op("tensor", lambda e, dck=dck, b_=b_: e.matmul(
                        ps_t[b_][:], qb[:, 128 * dck:128 * (dck + 1)], ident[:],
                        is_transpose=True, start=True, stop=True),
                        deps=[qcl, id_done, last_tevE[b_]])
                    ev = seq.op("vector", lambda e, dck=dck, blk=blk, b_=b_: e.tensor_copy(
                        out=w1qT[:, 4096 * dck + 128 * blk: 4096 * dck + 128 * (blk + 1)],
                        in_=ps_t[b_][:]), deps=[tr] + (w1_guard if blk < 2 else []))
                    last_tevE[b_] = ev
                    w1tevs.append(ev)
            ldxq1 = seq.op("sync", lambda e, tc=tc: e.dma_start(
                out=xq1buf, in_=bass.AP(xq1sp.tensor, TCW * tc,
                                        [[TT, 128], [128 * TT, 8], [1, TCW]])),
                dma=True, deps=xq_sts + [prevld.get("xq1"), fin_guard], skey="xq1ld")
            he_evs = {}
            NTT2 = TCW // 128
            for tt2 in range(NTT2):
                for hb8 in range(8):
                    pb = (tt2 * 8 + hb8) % 2
                    def hmm(e, tt2=tt2, hb8=hb8, pb=pb, tc=tc):
                        last = None
                        for dck in range(8):
                            last = e.matmul(
                                ps_mm[pb][:],
                                xq1buf[:, TCW * dck + 128 * tt2: TCW * dck + 128 * (tt2 + 1)],
                                w1qT[:, 4096 * dck + 512 * hb8: 4096 * dck + 512 * (hb8 + 1)],
                                start=(dck == 0), stop=False)
                        return e.matmul(
                            ps_mm[pb][:],
                            srow[0:1, 128 * (NTT2 * tc + tt2): 128 * (NTT2 * tc + tt2 + 1)],
                            fb1row[0:1, 512 * hb8: 512 * (hb8 + 1)],
                            start=False, stop=True)
                    mm = seq.op("tensor", hmm,
                                deps=[ldxq1, srow_done, c5, he_evs.get(("pb", pb))]
                                + w1tevs[-1:])
                    iv = seq.op("vector", lambda e, tt2=tt2, hb8=hb8, pb=pb, tc=tc: e.tensor_scalar_mul(
                        qs[:, 512 * pb:512 * (pb + 1)], ps_mm[pb][:],
                        icol[:, NTT2 * tc + tt2: NTT2 * tc + tt2 + 1]),
                        deps=[mm, ic2, he_evs.get(("iv", pb))])
                    ev = seq.op("scalar", lambda e, tt2=tt2, hb8=hb8, pb=pb: e.activation(
                        out=h2[:, 4096 * tt2 + 512 * hb8: 4096 * tt2 + 512 * (hb8 + 1)],
                        in_=qs[:, 512 * pb:512 * (pb + 1)], func=AF.Gelu), deps=[iv])
                    he_evs[(tt2, hb8)] = ev
                    he_evs[("pb", pb)] = ev
                    he_evs[("iv", pb)] = iv
            he_all = [he_evs[(a, b)] for a in range(NTT2) for b in range(8)]
            xq2_sts = []
            prev2 = None
            for tt2 in range(NTT2):
                a2 = lnh4.ln(h2[:, 4096 * tt2: 4096 * (tt2 + 1)], qs[:],
                             he_all + [prev2])
                af1 = seq.op("vector", lambda e: e.tensor_mul(qs[:], qs[:], l2w_bc[:]),
                             deps=[a2, c3])
                af2 = seq.op("vector", lambda e: e.tensor_add(qs[:], qs[:], l2b_bc[:]),
                             deps=[af1, c4])
                am = seq.op("vector", lambda e: e.tensor_reduce(
                    out=scol[:, 16:17], in_=qs[:], axis=AX.X, op=ALU.max,
                    apply_absolute_value=True), deps=[af2], small=True)
                am2a = seq.op("vector", lambda e: e.tensor_scalar_max(
                    scol[:, 16:17], scol[:, 16:17], 1e-5), deps=[am], small=True)
                am2b = seq.op("vector", lambda e, tt2=tt2: e.tensor_scalar_mul(
                    icolq[:, tt2:tt2 + 1], scol[:, 16:17], 1.0 / 127),
                    deps=[am2a], small=True)
                am2c = seq.op("vector", lambda e: e.reciprocal(
                    out=scol[:, 17:18], in_=scol[:, 16:17]), deps=[am2a], small=True)
                am2 = seq.op("vector", lambda e: e.tensor_scalar_mul(
                    scol[:, 18:19], scol[:, 17:18], 127.0), deps=[am2c, am2b], small=True)
                qm = seq.op("vector", lambda e: e.tensor_scalar_mul(
                    qs[:], qs[:], scol[:, 18:19]), deps=[am2])
                qm = seq.op("vector", lambda e: e.tensor_scalar(
                    out=qs[:], in0=qs[:], scalar1=MAGIC, scalar2=-MAGIC,
                    op0=ALU.add, op1=ALU.add), deps=[qm])
                qc = seq.op("vector", lambda e: e.tensor_copy(
                    out=qi[:], in_=qs[:]), deps=[qm])
                qf = seq.op("scalar", lambda e: e.copy(
                    out=qb[:], in_=qi[:]), deps=[qc])
                scm = seq.op("vector", lambda e, tt2=tt2: e.tensor_mul(
                    scol[:, 16:17], scol[:, 18:19], swcol[:, 2:3]),
                    deps=[qm, ldsw], small=True)
                scb = seq.op("vector", lambda e: e.tensor_copy(
                    out=stg2[:, 0:1], in_=scol[:, 16:17]), deps=[scm, prev2], small=True)
                trw = seq.op("tensor", lambda e: e.matmul(
                    ps_rb[0:1, 0:128], stg2[:, 0:1], ident[:, 0:128],
                    is_transpose=True, start=True, stop=True), deps=[scb, id_done],
                    small=True)
                evw = seq.op("vector", lambda e, tt2=tt2: e.tensor_copy(
                    out=s2row[0:1, 128 * tt2:128 * (tt2 + 1)], in_=ps_rb[0:1, 0:128]),
                    deps=[trw], small=True)
                ic2b = seq.op("vector", lambda e, tt2=tt2: e.tensor_scalar_mul(
                    icolq[:, tt2:tt2 + 1], icolq[:, tt2:tt2 + 1], swcol[:, 3:4]),
                    deps=[am2, ldsw], small=True)
                tevs = []
                for hck in range(32):
                    b_ = hck % 2
                    tr = seq.op("tensor", lambda e, hck=hck, b_=b_: e.matmul(
                        ps_t[b_][:], qb[:, 128 * hck:128 * (hck + 1)], ident[:],
                        is_transpose=True, start=True, stop=True),
                        deps=[qf, id_done, prev2, last_tevE[b_]])
                    ev = seq.op("vector", lambda e, hck=hck, b_=b_: e.tensor_copy(
                        out=stg2[:, 128 * hck:128 * (hck + 1)], in_=ps_t[b_][:]),
                        deps=[tr, prev2, evw])
                    last_tevE[b_] = ev
                    tevs.append(ev)
                st = seq.op("sync", lambda e, tc=tc, tt2=tt2: e.dma_start(
                    out=bass.AP(xq2sp.tensor, TCW * tc + 128 * tt2,
                                [[TT, 128], [128 * TT, 32], [1, 128]]),
                    in_=stg2[:]), dma=True, deps=tevs, skey="xq2st")
                xq2_sts.append(st)
                prev2 = st
            o2evs = {}
            for half in range(2):
                w2tevs = []
                for dblk in range(8):
                    for hq in range(2):
                        hqg = 2 * half + hq
                        sl = (dblk * 2 + hq) % 2
                        ld = seq.op("sync" if sl == 0 else "gpsimd",
                            lambda e, dblk=dblk, hqg=hqg, sl=sl: e.dma_start(
                            out=wstg[sl][:], in_=bass.AP(
                                fw2.tensor, dblk * 128 * 4 * D + D * hqg,
                                [[4 * D, 128], [1, D]])), dma=True,
                            deps=[prevld.get(sl)], skey=f"wstgE{sl}")
                        qm1 = seq.op("vector", lambda e, sl=sl: e.tensor_scalar_mul(
                            qs[:, 0:D], wstg[sl][:], swcol[:, 2:3]),
                            deps=[ld, ldsw, xq2_sts[-1],
                                  o2evs.get("done") if half == 1 else he_evs[("pb", 0)],
                                  he_evs[("pb", 1)]])
                        prevld[sl] = qm1
                        qm1 = seq.op("vector", lambda e: e.tensor_scalar(
                            out=qs[:, 0:D], in0=qs[:, 0:D], scalar1=MAGIC, scalar2=-MAGIC,
                            op0=ALU.add, op1=ALU.add), deps=[qm1])
                        qc1 = seq.op("vector", lambda e: e.tensor_copy(
                            out=qi[:, 0:D], in_=qs[:, 0:D]), deps=[qm1])
                        qf1 = seq.op("scalar", lambda e: e.copy(
                            out=qb[:, 0:D], in_=qi[:, 0:D]), deps=[qc1])
                        qcl = seq.op("vector", lambda e: e.tensor_scalar(
                            out=qb[:, 0:D], in0=qb[:, 0:D], scalar1=1.0, scalar2=-1.0,
                            op0=ALU.min, op1=ALU.max), deps=[qf1])
                        for j in range(8):
                            b_ = j % 2
                            hckL = 8 * hq + j
                            tr = seq.op("tensor", lambda e, j=j, b_=b_: e.matmul(
                                ps_t[b_][:], qb[:, 128 * j:128 * (j + 1)], ident[:],
                                is_transpose=True, start=True, stop=True),
                                deps=[qcl, id_done, last_tevE[b_]])
                            ev = seq.op("vector", lambda e, hckL=hckL, dblk=dblk, b_=b_: e.tensor_copy(
                                out=w2qT[:, 1024 * hckL + 128 * dblk: 1024 * hckL + 128 * (dblk + 1)],
                                in_=ps_t[b_][:]), deps=[tr])
                            last_tevE[b_] = ev
                            w2tevs.append(ev)
                ldxq2 = seq.op("sync", lambda e, tc=tc, half=half: e.dma_start(
                    out=xq2buf[:], in_=bass.AP(
                        xq2sp.tensor, TCW * tc + 16 * 128 * TT * half,
                        [[TT, 128], [128 * TT, 16], [1, TCW]])),
                    dma=True, deps=xq2_sts + [prevld.get("xq2"),
                                              he_evs[("pb", 0)], he_evs[("pb", 1)]],
                    skey="xq2ld")
                prevld["xq2"] = ldxq2
                prevld["xq1"] = ldxq2
                for tt2 in range(NTT2):
                    for dc2 in range(2):
                        pb = (tt2 * 2 + dc2) % 2
                        def ymm(e, tt2=tt2, dc2=dc2, pb=pb, half=half):
                            last = None
                            for j in range(16):
                                last = e.matmul(
                                    ps_mm[pb][:],
                                    xq2buf[:, TCW * j + 128 * tt2: TCW * j + 128 * (tt2 + 1)],
                                    w2qT[:, 1024 * j + 512 * dc2: 1024 * j + 512 * (dc2 + 1)],
                                    start=(j == 0), stop=(half == 1 and j == 15))
                            if half == 0:
                                last = e.matmul(
                                    ps_mm[pb][:],
                                    s2row[0:1, 128 * tt2: 128 * (tt2 + 1)],
                                    fb2row[0:1, 512 * dc2: 512 * (dc2 + 1)],
                                    start=False, stop=True)
                            return last
                        mm = seq.op("tensor", ymm,
                                    deps=[ldxq2, w2tevs[-1], c6, o2evs.get(("pb", pb))])
                        if half == 0:
                            ev = seq.op("vector", lambda e, tt2=tt2, dc2=dc2, pb=pb: e.tensor_copy(
                                out=o2acc[:, 1024 * tt2 + 512 * dc2: 1024 * tt2 + 512 * (dc2 + 1)],
                                in_=ps_mm[pb][:]),
                                deps=[mm] + ([fin_guard] if tt2 == 0 and dc2 == 0 else []))
                        else:
                            ev = seq.op("vector", lambda e, tt2=tt2, dc2=dc2, pb=pb: e.tensor_add(
                                o2acc[:, 1024 * tt2 + 512 * dc2: 1024 * tt2 + 512 * (dc2 + 1)],
                                o2acc[:, 1024 * tt2 + 512 * dc2: 1024 * tt2 + 512 * (dc2 + 1)],
                                ps_mm[pb][:]), deps=[mm])
                        o2evs[(half, tt2, dc2)] = ev
                        o2evs[("pb", pb)] = ev
                o2evs["done"] = o2evs[("pb", 0)]
            o2_all = [o2evs[(h_, a, b)] for h_ in range(2) for a in range(NTT2)
                      for b in range(2)]
            for tt2 in range(NTT2):
                tt = NTT2 * tc + tt2
                iv = seq.op("vector", lambda e, tt2=tt2: e.tensor_scalar_mul(
                    qs[:, 0:D], o2acc[:, 1024 * tt2: 1024 * (tt2 + 1)],
                    icolq[:, tt2:tt2 + 1]), deps=o2_all + [fin_guard])
                a1 = lnh.ln(qs[:, 0:D], t1[:], [iv])
                ldr = seq.op("sync", lambda e, tt=tt: e.dma_start(
                    out=t2[:], in_=xsp[128 * tt:128 * (tt + 1), :]), dma=True,
                    deps=[sp_x, fin_guard], skey="xrld")
                ad = seq.op("vector", lambda e: e.tensor_add(
                    t2[:], t2[:], t1[:]), deps=[a1, ldr], small=True)
                stx = seq.op("scalar", lambda e, tt=tt: e.dma_start(
                    out=r4[128 * tt:128 * (tt + 1), :], in_=t2[:]), dma=True,
                    deps=[ad], skey="r4st")
                r4_sts.append(stx)
                fin_guard = stx
        seq.op("sync", lambda e: e.nop(), deps=r4_sts)


def build_D2():
    nc = bass.Bass("TRN2", target_bir_lowering=False)
    G = dram_D2(nc)
    with ExitStack() as st_:
        seq = Seq(nc, st_)
        block = st_.enter_context(nc.Block())
        phase_D2(nc, seq, st_, G)
        seq.emit(block)
    return nc



def build_FUSED():
    nc = bass.Bass("TRN2", target_bir_lowering=False)
    GA = dram_A(nc, kout="Internal")
    GB = dram_B(nc, kout="Internal", aun_ap=GA["AUN"], den_ap=GA["DEN"])
    GC = dram_C(nc, kout="Internal", r2_ap=GB["R2"])
    kvx = nc.dram_tensor("KVX", [128, H * HD], F32).ap()
    GD1 = dram_D1(nc, kout="Internal", r2_ap=GB["R2"],
                  kvps_ap=kvx[0:64, :], kvpo_ap=kvx[64:128, :])
    GD2 = dram_D2(nc, r3_ap=GD1["R3"])
    with ExitStack() as st_top:
        seq = Seq(nc, st_top)
        block = st_top.enter_context(nc.Block())
        for ph, G in ((phase_A, GA), (phase_B, GB), (phase_C, GC)):
            with ExitStack() as st_:
                ph(nc, seq, st_, G)
            seq.barrier()
        seq.op("gpsimd", lambda e: e.collective_compute(
            "AllGather", ALU.bypass,
            replica_groups=[[0, 1], [2, 3], [4, 5], [6, 7]],
            ins=[GC["KVP"]], outs=[kvx]))
        seq.barrier()
        for ph, G in ((phase_D1, GD1), (phase_D2, GD2)):
            with ExitStack() as st_:
                ph(nc, seq, st_, G)
            seq.barrier()
        seq.emit(block)
    return nc



def _perm_my(p):
    col = np.arange(TT)
    return 32 * (col // 16) + 16 * p + (col % 16)


def _perm_q(p):
    idx = np.arange(TT)
    return 256 * (idx // 128) + 2 * (idx % 128) + p


def _masks_np(p):
    r = np.arange(128)[:, None, None, None]
    a_s = np.arange(32)[None, :, None]
    j = np.arange(16)[None, None, :]
    out = np.zeros((128, 32, 32, 16), np.float32)
    for jk in range(16):
        for dD in range(2):
            F = 1024 * dD + 32 * a_s + 16 * p + j - 16 * r[:, 0] - jk
            out[:, 2 * jk + dD] = (F >= 0).astype(np.float32)
    return np.ascontiguousarray(out.reshape(128, 32 * 512)).astype(BF16NP)


def run_launch_A(x, wqkv, bqkv):
    ncA = build_A()
    in_maps = []
    for i in range(8):
        b, p = i // 2, i % 2
        in_maps.append({
            "XQ": np.ascontiguousarray(x[b][_perm_q(p)]),
            "XF": np.ascontiguousarray(x[b]),
            "WQKV": np.ascontiguousarray(wqkv),
            "BQKV": np.ascontiguousarray(bqkv),
            "MASKS": _masks_np(p),
        })
    res = run_bass_kernel_spmd(ncA, in_maps, core_ids=list(range(8)))
    return [{k: np.asarray(v) for k, v in r.items()} for r in res.results]


def kernel(**inputs):
    f32c = lambda a: np.ascontiguousarray(np.asarray(a), dtype=np.float32)
    x = f32c(inputs["L0_attn_x"])
    ncF = build_FUSED()
    in_maps = []
    for i in range(8):
        b, p = i // 2, i % 2
        in_maps.append({
            "XQ": np.ascontiguousarray(x[b][_perm_q(p)]),
            "XF": np.ascontiguousarray(x[b]),
            "WQKV": f32c(inputs["L0_attn_Wqkv"]),
            "BQKV": f32c(inputs["L0_attn_bqkv"]),
            "MASKS": _masks_np(p),
            "XMY": np.ascontiguousarray(x[b][_perm_my(p)]),
            "WPROJ": f32c(inputs["L0_attn_Wproj"]),
            "BPROJ": f32c(inputs["L0_attn_bproj"]),
            "W1": f32c(inputs["L0_ffn_W1"]), "B1": f32c(inputs["L0_ffn_b1"]),
            "W2": f32c(inputs["L0_ffn_W2"]), "B2": f32c(inputs["L0_ffn_b2"]),
            "WU": f32c(inputs["L0_ffn_Wu"]), "BU": f32c(inputs["L0_ffn_bu"]),
            "WK": f32c(inputs["L1_attn_Wk"]), "WV": f32c(inputs["L1_attn_Wv"]),
            "WQ": f32c(inputs["L1_attn_Wq"]), "WUQ": f32c(inputs["L1_attn_Wu"]),
            "WO": f32c(inputs["L1_attn_Wo"]),
            "LN1W": f32c(inputs["L1_ffn_ln1_w"]), "LN1B": f32c(inputs["L1_ffn_ln1_b"]),
            "FW1": f32c(inputs["L1_ffn_w1"]), "FB1": f32c(inputs["L1_ffn_b1"]),
            "LN2W": f32c(inputs["L1_ffn_ln2_w"]), "LN2B": f32c(inputs["L1_ffn_ln2_b"]),
            "FW2": f32c(inputs["L1_ffn_w2"]), "FB2": f32c(inputs["L1_ffn_b2"]),
        })
    res = run_bass_kernel_spmd(ncF, in_maps, core_ids=list(range(8)))
    out = np.empty((NB, TB, D), dtype=np.float32)
    for i in range(8):
        b, p = i // 2, i % 2
        out[b, _perm_my(p)] = np.asarray(res.results[i]["R4"])
    return out


def kernel_unfused(**inputs):
    f32c = lambda a: np.ascontiguousarray(np.asarray(a), dtype=np.float32)
    x = f32c(inputs["L0_attn_x"])
    cores = list(range(8))

    outsA = run_launch_A(x, f32c(inputs["L0_attn_Wqkv"]), f32c(inputs["L0_attn_bqkv"]))

    ncB = build_B()
    in_maps = []
    for i in cores:
        b, p = i // 2, i % 2
        in_maps.append({
            "XMY": np.ascontiguousarray(x[b][_perm_my(p)]),
            "AUN": outsA[i]["AUN"], "DEN": outsA[i]["DEN"],
            "WPROJ": f32c(inputs["L0_attn_Wproj"]),
            "BPROJ": f32c(inputs["L0_attn_bproj"]),
            "W1": f32c(inputs["L0_ffn_W1"]), "B1": f32c(inputs["L0_ffn_b1"]),
            "W2": f32c(inputs["L0_ffn_W2"]), "B2": f32c(inputs["L0_ffn_b2"]),
            "WU": f32c(inputs["L0_ffn_Wu"]), "BU": f32c(inputs["L0_ffn_bu"]),
        })
    resB = run_bass_kernel_spmd(ncB, in_maps, core_ids=cores)
    r2s = [np.asarray(resB.results[i]["R2"]) for i in cores]

    ncC = build_C()
    in_maps = [{"R2": r2s[i],
                "WK": f32c(inputs["L1_attn_Wk"]),
                "WV": f32c(inputs["L1_attn_Wv"])} for i in cores]
    resC = run_bass_kernel_spmd(ncC, in_maps, core_ids=cores)
    kvps = [np.asarray(resC.results[i]["KVP"]) for i in cores]

    ncD1 = build_D1()
    in_maps = [{"R2": r2s[i],
                "KVPS": kvps[i], "KVPO": kvps[i ^ 1],
                "WQ": f32c(inputs["L1_attn_Wq"]),
                "WU": f32c(inputs["L1_attn_Wu"]),
                "WO": f32c(inputs["L1_attn_Wo"])} for i in cores]
    resD1 = run_bass_kernel_spmd(ncD1, in_maps, core_ids=cores)
    r3s = [np.asarray(resD1.results[i]["R3"]) for i in cores]

    ncD2 = build_D2()
    in_maps = [{"R3": r3s[i],
                "LN1W": f32c(inputs["L1_ffn_ln1_w"]), "LN1B": f32c(inputs["L1_ffn_ln1_b"]),
                "FW1": f32c(inputs["L1_ffn_w1"]), "FB1": f32c(inputs["L1_ffn_b1"]),
                "LN2W": f32c(inputs["L1_ffn_ln2_w"]), "LN2B": f32c(inputs["L1_ffn_ln2_b"]),
                "FW2": f32c(inputs["L1_ffn_w2"]), "FB2": f32c(inputs["L1_ffn_b2"])}
               for i in cores]
    resD2 = run_bass_kernel_spmd(ncD2, in_maps, core_ids=cores)

    out = np.empty((NB, TB, D), dtype=np.float32)
    for i in cores:
        b, p = i // 2, i % 2
        out[b, _perm_my(p)] = np.asarray(resD2.results[i]["R4"])
    return out
